# version 07 (final)
# baseline (speedup 1.0000x reference)
import math
import numpy as np
import ml_dtypes
import concourse.bass as bass
import concourse.mybir as mybir
from concourse.bass_utils import run_bass_kernel_spmd

F32 = mybir.dt.float32
BF16 = mybir.dt.bfloat16
AF = mybir.ActivationFunctionType
ALU = mybir.AluOpType
AX = mybir.AxisListType

NCORES = 8
B_PER_CORE = 2
T_SEQ = 2048
D = 1024
TOK = B_PER_CORE * T_SEQ


class Sem:

    def __init__(self, nc, name, step=1):
        self.h = nc.alloc_semaphore(name)
        self.v = 0
        self.step = step

    def inc(self, instr, k=None):
        k = self.step if k is None else k
        instr.then_inc(self.h, k)
        self.v += k
        return self.v


class Ctx:

    def __init__(self, nc, blk):
        self.nc = nc
        self.blk = blk
        self.sd = Sem(nc, "s_dma_sync", 16)
        self.sg = Sem(nc, "s_dma_gp", 16)
        self.sq = Sem(nc, "s_dma_sc", 16)
        self.sp = Sem(nc, "s_pe")
        self.sv = Sem(nc, "s_dve")
        self.sa = Sem(nc, "s_act")
        self.sm = Sem(nc, "s_pool")
        self._bufs = []
        self._serial = {}

    def alloc(self, name, shape, dtype, psum=False):
        n = self._serial.get(name, 0)
        self._serial[name] = n + 1
        uname = name if n == 0 else f"{name}__{n}"
        guard = (self.nc.psum_tensor if psum else self.nc.sbuf_tensor)(uname, shape, dtype)
        t = guard.__enter__()
        self._bufs.append((name, guard))
        return t

    def free(self, *names):
        keep = []
        for name, g in reversed(self._bufs):
            if name in names:
                g.__exit__(None, None, None)
            else:
                keep.append((name, g))
        self._bufs = keep[::-1]

    def barrier(self):
        blk = self.blk
        targets = [(s, s.v) for s in (self.sd, self.sg, self.sq, self.sp,
                                      self.sv, self.sa, self.sm)]
        for sec in (blk.sync, blk.tensor, blk.vector, blk.scalar, blk.gpsimd):
            def f(eng):
                for sem, val in targets:
                    eng.wait_ge(sem.h, val)
            sec(f)


def make_consts(ctx):
    blk = ctx.blk
    c = {}
    c["id16"] = ctx.alloc("id16", [128, 128], BF16)
    c["id32"] = ctx.alloc("id32", [128, 128], F32)
    c["ones32"] = ctx.alloc("ones32", [1, 128], F32)
    c["ones16"] = ctx.alloc("ones16", [1, 128], BF16)
    cc = ctx.alloc("cc", [128, 10], F32)
    c["sel16"] = ctx.alloc("sel16", [16, 16 * 128], F32)
    c["cc"] = cc
    c["c_one"] = cc[:, 0:1]
    c["c_neg1"] = cc[:, 1:2]
    c["c_inv1024"] = cc[:, 2:3]
    c["c_eps_ln"] = cc[:, 3:4]
    c["c_eps_rms"] = cc[:, 4:5]
    c["c_inv64"] = cc[:, 5:6]
    c["c_inv32"] = cc[:, 6:7]
    c["c_eps8"] = cc[:, 7:8]
    c["c_half"] = cc[:, 8:9]

    def f(g):
        ctx.sm.inc(g.memset(c["id16"][:, :], 0.0))
        ctx.sm.inc(g.memset(c["id32"][:, :], 0.0))
        ctx.sm.inc(g.memset(c["sel16"][:, :], 0.0))
        ctx.sm.inc(g.memset(c["ones32"][:, :], 1.0))
        ctx.sm.inc(g.memset(c["ones16"][:, :], 1.0))
        for i, v in enumerate([1.0, -1.0, 1.0 / 1024, 1e-5, 1e-6, 1.0 / 64, 1.0 / 32, 1e-8, 0.5, 0.0]):
            ctx.sm.inc(g.memset(cc[:, i:i + 1], v))
        g.wait_ge(ctx.sm.h, ctx.sm.v)
        for t in (c["id16"], c["id32"]):
            ctx.sm.inc(g.affine_select(out=t[:, :], in_=t[:, :], compare_op=ALU.not_equal,
                            fill=1.0, base=0, pattern=[[-1, 128]], channel_multiplier=1))
        sel3 = c["sel16"].ap().rearrange("p (n j) -> p n j", n=16)
        ctx.sm.inc(g.affine_select(out=sel3, in_=sel3, compare_op=ALU.not_equal,
                        fill=1.0, base=0, pattern=[[-1, 16], [0, 128]],
                        channel_multiplier=1))
    blk.gpsimd(f)
    ctx.barrier()
    return c


def emit_ln(ctx, c, x_tile, scratch, xn16_tile):
    blk = ctx.blk
    cent, sq, m_col, v_col, r_col = scratch

    def fv(v):
        ctx.sv.inc(v.reduce_sum(m_col, x_tile[:, :], axis=AX.X))
        v.wait_ge(ctx.sv.h, ctx.sv.v)
        ctx.sv.inc(v.tensor_scalar_mul(m_col, m_col, 1.0 / 1024))
        v.wait_ge(ctx.sv.h, ctx.sv.v)
        ctx.sv.inc(v.tensor_scalar(cent[:, :], x_tile[:, :], m_col, None, ALU.subtract))
    blk.vector(fv)

    def fa(s):
        s.wait_ge(ctx.sv.h, ctx.sv.v)
        ctx.sa.inc(s.activation(sq[:, :], cent[:, :], AF.Square, accum_out=v_col))
        s.wait_ge(ctx.sa.h, ctx.sa.v)
        ctx.sa.inc(s.activation(v_col, v_col, AF.Sqrt, bias=c["c_eps_ln"], scale=c["c_inv1024"]))
    blk.scalar(fa)

    def fv2(v):
        v.wait_ge(ctx.sa.h, ctx.sa.v)
        ctx.sv.inc(v.reciprocal(r_col, v_col))
        v.wait_ge(ctx.sv.h, ctx.sv.v)
        ctx.sv.inc(v.tensor_scalar_mul(xn16_tile[:, :], cent[:, :], r_col))
    blk.vector(fv2)


def emit_mamba(ctx, c, x_in, x_out, W_dt, b_dt, W_B, W_C, A_log, D_in, dbg=None):
    blk = ctx.blk

    def dump(name, src_ap):
        if dbg is None or name not in dbg:
            return
        dst = dbg[name]

        def fg(g):
            g.wait_ge(ctx.sv.h, ctx.sv.v)
            g.wait_ge(ctx.sa.h, ctx.sa.v)
            g.wait_ge(ctx.sp.h, ctx.sp.v)
            ctx.sg.inc(g.dma_start(out=dst, in_=src_ap))
        blk.gpsimd(fg)
        ctx.barrier()
    N_ST = 16
    HALF = 1024

    wdtT = ctx.alloc("wdtT", [128, 8 * 1024], BF16)
    wbT = ctx.alloc("wbT", [128, 8 * 16], BF16)
    wcT = ctx.alloc("wcT", [128, 8 * 16], BF16)
    bdt_col = ctx.alloc("bdt_col", [128, 8], F32)
    d_col = ctx.alloc("d_col", [128, 8], F32)
    a_sb = ctx.alloc("a_sb", [128, 128], F32)
    wload = ctx.alloc("wload", [128, 1024], F32)
    wload16 = ctx.alloc("wload16", [128, 1024], BF16)
    p16 = ctx.alloc("p_tr16", [128, 128], BF16, psum=True)
    p32 = ctx.alloc("p_tr32", [128, 128], F32, psum=True)

    for i in range(8):
        def fs(s, i=i):
            s.wait_ge(ctx.sp.h, ctx.sp.v)
            ctx.sd.inc(s.dma_start(out=wload[:, :], in_=W_dt[i * 128:(i + 1) * 128, :]))
        blk.sync(fs)

        def fa(a):
            a.wait_ge(ctx.sd.h, ctx.sd.v)
            a.wait_ge(ctx.sp.h, ctx.sp.v)
            ctx.sa.inc(a.copy(wload16[:, :], wload[:, :]))
        blk.scalar(fa)
        for k in range(8):
            def ft(t, i=i, k=k):
                t.wait_ge(ctx.sa.h, ctx.sa.v)
                t.wait_ge(ctx.sv.h, ctx.sv.v)
                ctx.sp.inc(t.transpose(p16[:, :], wload16[:, k * 128:(k + 1) * 128], c["id16"][:, :]))
            blk.tensor(ft)

            def fv(v, i=i, k=k):
                v.wait_ge(ctx.sp.h, ctx.sp.v)
                ctx.sv.inc(v.tensor_copy(wdtT[:, k * 1024 + i * 128: k * 1024 + (i + 1) * 128], p16[:, :]))
            blk.vector(fv)

    for (W, dst) in ((W_B, wbT), (W_C, wcT)):
        def fs(s, W=W):
            s.wait_ge(ctx.sp.h, ctx.sp.v)
            ctx.sd.inc(s.dma_start(out=wload[0:16, :], in_=W[:, :]))
        blk.sync(fs)

        def fa(a):
            a.wait_ge(ctx.sd.h, ctx.sd.v)
            a.wait_ge(ctx.sp.h, ctx.sp.v)
            ctx.sa.inc(a.copy(wload16[0:16, :], wload[0:16, :]))
        blk.scalar(fa)
        for k in range(8):
            def ft(t, k=k):
                t.wait_ge(ctx.sa.h, ctx.sa.v)
                t.wait_ge(ctx.sv.h, ctx.sv.v)
                ctx.sp.inc(t.transpose(p16[0:128, 0:16], wload16[0:16, k * 128:(k + 1) * 128], c["id16"][0:16, 0:16]))
            blk.tensor(ft)

            def fv(v, dst=dst, k=k):
                v.wait_ge(ctx.sp.h, ctx.sp.v)
                ctx.sv.inc(v.tensor_copy(dst[:, k * 16:(k + 1) * 16], p16[0:128, 0:16]))
            blk.vector(fv)

    for (W, dst) in ((b_dt, bdt_col), (D_in, d_col)):
        def fs(s, W=W):
            s.wait_ge(ctx.sp.h, ctx.sp.v)
            ctx.sd.inc(s.dma_start(out=wload[0:8, 0:128], in_=W.rearrange("(c p) -> c p", p=128)))
        blk.sync(fs)

        def ft(t):
            t.wait_ge(ctx.sd.h, ctx.sd.v)
            t.wait_ge(ctx.sv.h, ctx.sv.v)
            ctx.sp.inc(t.transpose(p32[0:128, 0:8], wload[0:8, 0:128], c["id32"][0:8, 0:8]))
        blk.tensor(ft)

        def fv(v, dst=dst):
            v.wait_ge(ctx.sp.h, ctx.sp.v)
            ctx.sv.inc(v.tensor_copy(dst[:, :], p32[0:128, 0:8]))
        blk.vector(fv)

    def fs(s):
        s.wait_ge(ctx.sp.h, ctx.sp.v)
        ctx.sd.inc(s.dma_start(out=wload[:, 0:128].rearrange("p (c n) -> p c n", c=8),
                               in_=A_log.rearrange("(c p) n -> p c n", p=128)))
    blk.sync(fs)

    def fa(a):
        a.wait_ge(ctx.sd.h, ctx.sd.v)
        ctx.sa.inc(a.activation(a_sb[:, :], wload[:, 0:128], AF.Exp))
    blk.scalar(fa)

    def fv(v):
        v.wait_ge(ctx.sa.h, ctx.sa.v)
        ctx.sv.inc(v.tensor_scalar_mul(a_sb[:, :], a_sb[:, :], -1.0))
    blk.vector(fv)
    ctx.barrier()

    xnT16 = ctx.alloc("m_xnT16", [128, 8 * HALF], BF16)
    xnT32 = ctx.alloc("m_xnT32", [128, 8 * HALF], F32)
    x_tile = ctx.alloc("m_xt", [128, 1024], F32)
    cent = ctx.alloc("m_cent", [128, 1024], F32)
    lncol = ctx.alloc("m_lncol", [128, 4], F32)
    xn16_t = ctx.alloc("m_xn16t", [128, 1024], BF16)
    dtT = ctx.alloc("m_dtT", [128, HALF], F32)
    u_c = ctx.alloc("m_u", [128, HALF], F32)
    v_sb = ctx.alloc("m_vsb", [128, HALF], F32)
    sp_a = ctx.alloc("m_spa", [128, HALF], F32)
    bp_bc = ctx.alloc("m_bpbc", [128, HALF], F32)
    cp_bc = ctx.alloc("m_cpbc", [128, HALF], F32)
    da_sb = ctx.alloc("m_da", [128, HALF], F32)
    dbx_sb = ctx.alloc("m_dbx", [128, HALF], F32)
    h_sb = ctx.alloc("m_h", [128, HALF], F32)
    y_acc = ctx.alloc("m_yacc", [128, HALF], F32)
    hc_sb = ctx.alloc("m_hc", [128, HALF], F32)
    y_tok = ctx.alloc("m_ytok", [128, 8 * 1024], F32)
    bpT = ctx.alloc("m_bpT", [16, HALF], F32)
    cpT = ctx.alloc("m_cpT", [16, HALF], F32)
    state = ctx.alloc("m_state", [128, 128], F32)
    pdt = ctx.alloc("m_pdt", [128, 512], F32, psum=True)
    pbc = ctx.alloc("m_pbc", [128, 512], F32, psum=True)
    pbp = ctx.alloc("m_pbp", [16, 512], F32, psum=True)

    ln_scratch = (cent, x_tile, lncol[:, 0:1], lncol[:, 1:2], lncol[:, 2:3])

    for b in range(B_PER_CORE):
        for half in range(2):
            t0 = half * HALF
            for ti in range(8):
                def fs(s, b=b, t0=t0, ti=ti):
                    s.wait_ge(ctx.sa.h, ctx.sa.v)
                    s.wait_ge(ctx.sv.h, ctx.sv.v)
                    ctx.sd.inc(s.dma_start(out=x_tile[:, :],
                                           in_=x_in[b, t0 + ti * 128: t0 + (ti + 1) * 128, :]))
                blk.sync(fs)

                def fv0(v):
                    v.wait_ge(ctx.sd.h, ctx.sd.v)
                blk.vector(fv0)
                emit_ln(ctx, c, x_tile, ln_scratch, xn16_t)

                def fv1(v):
                    ctx.sv.inc(v.tensor_scalar_mul(cent[:, :], cent[:, :], lncol[:, 2:3]))
                blk.vector(fv1)
                for k in range(8):
                    def ft(t, k=k):
                        t.wait_ge(ctx.sv.h, ctx.sv.v)
                        t.wait_ge(ctx.sa.h, ctx.sa.v)
                        ctx.sp.inc(t.transpose(p16[:, :], xn16_t[:, k * 128:(k + 1) * 128], c["id16"][:, :]))
                        ctx.sp.inc(t.transpose(p32[:, :], cent[:, k * 128:(k + 1) * 128], c["id32"][:, :]))
                    blk.tensor(ft)

                    def fa2(a, k=k, ti=ti):
                        a.wait_ge(ctx.sp.h, ctx.sp.v)
                        a.copy(xnT16[:, k * HALF + ti * 128: k * HALF + (ti + 1) * 128], p16[:, :])
                        ctx.sa.inc(a.copy(xnT32[:, k * HALF + ti * 128: k * HALF + (ti + 1) * 128], p32[:, :]))
                    blk.scalar(fa2)

            if b == 0 and half == 0:
                dump("xnT16", xnT16[:, :])
                dump("xnT32", xnT32[:, :])
            for (wT, dstT) in ((wbT, bpT), (wcT, cpT)):
                for tg in range(HALF // 512):
                    def ft(t, wT=wT, tg=tg):
                        t.wait_ge(ctx.sa.h, ctx.sa.v)
                        t.wait_ge(ctx.sv.h, ctx.sv.v)
                        for k in range(8):
                            ins = t.matmul(pbp[0:16, :], wT[:, k * 16:(k + 1) * 16],
                                           xnT16[:, k * HALF + tg * 512: k * HALF + (tg + 1) * 512],
                                           start=(k == 0), stop=(k == 7))
                        ctx.sp.inc(ins)
                    blk.tensor(ft)

                    def fv(v, dstT=dstT, tg=tg):
                        v.wait_ge(ctx.sp.h, ctx.sp.v)
                        ctx.sv.inc(v.tensor_copy(dstT[:, tg * 512:(tg + 1) * 512], pbp[0:16, :]))
                    blk.vector(fv)

            if b == 0 and half == 0:
                dump("bpT", bpT[:, :])
                dump("cpT", cpT[:, :])
            for cdx in range(8):
                for tg in range(HALF // 512):
                    def ft(t, cdx=cdx, tg=tg):
                        t.wait_ge(ctx.sv.h, ctx.sv.v)
                        t.wait_ge(ctx.sa.h, ctx.sa.v)
                        for k in range(8):
                            ins = t.matmul(pdt[:, :],
                                           wdtT[:, k * 1024 + cdx * 128: k * 1024 + (cdx + 1) * 128],
                                           xnT16[:, k * HALF + tg * 512: k * HALF + (tg + 1) * 512],
                                           start=(k == 0), stop=(k == 7))
                        ctx.sp.inc(ins)
                    blk.tensor(ft)

                    def fv(v, cdx=cdx, tg=tg):
                        v.wait_ge(ctx.sp.h, ctx.sp.v)
                        sl = slice(tg * 512, (tg + 1) * 512)
                        ctx.sv.inc(v.tensor_scalar(v_sb[:, sl], pdt[:, :],
                                                   bdt_col[:, cdx:cdx + 1], None, ALU.add))
                    blk.vector(fv)
                def fa(a, cdx=cdx):
                    a.wait_ge(ctx.sv.h, ctx.sv.v)
                    a.activation(sp_a[:, :], v_sb[:, :], AF.Abs)
                    a.activation(sp_a[:, :], sp_a[:, :], AF.Exp, scale=c["c_neg1"])
                    ctx.sa.inc(a.activation(sp_a[:, :], sp_a[:, :], AF.Ln, bias=c["c_one"]))
                blk.scalar(fa)

                def fv(v, cdx=cdx):
                    v.wait_ge(ctx.sa.h, ctx.sa.v)
                    v.tensor_scalar_max(dtT[:, :], v_sb[:, :], 0.0)
                    v.tensor_add(dtT[:, :], dtT[:, :], sp_a[:, :])
                    ctx.sv.inc(v.tensor_mul(u_c[:, :], dtT[:, :], xnT32[:, cdx * HALF:(cdx + 1) * HALF]))
                blk.vector(fv)

                if b == 0 and half == 0 and cdx == 0:
                    dump("dtT0", dtT[:, :])
                    dump("u0", u_c[:, :])
                for n in range(N_ST):
                    for tg in range(HALF // 512):
                        def ft(t, n=n, tg=tg):
                            t.wait_ge(ctx.sa.h, ctx.sa.v)
                            t.wait_ge(ctx.sv.h, ctx.sv.v)
                            ctx.sp.inc(t.matmul(pbc[:, :], c["sel16"][:, n * 128:(n + 1) * 128],
                                                bpT[0:16, tg * 512:(tg + 1) * 512],
                                                start=True, stop=True))
                        blk.tensor(ft)

                        def fa2(a, n=n, tg=tg):
                            a.wait_ge(ctx.sp.h, ctx.sp.v)
                            ctx.sa.inc(a.copy(bp_bc[:, tg * 512:(tg + 1) * 512], pbc[:, :]))
                        blk.scalar(fa2)

                        def ft2(t, n=n, tg=tg):
                            t.wait_ge(ctx.sa.h, ctx.sa.v)
                            ctx.sp.inc(t.matmul(pbc[:, :], c["sel16"][:, n * 128:(n + 1) * 128],
                                                cpT[0:16, tg * 512:(tg + 1) * 512],
                                                start=True, stop=True))
                        blk.tensor(ft2)

                        def fa3(a, n=n, tg=tg):
                            a.wait_ge(ctx.sp.h, ctx.sp.v)
                            ctx.sa.inc(a.copy(cp_bc[:, tg * 512:(tg + 1) * 512], pbc[:, :]))
                        blk.scalar(fa3)

                    def fv1(v, n=n, cdx=cdx):
                        v.wait_ge(ctx.sa.h, ctx.sa.v)
                        ctx.sv.inc(v.tensor_scalar_mul(da_sb[:, :], dtT[:, :],
                                                       a_sb[:, cdx * 16 + n: cdx * 16 + n + 1]))
                    blk.vector(fv1)

                    def fa4(a):
                        a.wait_ge(ctx.sv.h, ctx.sv.v)
                        ctx.sa.inc(a.activation(da_sb[:, :], da_sb[:, :], AF.Exp))
                    blk.scalar(fa4)

                    def fv2(v, half=half, n=n, cdx=cdx):
                        v.wait_ge(ctx.sa.h, ctx.sa.v)
                        v.tensor_mul(dbx_sb[:, :], u_c[:, :], bp_bc[:, :])
                        col = cdx * 16 + n
                        init = 0.0 if half == 0 else state[:, col:col + 1]
                        ctx.sv.inc(v.tensor_tensor_scan(h_sb[:, :], da_sb[:, :], dbx_sb[:, :],
                                             init, ALU.mult, ALU.add))
                        v.wait_ge(ctx.sv.h, ctx.sv.v)
                        v.tensor_mul(hc_sb[:, :], h_sb[:, :], cp_bc[:, :])
                        if n == 0:
                            v.tensor_copy(y_acc[:, :], hc_sb[:, :])
                        else:
                            v.tensor_add(y_acc[:, :], y_acc[:, :], hc_sb[:, :])
                        ctx.sv.inc(v.tensor_copy(state[:, col:col + 1], h_sb[:, HALF - 1:HALF]))
                    blk.vector(fv2)

                if b == 0 and half == 0 and cdx == 0:
                    dump("bp_bc0", bp_bc[:, :])
                    dump("da0", da_sb[:, :])
                    dump("h0", h_sb[:, :])
                    dump("yacc0", y_acc[:, :])
                def fv3(v, cdx=cdx):
                    ctx.sv.inc(v.scalar_tensor_tensor(y_acc[:, :], xnT32[:, cdx * HALF:(cdx + 1) * HALF],
                                                      d_col[:, cdx:cdx + 1], y_acc[:, :], ALU.mult, ALU.add))
                blk.vector(fv3)

                for ti in range(8):
                    def ft(t, ti=ti):
                        t.wait_ge(ctx.sv.h, ctx.sv.v)
                        t.wait_ge(ctx.sa.h, ctx.sa.v)
                        ctx.sp.inc(t.transpose(p32[:, :], y_acc[:, ti * 128:(ti + 1) * 128], c["id32"][:, :]))
                    blk.tensor(ft)

                    def fa5(a, ti=ti, cdx=cdx):
                        a.wait_ge(ctx.sp.h, ctx.sp.v)
                        ctx.sa.inc(a.copy(y_tok[:, ti * 1024 + cdx * 128: ti * 1024 + (cdx + 1) * 128],
                                          p32[:, :]))
                    blk.scalar(fa5)

            for ti in range(8):
                def fs(s, b=b, t0=t0, ti=ti):
                    s.wait_ge(ctx.sv.h, ctx.sv.v)
                    s.wait_ge(ctx.sa.h, ctx.sa.v)
                    ctx.sd.inc(s.dma_start(out=x_tile[:, :],
                                           in_=x_in[b, t0 + ti * 128: t0 + (ti + 1) * 128, :]))
                blk.sync(fs)

                def fv4(v, ti=ti):
                    v.wait_ge(ctx.sd.h, ctx.sd.v)
                    v.wait_ge(ctx.sa.h, ctx.sa.v)
                    ctx.sv.inc(v.tensor_add(y_tok[:, ti * 1024:(ti + 1) * 1024],
                                            y_tok[:, ti * 1024:(ti + 1) * 1024], x_tile[:, :]))
                blk.vector(fv4)

                def fg(g, b=b, t0=t0, ti=ti):
                    g.wait_ge(ctx.sv.h, ctx.sv.v)
                    ctx.sg.inc(g.dma_start(out=x_out[b, t0 + ti * 128: t0 + (ti + 1) * 128, :],
                                           in_=y_tok[:, ti * 1024:(ti + 1) * 1024]))
                blk.gpsimd(fg)
            ctx.barrier()

    ctx.free("wdtT", "wbT", "wcT", "bdt_col", "d_col", "a_sb", "wload", "wload16",
             "p_tr16", "p_tr32", "m_xnT16", "m_xnT32", "m_xt", "m_cent", "m_lncol",
             "m_xn16t", "m_dtT", "m_u", "m_vsb", "m_spa", "m_bpbc", "m_cpbc",
             "m_da", "m_dbx", "m_h", "m_yacc", "m_hc", "m_ytok", "m_bpT", "m_cpT",
             "m_state", "m_pdt", "m_pbc", "m_pbp")


def finish(ctx):
    blk = ctx.blk

    def fg(g):
        g.wait_ge(ctx.sg.h, ctx.sg.v)
        g.wait_ge(ctx.sd.h, ctx.sd.v)
        g.wait_ge(ctx.sq.h, ctx.sq.v)
    blk.gpsimd(fg)



def emit_ffn1_stage(ctx, c, W1, W2, Wg):
    nc, blk = ctx.nc, ctx.blk
    st = {}
    st["w1s"] = nc.dram_tensor("w1s", [8, 1024, 4096], BF16, kind="Internal").ap()
    st["w2s"] = nc.dram_tensor("w2s", [8, 4096, 1024], BF16, kind="Internal").ap()
    st["wg"] = ctx.alloc("f1_wg", [128, 8 * 8], F32)
    wld = ctx.alloc("f1_wld", [128, 2 * 4096], F32)
    wst = ctx.alloc("f1_wst", [128, 2 * 4096], BF16)

    tiles = []
    for e in range(8):
        for k in range(8):
            tiles.append((W1[e, k * 128:(k + 1) * 128, :],
                          st["w1s"][e, k * 128:(k + 1) * 128, :]))
    for e in range(8):
        for k in range(0, 32, 4):
            tiles.append((W2[e, k * 128:(k + 4) * 128, :].rearrange("(a p) h -> p a h", p=128),
                          st["w2s"][e, k * 128:(k + 4) * 128, :].rearrange("(a p) h -> p a h", p=128)))
    conv_done = [0, 0]
    store_done = [0, 0]
    load_done = [0, 0]
    for j, (src_ap, dst_ap) in enumerate(tiles):
        b = j % 2
        def fs(s, src_ap=src_ap, b=b):
            s.wait_ge(ctx.sa.h, conv_done[b])
            dst = wld[:, b * 4096:(b + 1) * 4096]
            if len(src_ap.shape) == 3:
                dst = dst.rearrange("p (a h) -> p a h", a=src_ap.shape[1])
            ctx.sd.inc(s.dma_start(out=dst, in_=src_ap))
            load_done[b] = ctx.sd.v
        blk.sync(fs)

        def fa(a, b=b):
            a.wait_ge(ctx.sd.h, load_done[b])
            a.wait_ge(ctx.sg.h, store_done[b])
            ctx.sa.inc(a.copy(wst[:, b * 4096:(b + 1) * 4096], wld[:, b * 4096:(b + 1) * 4096]))
            conv_done[b] = ctx.sa.v
        blk.scalar(fa)

        def fg(g, dst_ap=dst_ap, b=b):
            g.wait_ge(ctx.sa.h, conv_done[b])
            src2 = wst[:, b * 4096:(b + 1) * 4096]
            if len(dst_ap.shape) == 3:
                src2 = src2.rearrange("p (a h) -> p a h", a=dst_ap.shape[1])
            ctx.sg.inc(g.dma_start(out=dst_ap, in_=src2))
            store_done[b] = ctx.sg.v
        blk.gpsimd(fg)
    ctx.barrier()

    def fs(s):
        ctx.sd.inc(s.dma_start(out=wld[0:128, 1024:1024 + 64].rearrange("p (k e) -> p k e", k=8),
                               in_=Wg.rearrange("(k p) e -> p k e", p=128)))
    blk.sync(fs)

    def fa(a):
        a.wait_ge(ctx.sd.h, ctx.sd.v)
        ctx.sa.inc(a.copy(st["wg"][:, :], wld[0:128, 1024:1024 + 64]))
    blk.scalar(fa)
    ctx.barrier()
    ctx.free("f1_wld", "f1_wst")
    return st


def emit_ffn1_small(ctx, c, st, b1, b2):
    blk = ctx.blk
    st["b1c"] = ctx.alloc("f1_b1c", [128, 8 * 32], F32)
    st["b2r"] = ctx.alloc("f1_b2r", [1, 8 * 1024], BF16)
    wld = ctx.alloc("f1_wld2", [128, 8192], F32)
    pt = ctx.alloc("f1_pt2", [128, 128], F32, psum=True)

    def fs(s):
        s.wait_ge(ctx.sa.h, ctx.sa.v)
        s.wait_ge(ctx.sv.h, ctx.sv.v)
        for e in range(8):
            ctx.sd.inc(s.dma_start(out=wld[0:32, e * 128:(e + 1) * 128],
                                   in_=b1[e].rearrange("(a p) -> a p", p=128)))
        ctx.sd.inc(s.dma_start(out=wld[0:1, 4096:8192],
                               in_=b2.rearrange("e h -> (e h)").unsqueeze(0)[:, 0:4096]))
        ctx.sd.inc(s.dma_start(out=wld[32:33, 4096:8192],
                               in_=b2.rearrange("e h -> (e h)").unsqueeze(0)[:, 4096:8192]))
    blk.sync(fs)

    def fa(a):
        a.wait_ge(ctx.sd.h, ctx.sd.v)
        ctx.sa.inc(a.copy(st["b2r"][0:1, 0:4096], wld[0:1, 4096:8192]))
        ctx.sa.inc(a.copy(st["b2r"][0:1, 4096:8192], wld[32:33, 4096:8192]))
    blk.scalar(fa)
    for e in range(8):
        def ft(t, e=e):
            t.wait_ge(ctx.sd.h, ctx.sd.v)
            t.wait_ge(ctx.sv.h, ctx.sv.v)
            ctx.sp.inc(t.transpose(pt[0:128, 0:32], wld[0:32, e * 128:(e + 1) * 128],
                                   c["id32"][0:32, 0:32]))
        blk.tensor(ft)

        def fv(v, e=e):
            v.wait_ge(ctx.sp.h, ctx.sp.v)
            ctx.sv.inc(v.tensor_copy(st["b1c"][:, e * 32:(e + 1) * 32], pt[0:128, 0:32]))
        blk.vector(fv)
    ctx.barrier()
    ctx.free("f1_wld2", "f1_pt2")


def emit_ffn1(ctx, c, st, x_in, x_out, rank_mask, tag, cc_in, cc_all, dbg=None):
    nc, blk = ctx.nc, ctx.blk

    def dump(name, src_ap):
        if dbg is None or name not in dbg:
            return
        dst = dbg[name]

        def fgd(g):
            g.wait_ge(ctx.sv.h, ctx.sv.v)
            g.wait_ge(ctx.sa.h, ctx.sa.v)
            g.wait_ge(ctx.sp.h, ctx.sp.v)
            ctx.sg.inc(g.dma_start(out=dst, in_=src_ap))
        blk.gpsimd(fgd)
        ctx.barrier()
    NTOK = B_PER_CORE * T_SEQ
    NCH = NTOK // 128
    CAP = 4096.0

    x_tile = ctx.alloc("f_xt", [128, 1024], F32)
    cent = ctx.alloc("f_cent", [128, 1024], F32)
    lncol = ctx.alloc("f_lncol", [128, 4], F32)
    xn16_t = ctx.alloc("f_xn16t", [128, 1024], BF16)
    sel_all = ctx.alloc("f_sel", [128, NCH * 8], F32)
    xnT_tmp = ctx.alloc("f_xnTt", [128, 1024], F32)
    xn32_t = ctx.alloc("f_xn32t", [128, 1024], F32)
    gates_sb = ctx.alloc("f_gates", [128, 8], F32)
    gcol = ctx.alloc("f_gcol", [128, 4], F32)
    mask1_all = ctx.alloc("f_mask1", [128, NCH * 8], F32)
    g1_all = ctx.alloc("f_g1", [128, NCH], F32)
    m1T = ctx.alloc("f_m1T", [8, NTOK], F32)
    cumT = ctx.alloc("f_cumT", [8, NTOK], F32)
    ones_w = ctx.alloc("f_onesw", [8, NTOK], F32)
    ones8 = ctx.alloc("f_ones8", [8, 1], F32)
    cnt_sb = ctx.alloc("f_cnt", [8, 8], F32)
    rkm_sb = ctx.alloc("f_rkm", [8, 1], F32)
    pfx_sb = ctx.alloc("f_pfx", [8, 2], F32)
    crow = ctx.alloc("f_crow", [1, 128], F32)
    keep_row = ctx.alloc("f_keepr", [1, NTOK], F32)
    keep_cp = ctx.alloc("f_keepcp", [32, 128], F32)
    keepcol = ctx.alloc("f_keepc", [128, 32], F32)
    p16 = ctx.alloc("f_p16", [128, 128], BF16, psum=True)
    pg = ctx.alloc("f_pg", [128, 8], F32, psum=True)
    pT = ctx.alloc("f_pT", [128, 128], F32, psum=True)
    pk = ctx.alloc("f_pk", [1, 512], F32, psum=True)
    keep_scr = nc.dram_tensor(f"keep_scr_{tag}", [1, NTOK], F32, kind="Internal").ap()

    ln_scratch = (cent, x_tile, lncol[:, 0:1], lncol[:, 1:2], lncol[:, 2:3])

    def fg0(g):
        ctx.sm.inc(g.memset(ones_w[:, :], 1.0))
        ctx.sm.inc(g.memset(ones8[:, :], 1.0))
        ctx.sm.inc(g.memset(crow[:, :], 0.0))
    blk.gpsimd(fg0)

    def fs0(s):
        ctx.sd.inc(s.dma_start(out=rkm_sb[:, 0:1], in_=rank_mask[:, :]))
    blk.sync(fs0)

    for ch in range(NCH):
        b, trow = divmod(ch * 128, T_SEQ)

        def fs(s, b=b, trow=trow):
            s.wait_ge(ctx.sa.h, ctx.sa.v)
            s.wait_ge(ctx.sv.h, ctx.sv.v)
            ctx.sd.inc(s.dma_start(out=x_tile[:, :], in_=x_in[b, trow:trow + 128, :]))
        blk.sync(fs)

        def fv0(v):
            v.wait_ge(ctx.sd.h, ctx.sd.v)
        blk.vector(fv0)
        emit_ln(ctx, c, x_tile, ln_scratch, xn16_t)
        def fvx(v):
            ctx.sv.inc(v.tensor_scalar_mul(xn32_t[:, :], cent[:, :], lncol[:, 2:3]))
        blk.vector(fvx)
        for k in range(8):
            def ft(t, k=k):
                t.wait_ge(ctx.sv.h, ctx.sv.v)
                t.wait_ge(ctx.sa.h, ctx.sa.v)
                ctx.sp.inc(t.transpose(pT[:, :], xn32_t[:, k * 128:(k + 1) * 128], c["id32"][:, :]))
            blk.tensor(ft)

            def fa(a, k=k):
                a.wait_ge(ctx.sp.h, ctx.sp.v)
                ctx.sa.inc(a.copy(xnT_tmp[:, k * 128:(k + 1) * 128], pT[:, :]))
            blk.scalar(fa)

        def ft2(t):
            t.wait_ge(ctx.sa.h, ctx.sa.v)
            t.wait_ge(ctx.sv.h, ctx.sv.v)
            for k in range(8):
                ins = t.matmul(pg[:, :], xnT_tmp[:, k * 128:(k + 1) * 128],
                               st["wg"][:, k * 8:(k + 1) * 8], start=(k == 0), stop=(k == 7))
            ctx.sp.inc(ins)
        blk.tensor(ft2)

        def fv1(v):
            v.wait_ge(ctx.sp.h, ctx.sp.v)
            ctx.sv.inc(v.reduce_max(gcol[:, 0:1], pg[:, :], axis=AX.X, negate=True))
        blk.vector(fv1)

        def fa2(a):
            a.wait_ge(ctx.sv.h, ctx.sv.v)
            ctx.sa.inc(a.activation(gates_sb[:, :], pg[:, :], AF.Exp,
                                    bias=gcol[:, 0:1], accum_out=gcol[:, 2:3]))
        blk.scalar(fa2)

        def fv2(v, ch=ch):
            v.wait_ge(ctx.sa.h, ctx.sa.v)
            ctx.sv.inc(v.reduce_max(gcol[:, 1:2], gates_sb[:, :], axis=AX.X))
            v.wait_ge(ctx.sv.h, ctx.sv.v)
            v.tensor_scalar(mask1_all[:, ch * 8:(ch + 1) * 8], gates_sb[:, :],
                            gcol[:, 1:2], None, ALU.is_ge)
            ctx.sv.inc(v.reciprocal(gcol[:, 3:4], gcol[:, 2:3]))
            v.wait_ge(ctx.sv.h, ctx.sv.v)
            ctx.sv.inc(v.tensor_tensor(g1_all[:, ch:ch + 1], gcol[:, 1:2], gcol[:, 3:4], ALU.mult))
        blk.vector(fv2)

        def ft3(t, ch=ch):
            t.wait_ge(ctx.sv.h, ctx.sv.v)
            t.wait_ge(ctx.sa.h, ctx.sa.v)
            ctx.sp.inc(t.transpose(pT[0:8, 0:128], mask1_all[:, ch * 8:(ch + 1) * 8],
                                   c["id32"][:, :]))
        blk.tensor(ft3)

        def fa3(a, ch=ch):
            a.wait_ge(ctx.sp.h, ctx.sp.v)
            ctx.sa.inc(a.copy(m1T[:, ch * 128:(ch + 1) * 128], pT[0:8, 0:128]))
        blk.scalar(fa3)

    def fv3(v):
        v.wait_ge(ctx.sa.h, ctx.sa.v)
        v.wait_ge(ctx.sm.h, ctx.sm.v)
        ctx.sv.inc(v.tensor_tensor_scan(cumT[:, :], ones_w[:, :], m1T[:, :], 0.0, ALU.mult, ALU.add))
        v.wait_ge(ctx.sv.h, ctx.sv.v)
        ctx.sv.inc(v.tensor_copy(cnt_sb[:, 0:1], cumT[:, NTOK - 1:NTOK]))
    blk.vector(fv3)

    def ftc(t):
        t.wait_ge(ctx.sv.h, ctx.sv.v)
        t.wait_ge(ctx.sa.h, ctx.sa.v)
        ctx.sp.inc(t.transpose(pT[0:1, 0:8], cnt_sb[:, 0:1], c["id32"][0:8, 0:8]))
    blk.tensor(ftc)

    def fvc(v):
        v.wait_ge(ctx.sp.h, ctx.sp.v)
        v.wait_ge(ctx.sm.h, ctx.sm.v)
        ctx.sv.inc(v.tensor_copy(crow[0:1, 0:8], pT[0:1, 0:8]))
    blk.vector(fvc)

    def fg1(g):
        g.wait_ge(ctx.sv.h, ctx.sv.v)
        ctx.sg.inc(g.dma_start(out=cc_in[:, :], in_=crow[0:1, :]))
        g.wait_ge(ctx.sg.h, ctx.sg.v)
        ctx.sm.inc(g.collective_compute(
            "AllGather", ALU.bypass, replica_groups=[[0, 1, 2, 3, 4, 5, 6, 7]],
            ins=[cc_in.opt()], outs=[cc_all.opt()]))
        g.wait_ge(ctx.sm.h, ctx.sm.v)
        ctx.sg.inc(g.dma_start(out=cnt_sb[:, 0:8], in_=cc_all[:, 0:8]))
    blk.gpsimd(fg1)

    def ft4(t):
        t.wait_ge(ctx.sg.h, ctx.sg.v)
        t.wait_ge(ctx.sd.h, ctx.sd.v)
        t.wait_ge(ctx.sa.h, ctx.sa.v)
        t.wait_ge(ctx.sv.h, ctx.sv.v)
        ctx.sp.inc(t.matmul(pT[0:1, 0:8], rkm_sb[:, 0:1], cnt_sb[:, 0:8], start=True, stop=True))
    blk.tensor(ft4)

    def fv4(v):
        v.wait_ge(ctx.sp.h, ctx.sp.v)
        ctx.sv.inc(v.tensor_copy(cnt_sb[0:1, 0:8], pT[0:1, 0:8]))
    blk.vector(fv4)

    def ft5(t):
        t.wait_ge(ctx.sv.h, ctx.sv.v)
        ctx.sp.inc(t.transpose(pT[0:8, 0:1], cnt_sb[0:1, 0:8], c["id32"][0:1, 0:1]))
    blk.tensor(ft5)

    def fv5(v):
        v.wait_ge(ctx.sp.h, ctx.sp.v)
        ctx.sv.inc(v.tensor_copy(pfx_sb[:, 0:1], pT[0:8, 0:1]))
        v.wait_ge(ctx.sv.h, ctx.sv.v)
        ctx.sv.inc(v.tensor_scalar(cumT[:, :], cumT[:, :], pfx_sb[:, 0:1], None, ALU.add))
        v.wait_ge(ctx.sv.h, ctx.sv.v)
        ctx.sv.inc(v.tensor_scalar(ones_w[:, :], cumT[:, :], CAP, None, ALU.is_le))
        v.wait_ge(ctx.sv.h, ctx.sv.v)
        ctx.sv.inc(v.tensor_mul(m1T[:, :], m1T[:, :], ones_w[:, :]))
    blk.vector(fv5)

    for tg in range(NTOK // 512):
        def ft6(t, tg=tg):
            t.wait_ge(ctx.sv.h, ctx.sv.v)
            t.wait_ge(ctx.sa.h, ctx.sa.v)
            ctx.sp.inc(t.matmul(pk[0:1, :], ones8[:, 0:1], m1T[:, tg * 512:(tg + 1) * 512],
                                start=True, stop=True))
        blk.tensor(ft6)

        def fa4(a, tg=tg):
            a.wait_ge(ctx.sp.h, ctx.sp.v)
            ctx.sa.inc(a.copy(keep_row[:, tg * 512:(tg + 1) * 512], pk[0:1, :]))
        blk.scalar(fa4)

    def fg2(g):
        g.wait_ge(ctx.sa.h, ctx.sa.v)
        ctx.sg.inc(g.dma_start(out=keep_scr[:, :], in_=keep_row[0:1, :]))
        g.wait_ge(ctx.sg.h, ctx.sg.v)
        ctx.sg.inc(g.dma_start(out=keep_cp[:, :], in_=keep_scr.rearrange("o (a p) -> (o a) p", p=128)))
    blk.gpsimd(fg2)

    def ft7(t):
        t.wait_ge(ctx.sg.h, ctx.sg.v)
        t.wait_ge(ctx.sv.h, ctx.sv.v)
        ctx.sp.inc(t.transpose(pT[0:128, 0:32], keep_cp[:, :], c["id32"][0:32, 0:32]))
    blk.tensor(ft7)

    dump("m1T_cap", m1T[:, :])
    dump("cumT", cumT[:, :])
    dump("pfx", pfx_sb[:, 0:1])

    def fv6(v):
        v.wait_ge(ctx.sp.h, ctx.sp.v)
        ctx.sv.inc(v.tensor_copy(keepcol[:, :], pT[0:128, 0:32]))
        v.wait_ge(ctx.sv.h, ctx.sv.v)
        for ch in range(NCH):
            v.tensor_scalar(sel_all[:, ch * 8:(ch + 1) * 8], mask1_all[:, ch * 8:(ch + 1) * 8],
                            g1_all[:, ch:ch + 1], keepcol[:, ch:ch + 1], ALU.mult, ALU.mult)
        ctx.sv.inc(v.tensor_scalar_mul(sel_all[:, :], sel_all[:, :], 0.5))
    blk.vector(fv6)
    dump("sel", sel_all[:, :])
    dump("g1", g1_all[:, :])
    dump("mask1", mask1_all[:, :])
    dump("keepcol", keepcol[:, :])
    ctx.barrier()
    ctx.free("f_xnTt", "f_xn32t", "f_gates", "f_gcol", "f_mask1", "f_g1", "f_m1T", "f_cumT",
             "f_onesw", "f_ones8", "f_cnt", "f_rkm", "f_pfx", "f_crow", "f_keepr", "f_keepcp",
             "f_keepc", "f_p16", "f_pg", "f_pT", "f_pk")

    ST = 1024
    xnT_st = ctx.alloc("f_xnT", [128, 8 * ST], BF16)
    out_acc = ctx.alloc("f_oacc", [128, 8 * 1024], F32)
    h_sb = ctx.alloc("f_h", [128, 8 * ST], BF16)
    w1_sb = ctx.alloc("f_w1", [128, 2 * 8192], BF16)
    w2_sb = ctx.alloc("f_w2", [128, 2 * 8192], BF16)
    p16b = ctx.alloc("f_p16b", [128, 128], BF16, psum=True)
    ph = [ctx.alloc("f_ph0", [128, 512], F32, psum=True),
          ctx.alloc("f_ph1", [128, 512], F32, psum=True)]
    po = [ctx.alloc("f_po0", [128, 512], F32, psum=True),
          ctx.alloc("f_po1", [128, 512], F32, psum=True)]

    for sti in range(NTOK // ST):
        for ti in range(ST // 128):
            gch = sti * 8 + ti
            b, trow = divmod(gch * 128, T_SEQ)

            def fs(s, b=b, trow=trow):
                s.wait_ge(ctx.sa.h, ctx.sa.v)
                s.wait_ge(ctx.sv.h, ctx.sv.v)
                ctx.sd.inc(s.dma_start(out=x_tile[:, :], in_=x_in[b, trow:trow + 128, :]))
            blk.sync(fs)

            def fv0(v):
                v.wait_ge(ctx.sd.h, ctx.sd.v)
            blk.vector(fv0)
            emit_ln(ctx, c, x_tile, ln_scratch, xn16_t)
            for k in range(8):
                def ft(t, k=k):
                    t.wait_ge(ctx.sv.h, ctx.sv.v)
                    t.wait_ge(ctx.sa.h, ctx.sa.v)
                    ctx.sp.inc(t.transpose(p16b[:, :], xn16_t[:, k * 128:(k + 1) * 128], c["id16"][:, :]))
                blk.tensor(ft)

                def fa(a, k=k, ti=ti):
                    a.wait_ge(ctx.sp.h, ctx.sp.v)
                    ctx.sa.inc(a.copy(xnT_st[:, k * ST + ti * 128: k * ST + (ti + 1) * 128], p16b[:, :]))
                blk.scalar(fa)
        ctx.barrier()

        iters = [(e, fq) for e in range(8) for fq in range(4)]
        load_done = [0, 0]
        comp_done = [0, 0]
        ph_evac = [0, 0]
        po_evac = [0, 0]
        for j, (e, fq) in enumerate(iters):
            bw = j % 2

            def fs(s, e=e, fq=fq, bw=bw):
                s.wait_ge(ctx.sp.h, comp_done[bw])
                for k in range(8):
                    ctx.sd.inc(s.dma_start(
                        out=w1_sb[:, bw * 8192 + k * 1024: bw * 8192 + (k + 1) * 1024],
                        in_=st["w1s"][e, k * 128:(k + 1) * 128, fq * 1024:(fq + 1) * 1024]))
                for k in range(8):
                    ctx.sd.inc(s.dma_start(
                        out=w2_sb[:, bw * 8192 + k * 1024: bw * 8192 + (k + 1) * 1024],
                        in_=st["w2s"][e, fq * 1024 + k * 128: fq * 1024 + (k + 1) * 128, :]))
                load_done[bw] = ctx.sd.v
            blk.sync(fs)

            for fc in range(8):
                bh = fc % 2
                for tg in range(ST // 512):
                    def ft(t, fc=fc, tg=tg, bw=bw, bh=bh):
                        t.wait_ge(ctx.sd.h, load_done[bw])
                        t.wait_ge(ctx.sa.h, ph_evac[bh])
                        if fc == 0 and tg == 0:
                            t.wait_ge(ctx.sa.h, ctx.sa.v)
                        for k in range(8):
                            ins = t.matmul(ph[bh][:, :],
                                           w1_sb[:, bw * 8192 + k * 1024 + fc * 128: bw * 8192 + k * 1024 + (fc + 1) * 128],
                                           xnT_st[:, k * ST + tg * 512: k * ST + (tg + 1) * 512],
                                           start=(k == 0), stop=(k == 7))
                        ctx.sp.inc(ins)
                    blk.tensor(ft)

                    def fa(a, e=e, fq=fq, fc=fc, tg=tg, bh=bh):
                        a.wait_ge(ctx.sp.h, ctx.sp.v)
                        ctx.sa.inc(a.activation(h_sb[:, fc * ST + tg * 512: fc * ST + (tg + 1) * 512],
                                                ph[bh][:, :], AF.Relu,
                                                bias=st["b1c"][:, e * 32 + fq * 8 + fc: e * 32 + fq * 8 + fc + 1]))
                        ph_evac[bh] = ctx.sa.v
                    blk.scalar(fa)

            for tc in range(ST // 128):
                gch = sti * 8 + tc
                for hh in range(2):
                    bo = (tc * 2 + hh) % 2

                    def ft(t, e=e, fq=fq, tc=tc, hh=hh, bo=bo, bw=bw):
                        t.wait_ge(ctx.sa.h, ctx.sa.v)
                        t.wait_ge(ctx.sv.h, po_evac[bo])
                        for k in range(8):
                            ins = t.matmul(po[bo][:, :],
                                           h_sb[:, k * ST + tc * 128: k * ST + (tc + 1) * 128],
                                           w2_sb[:, bw * 8192 + k * 1024 + hh * 512: bw * 8192 + k * 1024 + (hh + 1) * 512],
                                           start=(k == 0), stop=(k == 7 and fq != 0))
                        if fq == 0:
                            ins = t.matmul(po[bo][:, :], c["ones16"][0:1, :],
                                           st["b2r"][0:1, e * 1024 + hh * 512: e * 1024 + (hh + 1) * 512],
                                           start=False, stop=True)
                        ctx.sp.inc(ins)
                    blk.tensor(ft)

                    def fv(v, e=e, fq=fq, tc=tc, hh=hh, bo=bo, gch=gch):
                        v.wait_ge(ctx.sp.h, ctx.sp.v)
                        dst = out_acc[:, tc * 1024 + hh * 512: tc * 1024 + (hh + 1) * 512]
                        selc = sel_all[:, gch * 8 + e: gch * 8 + e + 1]
                        if e == 0 and fq == 0:
                            ctx.sv.inc(v.tensor_scalar(dst, po[bo][:, :], selc, None, ALU.mult))
                        else:
                            ctx.sv.inc(v.scalar_tensor_tensor(dst, po[bo][:, :], selc, dst,
                                                              ALU.mult, ALU.add))
                        po_evac[bo] = ctx.sv.v
                    blk.vector(fv)
            comp_done[bw] = ctx.sp.v

        if sti == 0:
            dump("oacc0", out_acc[:, :])
            dump("xnT0", xnT_st[:, :])
        for tc in range(ST // 128):
            gch = sti * 8 + tc
            b, trow = divmod(gch * 128, T_SEQ)

            def fs(s, b=b, trow=trow):
                s.wait_ge(ctx.sv.h, ctx.sv.v)
                s.wait_ge(ctx.sa.h, ctx.sa.v)
                ctx.sd.inc(s.dma_start(out=x_tile[:, :], in_=x_in[b, trow:trow + 128, :]))
            blk.sync(fs)

            def fv7(v, tc=tc):
                v.wait_ge(ctx.sd.h, ctx.sd.v)
                ctx.sv.inc(v.tensor_add(out_acc[:, tc * 1024:(tc + 1) * 1024],
                                        out_acc[:, tc * 1024:(tc + 1) * 1024], x_tile[:, :]))
            blk.vector(fv7)

            def fg(g, b=b, trow=trow, tc=tc):
                g.wait_ge(ctx.sv.h, ctx.sv.v)
                ctx.sg.inc(g.dma_start(out=x_out[b, trow:trow + 128, :],
                                       in_=out_acc[:, tc * 1024:(tc + 1) * 1024]))
            blk.gpsimd(fg)
        ctx.barrier()

    ctx.free("f_xt", "f_cent", "f_lncol", "f_xn16t", "f_sel",
             "f_xnT", "f_oacc", "f_h", "f_w1", "f_w2", "f_p16b",
             "f_ph0", "f_ph1", "f_po0", "f_po1")



def emit_attn(ctx, c, x_in, x_out, Wq, Wkv, Wk, Wv, Wo, q_g, k_g):
    nc, blk = ctx.nc, ctx.blk
    NH, HD, RP, NG = 16, 64, 32, 4

    wkv_sb = ctx.alloc("a_wkv", [128, 8 * 128], BF16)
    qg_eff = ctx.alloc("a_qg", [128, 32], F32)
    kg_bc = ctx.alloc("a_kg", [128, 32], F32)
    mask_var = ctx.alloc("a_mask", [128, 4 * 512], F32)
    grow = ctx.alloc("a_grow", [1, 64], F32)
    wldq = ctx.alloc("a_wld", [128, 1024], F32)
    p32a = ctx.alloc("a_p32", [128, 512], F32, psum=True)

    for k in range(8):
        def fs(s, k=k):
            s.wait_ge(ctx.sa.h, ctx.sa.v)
            ctx.sd.inc(s.dma_start(out=wldq[:, 0:128], in_=Wkv[k * 128:(k + 1) * 128, :]))
        blk.sync(fs)

        def fa(a, k=k):
            a.wait_ge(ctx.sd.h, ctx.sd.v)
            ctx.sa.inc(a.copy(wkv_sb[:, k * 128:(k + 1) * 128], wldq[:, 0:128]))
        blk.scalar(fa)

    def fs3(s):
        s.wait_ge(ctx.sa.h, ctx.sa.v)
        ctx.sd.inc(s.dma_start(out=grow[0:1, 0:32], in_=q_g.unsqueeze(0)[:, 0:32]))
        ctx.sd.inc(s.dma_start(out=grow[0:1, 32:64], in_=k_g.unsqueeze(0)))
    blk.sync(fs3)

    def ft(t):
        t.wait_ge(ctx.sd.h, ctx.sd.v)
        ctx.sp.inc(t.matmul(p32a[:, 0:32], c["ones32"][0:1, :], grow[0:1, 0:32],
                            start=True, stop=True))
        ctx.sp.inc(t.matmul(p32a[:, 32:64], c["ones32"][0:1, :], grow[0:1, 32:64],
                            start=True, stop=True))
    blk.tensor(ft)

    def fv(v):
        v.wait_ge(ctx.sp.h, ctx.sp.v)
        ctx.sv.inc(v.tensor_scalar_mul(qg_eff[:, :], p32a[:, 0:32], 0.125))
        ctx.sv.inc(v.tensor_copy(kg_bc[:, :], p32a[:, 32:64]))
    blk.vector(fv)

    def fg(g):
        for r in range(4):
            ctx.sm.inc(g.memset(mask_var[:, r * 512:(r + 1) * 512], 0.0))
        g.wait_ge(ctx.sm.h, ctx.sm.v)
        for r in range(4):
            ctx.sm.inc(g.affine_select(
                out=mask_var[:, r * 512:(r + 1) * 512],
                in_=mask_var[:, r * 512:(r + 1) * 512],
                compare_op=ALU.is_ge, fill=-1e30, base=r * 128,
                pattern=[[-1, 512]], channel_multiplier=1))
    blk.gpsimd(fg)
    ctx.barrier()
    ctx.free("a_p32")

    xnT = ctx.alloc("a_xnT", [128, 8 * 2048], BF16)
    out_acc = ctx.alloc("a_oacc", [128, 16 * 1024], F32)
    x_tile = ctx.alloc("a_xt", [128, 1024], F32)
    cent = ctx.alloc("a_cent", [128, 1024], F32)
    lncol = ctx.alloc("a_lncol", [128, 4], F32)
    xn16_t = ctx.alloc("a_xn16t", [128, 1024], BF16)
    wq_g = ctx.alloc("a_wqg", [128, 8 * 256], BF16)
    wk_g = ctx.alloc("a_wkg", [128, 128], BF16)
    wv_g = ctx.alloc("a_wvg", [128, 128], BF16)
    wo_g = ctx.alloc("a_wog", [32, 4 * 1024], BF16)
    kvT = ctx.alloc("a_kvT", [128, 2048], BF16)
    q_sb = ctx.alloc("a_q", [128, 256], F32)
    qsc_sb = ctx.alloc("a_qsc", [128, 128], BF16)
    rcol = ctx.alloc("a_rcol", [128, 4], F32)
    qT = ctx.alloc("a_qT", [32, 4 * 2048], BF16)
    kT = ctx.alloc("a_kT", [32, 4 * 2048], BF16)
    v_g = ctx.alloc("a_vg", [128, 16 * 128], BF16)
    scores = ctx.alloc("a_scores", [128, 2048], F32)
    probs = ctx.alloc("a_probs", [128, 2048], BF16)
    smcol = ctx.alloc("a_smcol", [128, 2], F32)
    pbT_buf = ctx.alloc("a_pbT", [128, 2048], BF16)
    oc_sb = ctx.alloc("a_oc", [32, 128], BF16)
    p16 = ctx.alloc("a_p16", [128, 128], BF16, psum=True)
    ps = [ctx.alloc("a_ps0", [128, 512], F32, psum=True),
          ctx.alloc("a_ps1", [128, 512], F32, psum=True)]
    pp = [ctx.alloc("a_pp0", [128, 128], BF16, psum=True),
          ctx.alloc("a_pp1", [128, 128], BF16, psum=True)]
    pav = ctx.alloc("a_pav", [32, 128], F32, psum=True)
    po = [ctx.alloc("a_po0", [128, 512], F32, psum=True),
          ctx.alloc("a_po1", [128, 512], F32, psum=True)]

    ln_scratch = (cent, x_tile, lncol[:, 0:1], lncol[:, 1:2], lncol[:, 2:3])

    for b in range(B_PER_CORE):
        for ti in range(16):
            def fs(s, b=b, ti=ti):
                s.wait_ge(ctx.sa.h, ctx.sa.v)
                s.wait_ge(ctx.sv.h, ctx.sv.v)
                ctx.sd.inc(s.dma_start(out=x_tile[:, :], in_=x_in[b, ti * 128:(ti + 1) * 128, :]))
            blk.sync(fs)

            def fv0(v):
                v.wait_ge(ctx.sd.h, ctx.sd.v)
            blk.vector(fv0)
            emit_ln(ctx, c, x_tile, ln_scratch, xn16_t)
            for k in range(8):
                def ft(t, k=k):
                    t.wait_ge(ctx.sv.h, ctx.sv.v)
                    t.wait_ge(ctx.sa.h, ctx.sa.v)
                    ctx.sp.inc(t.transpose(p16[:, :], xn16_t[:, k * 128:(k + 1) * 128], c["id16"][:, :]))
                blk.tensor(ft)

                def fa(a, k=k, ti=ti):
                    a.wait_ge(ctx.sp.h, ctx.sp.v)
                    ctx.sa.inc(a.copy(xnT[:, k * 2048 + ti * 128: k * 2048 + (ti + 1) * 128], p16[:, :]))
                blk.scalar(fa)
        for ti in range(16):
            def ft3(t, ti=ti):
                t.wait_ge(ctx.sa.h, ctx.sa.v)
                t.wait_ge(ctx.sv.h, ctx.sv.v)
                for k in range(8):
                    ins = t.matmul(ps[0][:, 0:128],
                                   xnT[:, k * 2048 + ti * 128: k * 2048 + (ti + 1) * 128],
                                   wkv_sb[:, k * 128:(k + 1) * 128],
                                   start=(k == 0), stop=(k == 7))
                ctx.sp.inc(ins)
            blk.tensor(ft3)

            def fv4(v):
                v.wait_ge(ctx.sp.h, ctx.sp.v)
                ctx.sv.inc(v.tensor_copy(xn16_t[:, 0:128], ps[0][:, 0:128]))
            blk.vector(fv4)

            def ft4(t, ti=ti):
                t.wait_ge(ctx.sv.h, ctx.sv.v)
                t.wait_ge(ctx.sa.h, ctx.sa.v)
                ctx.sp.inc(t.transpose(p16[:, :], xn16_t[:, 0:128], c["id16"][:, :]))
            blk.tensor(ft4)

            def fa6(a, ti=ti):
                a.wait_ge(ctx.sp.h, ctx.sp.v)
                ctx.sa.inc(a.copy(kvT[:, ti * 128:(ti + 1) * 128], p16[:, :]))
            blk.scalar(fa6)

        for hg in range(NG):
            def fsw(s, hg=hg):
                s.wait_ge(ctx.sa.h, ctx.sa.v)
                s.wait_ge(ctx.sp.h, ctx.sp.v)
                for k in range(8):
                    ctx.sd.inc(s.dma_start(out=wldq[:, k * 128:(k + 1) * 128],
                                           in_=Wq[k * 128:(k + 1) * 128, hg * 256: hg * 256 + 128]))
            blk.sync(fsw)

            def faw0(a):
                a.wait_ge(ctx.sd.h, ctx.sd.v)
                for k in range(8):
                    ins = a.copy(wq_g[:, k * 256: k * 256 + 128], wldq[:, k * 128:(k + 1) * 128])
                ctx.sa.inc(ins)
            blk.scalar(faw0)

            def fsw3(s, hg=hg):
                s.wait_ge(ctx.sa.h, ctx.sa.v)
                for k in range(8):
                    ctx.sd.inc(s.dma_start(out=wldq[:, k * 128:(k + 1) * 128],
                                           in_=Wq[k * 128:(k + 1) * 128, hg * 256 + 128: hg * 256 + 256]))
            blk.sync(fsw3)

            def faw1(a):
                a.wait_ge(ctx.sd.h, ctx.sd.v)
                for k in range(8):
                    ins = a.copy(wq_g[:, k * 256 + 128: k * 256 + 256], wldq[:, k * 128:(k + 1) * 128])
                ctx.sa.inc(ins)
            blk.scalar(faw1)

            def fsw4(s, hg=hg):
                s.wait_ge(ctx.sa.h, ctx.sa.v)
                ctx.sd.inc(s.dma_start(out=wldq[:, 0:128], in_=Wk[:, hg * 128:(hg + 1) * 128]))
                ctx.sd.inc(s.dma_start(out=wldq[:, 128:256], in_=Wv[:, hg * 128:(hg + 1) * 128]))
            blk.sync(fsw4)

            def faw2(a):
                a.wait_ge(ctx.sd.h, ctx.sd.v)
                a.copy(wk_g[:, :], wldq[:, 0:128])
                ctx.sa.inc(a.copy(wv_g[:, :], wldq[:, 128:256]))
            blk.scalar(faw2)
            for hi in range(4):
                def fsw5(s, hg=hg, hi=hi):
                    s.wait_ge(ctx.sa.h, ctx.sa.v)
                    h = hg * 4 + hi
                    ctx.sd.inc(s.dma_start(out=wldq[0:32, 0:1024], in_=Wo[h * 64: h * 64 + 32, :]))
                blk.sync(fsw5)

                def faw3(a, hi=hi):
                    a.wait_ge(ctx.sd.h, ctx.sd.v)
                    ctx.sa.inc(a.copy(wo_g[0:32, hi * 1024:(hi + 1) * 1024], wldq[0:32, 0:1024]))
                blk.scalar(faw3)

            for ti in range(16):
                def ftq(t, ti=ti):
                    t.wait_ge(ctx.sa.h, ctx.sa.v)
                    t.wait_ge(ctx.sv.h, ctx.sv.v)
                    for k in range(8):
                        ins = t.matmul(ps[0][:, 0:256],
                                       xnT[:, k * 2048 + ti * 128: k * 2048 + (ti + 1) * 128],
                                       wq_g[:, k * 256:(k + 1) * 256],
                                       start=(k == 0), stop=(k == 7))
                    ctx.sp.inc(ins)
                blk.tensor(ftq)

                def fvq(v):
                    v.wait_ge(ctx.sp.h, ctx.sp.v)
                    ctx.sv.inc(v.tensor_copy(q_sb[:, :], ps[0][:, 0:256]))
                blk.vector(fvq)

                def faq(a):
                    a.wait_ge(ctx.sv.h, ctx.sv.v)
                    ctx.sa.inc(a.activation(cent[:, 0:256], q_sb[:, :], AF.Square))
                blk.scalar(faq)

                def fvq2(v):
                    v.wait_ge(ctx.sa.h, ctx.sa.v)
                    ctx.sv.inc(v.reduce_sum(rcol[:, 0:4], cent.ap()[:, 0:256].rearrange("p (h d) -> p h d", h=4), axis=AX.X))
                blk.vector(fvq2)

                def faq2(a):
                    a.wait_ge(ctx.sv.h, ctx.sv.v)
                    ctx.sa.inc(a.activation(rcol[:, 0:4], rcol[:, 0:4], AF.Sqrt,
                                            scale=c["c_inv64"], bias=c["c_eps_rms"]))
                blk.scalar(faq2)

                def fvq3(v):
                    v.wait_ge(ctx.sa.h, ctx.sa.v)
                    ctx.sv.inc(v.reciprocal(rcol[:, 0:4], rcol[:, 0:4]))
                    v.wait_ge(ctx.sv.h, ctx.sv.v)
                    for hi in range(4):
                        v.tensor_scalar_mul(cent[:, hi * 32:(hi + 1) * 32],
                                            q_sb[:, hi * 64: hi * 64 + 32], rcol[:, hi:hi + 1])
                    for hi in range(4):
                        ins = v.tensor_mul(qsc_sb[:, hi * 32:(hi + 1) * 32],
                                           cent[:, hi * 32:(hi + 1) * 32], qg_eff[:, :])
                    ctx.sv.inc(ins)
                blk.vector(fvq3)
                for hi in range(4):
                    def ftq2(t, hi=hi, ti=ti):
                        t.wait_ge(ctx.sv.h, ctx.sv.v)
                        t.wait_ge(ctx.sa.h, ctx.sa.v)
                        ctx.sp.inc(t.transpose(p16[0:32, 0:128], qsc_sb[:, hi * 32:(hi + 1) * 32], c["id16"][:, :]))
                    blk.tensor(ftq2)

                    def faq3(a, hi=hi, ti=ti):
                        a.wait_ge(ctx.sp.h, ctx.sp.v)
                        ctx.sa.inc(a.copy(qT[0:32, hi * 2048 + ti * 128: hi * 2048 + (ti + 1) * 128], p16[0:32, 0:128]))
                    blk.scalar(faq3)

                def ftk(t, ti=ti):
                    t.wait_ge(ctx.sa.h, ctx.sa.v)
                    t.wait_ge(ctx.sv.h, ctx.sv.v)
                    ctx.sp.inc(t.matmul(ps[1][:, 0:128], kvT[:, ti * 128:(ti + 1) * 128], wk_g[:, :],
                                        start=True, stop=True))
                    ctx.sp.inc(t.matmul(ps[1][:, 128:256], kvT[:, ti * 128:(ti + 1) * 128], wv_g[:, :],
                                        start=True, stop=True))
                blk.tensor(ftk)

                def fvk(v, ti=ti):
                    v.wait_ge(ctx.sp.h, ctx.sp.v)
                    ctx.sv.inc(v.tensor_copy(q_sb[:, 0:128], ps[1][:, 0:128]))
                    ctx.sv.inc(v.tensor_copy(v_g[:, ti * 128:(ti + 1) * 128], ps[1][:, 128:256]))
                blk.vector(fvk)

                def fak(a):
                    a.wait_ge(ctx.sv.h, ctx.sv.v)
                    ctx.sa.inc(a.activation(cent[:, 0:128], q_sb[:, 0:128], AF.Square))
                blk.scalar(fak)

                def fvk2(v):
                    v.wait_ge(ctx.sa.h, ctx.sa.v)
                    ctx.sv.inc(v.reduce_sum(rcol[:, 0:4], cent.ap()[:, 0:128].rearrange("p (h d) -> p h d", h=4), axis=AX.X))
                blk.vector(fvk2)

                def fak2(a):
                    a.wait_ge(ctx.sv.h, ctx.sv.v)
                    ctx.sa.inc(a.activation(rcol[:, 0:4], rcol[:, 0:4], AF.Sqrt,
                                            scale=c["c_inv32"], bias=c["c_eps_rms"]))
                blk.scalar(fak2)

                def fvk3(v):
                    v.wait_ge(ctx.sa.h, ctx.sa.v)
                    ctx.sv.inc(v.reciprocal(rcol[:, 0:4], rcol[:, 0:4]))
                    v.wait_ge(ctx.sv.h, ctx.sv.v)
                    for hi in range(4):
                        v.tensor_scalar_mul(cent[:, hi * 32:(hi + 1) * 32],
                                            q_sb[:, hi * 32:(hi + 1) * 32], rcol[:, hi:hi + 1])
                    for hi in range(4):
                        ins = v.tensor_mul(qsc_sb[:, hi * 32:(hi + 1) * 32],
                                           cent[:, hi * 32:(hi + 1) * 32], kg_bc[:, :])
                    ctx.sv.inc(ins)
                blk.vector(fvk3)
                for hi in range(4):
                    def ftk2(t, hi=hi, ti=ti):
                        t.wait_ge(ctx.sv.h, ctx.sv.v)
                        t.wait_ge(ctx.sa.h, ctx.sa.v)
                        ctx.sp.inc(t.transpose(p16[0:32, 0:128], qsc_sb[:, hi * 32:(hi + 1) * 32], c["id16"][:, :]))
                    blk.tensor(ftk2)

                    def fak3(a, hi=hi, ti=ti):
                        a.wait_ge(ctx.sp.h, ctx.sp.v)
                        ctx.sa.inc(a.copy(kT[0:32, hi * 2048 + ti * 128: hi * 2048 + (ti + 1) * 128], p16[0:32, 0:128]))
                    blk.scalar(fak3)

            for tq in range(16):
                nkc = (tq * 128 + 128 + 511) // 512
                dkc = nkc - 1
                for hi in range(4):
                    for kc in range(nkc):
                        bs = kc % 2
                        def ft7(t, tq=tq, hi=hi, kc=kc, bs=bs):
                            t.wait_ge(ctx.sv.h, ctx.sv.v)
                            t.wait_ge(ctx.sa.h, ctx.sa.v)
                            ctx.sp.inc(t.matmul(ps[bs][:, :],
                                                qT[0:32, hi * 2048 + tq * 128: hi * 2048 + (tq + 1) * 128],
                                                kT[0:32, hi * 2048 + kc * 512: hi * 2048 + (kc + 1) * 512],
                                                start=True, stop=True))
                        blk.tensor(ft7)

                        def fv8(v, tq=tq, kc=kc, bs=bs, dkc=dkc):
                            v.wait_ge(ctx.sp.h, ctx.sp.v)
                            if kc == dkc:
                                ctx.sv.inc(v.tensor_add(scores[:, kc * 512:(kc + 1) * 512],
                                                        ps[bs][:, :], mask_var[:, (tq % 4) * 512:(tq % 4 + 1) * 512]))
                            else:
                                ctx.sv.inc(v.tensor_copy(scores[:, kc * 512:(kc + 1) * 512], ps[bs][:, :]))
                        blk.vector(fv8)

                    L = nkc * 512

                    def fv9(v, L=L):
                        ctx.sv.inc(v.reduce_max(smcol[:, 0:1], scores[:, 0:L], axis=AX.X, negate=True))
                    blk.vector(fv9)

                    def fa10(a, L=L):
                        a.wait_ge(ctx.sv.h, ctx.sv.v)
                        ctx.sa.inc(a.activation(probs[:, 0:L], scores[:, 0:L], AF.Exp,
                                                bias=smcol[:, 0:1], accum_out=smcol[:, 1:2]))
                    blk.scalar(fa10)

                    def fv10(v, L=L):
                        v.wait_ge(ctx.sa.h, ctx.sa.v)
                        ctx.sv.inc(v.reciprocal(smcol[:, 1:2], smcol[:, 1:2]))
                        v.wait_ge(ctx.sv.h, ctx.sv.v)
                        ctx.sv.inc(v.tensor_scalar_mul(probs[:, 0:L], probs[:, 0:L], smcol[:, 1:2]))
                    blk.vector(fv10)

                    probs_ready = ctx.sv.v
                    pp_free = [ctx.sa.v, ctx.sa.v]
                    for kt in range(tq + 1):
                        bq = kt % 2

                        def ft8(t, kt=kt, bq=bq, probs_ready=probs_ready,
                                ppf=pp_free[bq]):
                            t.wait_ge(ctx.sv.h, probs_ready)
                            t.wait_ge(ctx.sa.h, ppf)
                            ctx.sp.inc(t.transpose(pp[bq][:, :], probs[:, kt * 128:(kt + 1) * 128], c["id16"][:, :]))
                        blk.tensor(ft8)
                        tr_done = ctx.sp.v

                        def fa11(a, kt=kt, bq=bq, tr_done=tr_done):
                            a.wait_ge(ctx.sp.h, tr_done)
                            ctx.sa.inc(a.copy(pbT_buf[:, kt * 128:(kt + 1) * 128], pp[bq][:, :]))
                        blk.scalar(fa11)
                        pp_free[bq] = ctx.sa.v
                    copies_done = ctx.sa.v

                    def ft9(t, tq=tq, hi=hi, copies_done=copies_done):
                        t.wait_ge(ctx.sa.h, copies_done)
                        for kt in range(tq + 1):
                            ins = t.matmul(pav[0:32, :],
                                           v_g[:, kt * 128 + hi * 32: kt * 128 + (hi + 1) * 32],
                                           pbT_buf[:, kt * 128:(kt + 1) * 128],
                                           start=(kt == 0), stop=(kt == tq))
                        ctx.sp.inc(ins)
                    blk.tensor(ft9)

                    def fv11(v):
                        v.wait_ge(ctx.sp.h, ctx.sp.v)
                        ctx.sv.inc(v.tensor_copy(oc_sb[0:32, :], pav[0:32, :]))
                    blk.vector(fv11)

                    for nh in range(2):
                        def ft10(t, hi=hi, nh=nh):
                            t.wait_ge(ctx.sv.h, ctx.sv.v)
                            ctx.sp.inc(t.matmul(po[nh][:, :], oc_sb[0:32, :],
                                                wo_g[0:32, hi * 1024 + nh * 512: hi * 1024 + (nh + 1) * 512],
                                                start=(hi == 0), stop=(hi == 3)))
                        blk.tensor(ft10)

                def fv13(v, hg=hg, tq=tq):
                    v.wait_ge(ctx.sp.h, ctx.sp.v)
                    dst0 = out_acc[:, tq * 1024: tq * 1024 + 512]
                    dst1 = out_acc[:, tq * 1024 + 512: (tq + 1) * 1024]
                    if hg == 0:
                        v.tensor_copy(dst0, po[0][:, :])
                        ctx.sv.inc(v.tensor_copy(dst1, po[1][:, :]))
                    else:
                        v.tensor_add(dst0, dst0, po[0][:, :])
                        ctx.sv.inc(v.tensor_add(dst1, dst1, po[1][:, :]))
                blk.vector(fv13)

        for tq in range(16):
            def fs4(s, b=b, tq=tq):
                s.wait_ge(ctx.sa.h, ctx.sa.v)
                s.wait_ge(ctx.sv.h, ctx.sv.v)
                ctx.sd.inc(s.dma_start(out=x_tile[:, :], in_=x_in[b, tq * 128:(tq + 1) * 128, :]))
            blk.sync(fs4)

            def fv12(v, tq=tq):
                v.wait_ge(ctx.sd.h, ctx.sd.v)
                ctx.sv.inc(v.tensor_add(out_acc[:, tq * 1024:(tq + 1) * 1024],
                                        out_acc[:, tq * 1024:(tq + 1) * 1024], x_tile[:, :]))
            blk.vector(fv12)

            def fg2(g, b=b, tq=tq):
                g.wait_ge(ctx.sv.h, ctx.sv.v)
                ctx.sg.inc(g.dma_start(out=x_out[b, tq * 128:(tq + 1) * 128, :],
                                       in_=out_acc[:, tq * 1024:(tq + 1) * 1024]))
            blk.gpsimd(fg2)
        ctx.barrier()

    ctx.free("a_wkv", "a_qg", "a_kg", "a_mask", "a_grow", "a_wld",
             "a_xnT", "a_oacc", "a_xt", "a_cent", "a_lncol", "a_xn16t",
             "a_wqg", "a_wkg", "a_wvg", "a_wog", "a_kvT", "a_q", "a_qsc", "a_rcol",
             "a_qT", "a_kT", "a_vg", "a_scores", "a_probs", "a_smcol", "a_pbT",
             "a_oc", "a_p16", "a_ps0", "a_ps1", "a_pp0", "a_pp1", "a_pav", "a_po0", "a_po1")



def emit_ffn0_stage(ctx, c, eg_w, eu_w, ed_w, sg_w, su_w, sd_w, gate_w, gate_bias):
    nc, blk = ctx.nc, ctx.blk
    st = {}
    st["eg"] = nc.dram_tensor("eg_s", [66, 512, 1024], BF16, kind="Internal").ap()
    st["eu"] = nc.dram_tensor("eu_s", [66, 512, 1024], BF16, kind="Internal").ap()
    st["ed"] = nc.dram_tensor("ed_s", [66, 1024, 512], BF16, kind="Internal").ap()
    st["gwT"] = ctx.alloc("f0_gwT", [128, 8 * 64], F32)
    st["gb"] = ctx.alloc("f0_gb", [128, 64], F32)
    wld = ctx.alloc("f0_wld", [128, 2 * 1024], F32)
    wst = ctx.alloc("f0_wst", [128, 2 * 1024], BF16)
    pt = ctx.alloc("f0_pt", [128, 128], F32, psum=True)

    tiles = []
    for e in range(64):
        for k in range(4):
            tiles.append((eg_w[e, k * 128:(k + 1) * 128, :], st["eg"][e, k * 128:(k + 1) * 128, :]))
            tiles.append((eu_w[e, k * 128:(k + 1) * 128, :], st["eu"][e, k * 128:(k + 1) * 128, :]))
        for k in range(8):
            tiles.append((ed_w[e, k * 128:(k + 1) * 128, :],
                          st["ed"][e, k * 128:(k + 1) * 128, :]))
    for half in range(2):
        for k in range(4):
            r = half * 512 + k * 128
            tiles.append((sg_w[r:r + 128, :], st["eg"][64 + half, k * 128:(k + 1) * 128, :]))
            tiles.append((su_w[r:r + 128, :], st["eu"][64 + half, k * 128:(k + 1) * 128, :]))
        for k in range(8):
            tiles.append((sd_w[k * 128:(k + 1) * 128, half * 512:(half + 1) * 512],
                          st["ed"][64 + half, k * 128:(k + 1) * 128, :]))
    conv_done = [0, 0]
    store_done = [0, 0]
    load_done = [0, 0]
    for j, (src_ap, dst_ap) in enumerate(tiles):
        b = j % 2
        w = src_ap.shape[1]
        def fs(s, src_ap=src_ap, b=b, w=w):
            s.wait_ge(ctx.sa.h, conv_done[b])
            ctx.sd.inc(s.dma_start(out=wld[:, b * 1024: b * 1024 + w], in_=src_ap))
            load_done[b] = ctx.sd.v
        blk.sync(fs)

        def fa(a, b=b, w=w):
            a.wait_ge(ctx.sd.h, load_done[b])
            a.wait_ge(ctx.sg.h, store_done[b])
            ctx.sa.inc(a.copy(wst[:, b * 1024: b * 1024 + w], wld[:, b * 1024: b * 1024 + w]))
            conv_done[b] = ctx.sa.v
        blk.scalar(fa)

        def fg(g, dst_ap=dst_ap, b=b, w=w):
            g.wait_ge(ctx.sa.h, conv_done[b])
            ctx.sg.inc(g.dma_start(out=dst_ap, in_=wst[:, b * 1024: b * 1024 + w]))
            store_done[b] = ctx.sg.v
        blk.gpsimd(fg)
    ctx.barrier()

    def fs(s):
        ctx.sd.inc(s.dma_start(out=wld[0:64, 0:1024], in_=gate_w[:, :]))
        ctx.sd.inc(s.dma_start(out=wld[0:1, 1024:1088], in_=gate_bias.unsqueeze(0)))
    blk.sync(fs)
    for k in range(8):
        def ft(t, k=k):
            t.wait_ge(ctx.sd.h, ctx.sd.v)
            t.wait_ge(ctx.sv.h, ctx.sv.v)
            ctx.sp.inc(t.transpose(pt[0:128, 0:64], wld[0:64, k * 128:(k + 1) * 128],
                                   c["id32"][0:64, 0:64]))
        blk.tensor(ft)

        def fv(v, k=k):
            v.wait_ge(ctx.sp.h, ctx.sp.v)
            ctx.sv.inc(v.tensor_copy(st["gwT"][:, k * 64:(k + 1) * 64], pt[0:128, 0:64]))
        blk.vector(fv)

    def ft2(t):
        t.wait_ge(ctx.sd.h, ctx.sd.v)
        t.wait_ge(ctx.sv.h, ctx.sv.v)
        ctx.sp.inc(t.matmul(pt[0:128, 0:64], c["ones32"][0:1, :], wld[0:1, 1024:1088],
                            start=True, stop=True))
    blk.tensor(ft2)

    def fv2(v):
        v.wait_ge(ctx.sp.h, ctx.sp.v)
        ctx.sv.inc(v.tensor_copy(st["gb"][:, :], pt[0:128, 0:64]))
    blk.vector(fv2)
    ctx.barrier()
    ctx.free("f0_wld", "f0_wst", "f0_pt")
    return st


def emit_ffn0(ctx, c, st, x_in, x_out, tag):
    nc, blk = ctx.nc, ctx.blk
    NTOK = B_PER_CORE * T_SEQ
    ST = 2048
    NCH = ST // 128
    BIG = 1e30

    x_tile = ctx.alloc("g_xt", [128, 1024], F32)
    cent = ctx.alloc("g_cent", [128, 1024], F32)
    lncol = ctx.alloc("g_lncol", [128, 4], F32)
    xn16_t = ctx.alloc("g_xn16t", [128, 1024], BF16)
    xn32_t = ctx.alloc("g_xn32t", [128, 1024], F32)
    xnT32_tmp = ctx.alloc("g_xnT32", [128, 1024], F32)
    xnT = ctx.alloc("g_xnT", [128, 8 * ST], BF16)
    out_acc = ctx.alloc("g_oacc", [128, 16 * 1024], F32)
    comb_all = ctx.alloc("g_comb", [128, NCH * 64], F32)
    sc_sb = ctx.alloc("g_sc", [128, 64], F32)
    ws_sb = ctx.alloc("g_ws", [128, 64], F32)
    gs_sb = ctx.alloc("g_gs", [128, 24], F32)
    hw_sb = ctx.alloc("g_h", [128, 4 * ST], BF16)
    htmp = ctx.alloc("g_htmp", [128, 2 * 512], BF16)
    weg = [ctx.alloc("g_weg0", [128, 8 * 512], BF16), ctx.alloc("g_weg1", [128, 8 * 512], BF16)]
    weu = [ctx.alloc("g_weu0", [128, 8 * 512], BF16), ctx.alloc("g_weu1", [128, 8 * 512], BF16)]
    wed = [ctx.alloc("g_wed0", [128, 4 * 1024], BF16), ctx.alloc("g_wed1", [128, 4 * 1024], BF16)]
    p16 = ctx.alloc("g_p16", [128, 128], BF16, psum=True)
    pT = ctx.alloc("g_pT", [128, 128], F32, psum=True)
    pg = pT[:, 0:64]
    pA = [ctx.alloc("g_pA0", [128, 512], F32, psum=True),
          ctx.alloc("g_pA1", [128, 512], F32, psum=True)]
    pB = [ctx.alloc("g_pB0", [128, 512], F32, psum=True),
          ctx.alloc("g_pB1", [128, 512], F32, psum=True)]
    po = [ctx.alloc("g_po0", [128, 512], F32, psum=True),
          ctx.alloc("g_po1", [128, 512], F32, psum=True)]

    ln_scratch = (cent, x_tile, lncol[:, 0:1], lncol[:, 1:2], lncol[:, 2:3])

    for sti in range(NTOK // ST):
        for ti in range(NCH):
            gch = sti * NCH + ti
            b, trow = divmod(gch * 128, T_SEQ)

            def fs(s, b=b, trow=trow):
                s.wait_ge(ctx.sa.h, ctx.sa.v)
                s.wait_ge(ctx.sv.h, ctx.sv.v)
                ctx.sd.inc(s.dma_start(out=x_tile[:, :], in_=x_in[b, trow:trow + 128, :]))
            blk.sync(fs)

            def fv0(v):
                v.wait_ge(ctx.sd.h, ctx.sd.v)
            blk.vector(fv0)
            emit_ln(ctx, c, x_tile, ln_scratch, xn16_t)

            def fvx(v):
                ctx.sv.inc(v.tensor_scalar_mul(xn32_t[:, :], cent[:, :], lncol[:, 2:3]))
            blk.vector(fvx)
            for k in range(8):
                def ft(t, k=k):
                    t.wait_ge(ctx.sv.h, ctx.sv.v)
                    t.wait_ge(ctx.sa.h, ctx.sa.v)
                    ctx.sp.inc(t.transpose(p16[:, :], xn16_t[:, k * 128:(k + 1) * 128], c["id16"][:, :]))
                    ctx.sp.inc(t.transpose(pT[:, :], xn32_t[:, k * 128:(k + 1) * 128], c["id32"][:, :]))
                blk.tensor(ft)

                def fa(a, k=k, ti=ti):
                    a.wait_ge(ctx.sp.h, ctx.sp.v)
                    a.copy(xnT[:, k * ST + ti * 128: k * ST + (ti + 1) * 128], p16[:, :])
                    ctx.sa.inc(a.copy(xnT32_tmp[:, k * 128:(k + 1) * 128], pT[:, :]))
                blk.scalar(fa)

            def ftg(t):
                t.wait_ge(ctx.sa.h, ctx.sa.v)
                t.wait_ge(ctx.sv.h, ctx.sv.v)
                for k in range(8):
                    ins = t.matmul(pg, xnT32_tmp[:, k * 128:(k + 1) * 128],
                                   st["gwT"][:, k * 64:(k + 1) * 64], start=(k == 0), stop=(k == 7))
                ctx.sp.inc(ins)
            blk.tensor(ftg)

            def fag(a):
                a.wait_ge(ctx.sp.h, ctx.sp.v)
                ctx.sa.inc(a.activation(sc_sb[:, :], pg, AF.Sigmoid))
            blk.scalar(fag)

            def fvg(v, ti=ti):
                v.wait_ge(ctx.sa.h, ctx.sa.v)
                ctx.sv.inc(v.tensor_add(sc_sb[:, :], sc_sb[:, :], st["gb"][:, :]))
                v.wait_ge(ctx.sv.h, ctx.sv.v)
                ctx.sv.inc(v.reduce_max(gs_sb[:, 0:8],
                                        sc_sb.ap().rearrange("p (g e) -> p g e", g=8), axis=AX.X))
                v.wait_ge(ctx.sv.h, ctx.sv.v)
                ctx.sv.inc(v.tensor_copy(gs_sb[:, 8:16], gs_sb[:, 0:8]))
                v.wait_ge(ctx.sv.h, ctx.sv.v)
                for it in range(4):
                    ctx.sv.inc(v.reduce_max(gs_sb[:, 16 + it:17 + it], gs_sb[:, 8:16], axis=AX.X))
                    v.wait_ge(ctx.sv.h, ctx.sv.v)
                    if it < 3:
                        ctx.sv.inc(v.tensor_scalar(ws_sb[:, 0:8], gs_sb[:, 8:16],
                                                   gs_sb[:, 16 + it:17 + it], None, ALU.is_ge))
                        v.wait_ge(ctx.sv.h, ctx.sv.v)
                        ctx.sv.inc(v.scalar_tensor_tensor(gs_sb[:, 8:16], ws_sb[:, 0:8], -BIG,
                                                          gs_sb[:, 8:16], ALU.mult, ALU.add))
                        v.wait_ge(ctx.sv.h, ctx.sv.v)
                ctx.sv.inc(v.tensor_scalar(gs_sb[:, 8:16], gs_sb[:, 0:8],
                                           gs_sb[:, 19:20], None, ALU.is_ge))
                v.wait_ge(ctx.sv.h, ctx.sv.v)
                gmb = gs_sb.ap()[:, 8:16].unsqueeze(2).broadcast_to([128, 8, 8])
                ctx.sv.inc(v.tensor_tensor(ws_sb.ap().rearrange("p (g e) -> p g e", g=8),
                                           sc_sb.ap().rearrange("p (g e) -> p g e", g=8),
                                           gmb, ALU.mult))
                v.wait_ge(ctx.sv.h, ctx.sv.v)
                ctx.sv.inc(v.tensor_copy(sc_sb[:, :], ws_sb[:, :]))
                v.wait_ge(ctx.sv.h, ctx.sv.v)
                for it in range(6):
                    ctx.sv.inc(v.reduce_max(gs_sb[:, 16 + it:17 + it], sc_sb[:, :], axis=AX.X))
                    v.wait_ge(ctx.sv.h, ctx.sv.v)
                    if it < 5:
                        ctx.sv.inc(v.tensor_scalar(comb_all[:, ti * 64:(ti + 1) * 64], sc_sb[:, :],
                                                   gs_sb[:, 16 + it:17 + it], None, ALU.is_ge))
                        v.wait_ge(ctx.sv.h, ctx.sv.v)
                        ctx.sv.inc(v.scalar_tensor_tensor(sc_sb[:, :], comb_all[:, ti * 64:(ti + 1) * 64],
                                                          -BIG, sc_sb[:, :], ALU.mult, ALU.add))
                        v.wait_ge(ctx.sv.h, ctx.sv.v)
                ctx.sv.inc(v.tensor_scalar(sc_sb[:, :], ws_sb[:, :], gs_sb[:, 21:22],
                                           None, ALU.is_ge))
                ctx.sv.inc(v.tensor_scalar(comb_all[:, ti * 64:(ti + 1) * 64], ws_sb[:, :], 0.0,
                                           None, ALU.is_gt))
                v.wait_ge(ctx.sv.h, ctx.sv.v)
                ctx.sv.inc(v.tensor_mul(sc_sb[:, :], sc_sb[:, :], comb_all[:, ti * 64:(ti + 1) * 64]))
                v.wait_ge(ctx.sv.h, ctx.sv.v)
                ctx.sv.inc(v.tensor_mul(ws_sb[:, :], ws_sb[:, :], sc_sb[:, :]))
                v.wait_ge(ctx.sv.h, ctx.sv.v)
                ctx.sv.inc(v.reduce_sum(gs_sb[:, 22:23], ws_sb[:, :], axis=AX.X))
                v.wait_ge(ctx.sv.h, ctx.sv.v)
                ctx.sv.inc(v.tensor_scalar_add(gs_sb[:, 22:23], gs_sb[:, 22:23], 1e-8))
                v.wait_ge(ctx.sv.h, ctx.sv.v)
                ctx.sv.inc(v.reciprocal(gs_sb[:, 22:23], gs_sb[:, 22:23]))
                v.wait_ge(ctx.sv.h, ctx.sv.v)
                ctx.sv.inc(v.tensor_scalar(comb_all[:, ti * 64:(ti + 1) * 64], ws_sb[:, :],
                                           gs_sb[:, 22:23], 0.5, ALU.mult, ALU.mult))
            blk.vector(fvg)
        ctx.barrier()

        load_done = [0, 0]
        comp_done = [0, 0]
        po_evac = [0, 0]
        hAB_done = [0, 0]
        for e in range(66):
            bw = e % 2

            def fsw(s, e=e, bw=bw):
                s.wait_ge(ctx.sp.h, comp_done[bw])
                for k in range(8):
                    ctx.sd.inc(s.dma_start(out=weg[bw][:, k * 512:(k + 1) * 512],
                                           in_=st["eg"][e, 0:512, k * 128:(k + 1) * 128],
                                           transpose=True))
                for k in range(8):
                    ctx.sd.inc(s.dma_start(out=weu[bw][:, k * 512:(k + 1) * 512],
                                           in_=st["eu"][e, 0:512, k * 128:(k + 1) * 128],
                                           transpose=True))
                for k in range(4):
                    ctx.sd.inc(s.dma_start(out=wed[bw][:, k * 1024:(k + 1) * 1024],
                                           in_=st["ed"][e, 0:1024, k * 128:(k + 1) * 128],
                                           transpose=True))
                load_done[bw] = ctx.sd.v
            blk.sync(fsw)

            for m in range(4):
                for tg in range(ST // 512):
                    jj = m * (ST // 512) + tg
                    bp = jj % 2

                    def fth(t, e=e, bw=bw, m=m, tg=tg, bp=bp):
                        t.wait_ge(ctx.sd.h, load_done[bw])
                        t.wait_ge(ctx.sv.h, hAB_done[bp])
                        if m == 0 and tg == 0:
                            t.wait_ge(ctx.sa.h, ctx.sa.v)
                        for k in range(8):
                            t.matmul(pA[bp][:, :], weg[bw][:, k * 512 + m * 128: k * 512 + (m + 1) * 128],
                                     xnT[:, k * ST + tg * 512: k * ST + (tg + 1) * 512],
                                     start=(k == 0), stop=(k == 7))
                        for k in range(8):
                            ins = t.matmul(pB[bp][:, :], weu[bw][:, k * 512 + m * 128: k * 512 + (m + 1) * 128],
                                           xnT[:, k * ST + tg * 512: k * ST + (tg + 1) * 512],
                                           start=(k == 0), stop=(k == 7))
                        ctx.sp.inc(ins)
                    blk.tensor(fth)

                    def fah(a, m=m, tg=tg, bp=bp):
                        a.wait_ge(ctx.sp.h, ctx.sp.v)
                        ctx.sa.inc(a.activation(htmp[:, bp * 512:(bp + 1) * 512], pA[bp][:, :], AF.Silu))
                    blk.scalar(fah)

                    def fvh(v, m=m, tg=tg, bp=bp):
                        v.wait_ge(ctx.sa.h, ctx.sa.v)
                        ctx.sv.inc(v.tensor_mul(hw_sb[:, m * ST + tg * 512: m * ST + (tg + 1) * 512],
                                                htmp[:, bp * 512:(bp + 1) * 512], pB[bp][:, :]))
                        hAB_done[bp] = ctx.sv.v
                    blk.vector(fvh)

            for tc in range(NCH):
                gch = sti * NCH + tc
                for hh in range(2):
                    bo = (tc * 2 + hh) % 2

                    def fto(t, e=e, bw=bw, tc=tc, hh=hh, bo=bo):
                        t.wait_ge(ctx.sv.h, po_evac[bo])
                        if tc == 0 and hh == 0:
                            t.wait_ge(ctx.sv.h, ctx.sv.v)
                        for k in range(4):
                            ins = t.matmul(po[bo][:, :],
                                           hw_sb[:, k * ST + tc * 128: k * ST + (tc + 1) * 128],
                                           wed[bw][:, k * 1024 + hh * 512: k * 1024 + (hh + 1) * 512],
                                           start=(k == 0), stop=(k == 3))
                        ctx.sp.inc(ins)
                    blk.tensor(fto)

                    def fvo(v, e=e, tc=tc, hh=hh, bo=bo, gch=gch, sti=sti):
                        v.wait_ge(ctx.sp.h, ctx.sp.v)
                        dst = out_acc[:, tc * 1024 + hh * 512: tc * 1024 + (hh + 1) * 512]
                        if e >= 64:
                            selc = c["c_half"]
                        else:
                            selc = comb_all[:, tc * 64 + e: tc * 64 + e + 1]
                        if e == 0:
                            ctx.sv.inc(v.tensor_scalar(dst, po[bo][:, :], selc, None, ALU.mult))
                        else:
                            ctx.sv.inc(v.scalar_tensor_tensor(dst, po[bo][:, :], selc, dst,
                                                              ALU.mult, ALU.add))
                        po_evac[bo] = ctx.sv.v
                    blk.vector(fvo)
            comp_done[bw] = ctx.sp.v

        for tc in range(NCH):
            gch = sti * NCH + tc
            b, trow = divmod(gch * 128, T_SEQ)

            def fs(s, b=b, trow=trow):
                s.wait_ge(ctx.sv.h, ctx.sv.v)
                s.wait_ge(ctx.sa.h, ctx.sa.v)
                ctx.sd.inc(s.dma_start(out=x_tile[:, :], in_=x_in[b, trow:trow + 128, :]))
            blk.sync(fs)

            def fv7(v, tc=tc):
                v.wait_ge(ctx.sd.h, ctx.sd.v)
                ctx.sv.inc(v.tensor_add(out_acc[:, tc * 1024:(tc + 1) * 1024],
                                        out_acc[:, tc * 1024:(tc + 1) * 1024], x_tile[:, :]))
            blk.vector(fv7)

            def fg(g, b=b, trow=trow, tc=tc):
                g.wait_ge(ctx.sv.h, ctx.sv.v)
                ctx.sg.inc(g.dma_start(out=x_out[b, trow:trow + 128, :],
                                       in_=out_acc[:, tc * 1024:(tc + 1) * 1024]))
            blk.gpsimd(fg)
        ctx.barrier()

    ctx.free("g_xt", "g_cent", "g_lncol", "g_xn16t", "g_xn32t", "g_xnT32", "g_xnT",
             "g_oacc", "g_comb", "g_sc", "g_ws", "g_gs", "g_h", "g_htmp",
             "g_weg0", "g_weg1", "g_weu0", "g_weu1", "g_wed0", "g_wed1",
             "g_p16", "g_pT", "g_pA0", "g_pA1", "g_pB0", "g_pB1", "g_po0", "g_po1")


def _decl(nc, name, shape, kind="ExternalInput"):
    return nc.dram_tensor(name, list(shape), F32, kind=kind).ap()


def _build_full():
    nc = bass.Bass("TRN2", target_bir_lowering=False)
    g = {}
    g["x_in"] = _decl(nc, "x_in", [B_PER_CORE, T_SEQ, D])
    for nm, sh in [("eg_w", (64, 512, 1024)), ("eu_w", (64, 512, 1024)),
                   ("ed_w", (64, 1024, 512)), ("sg_w", (1024, 1024)),
                   ("su_w", (1024, 1024)), ("sd_w", (1024, 1024)),
                   ("gate_w", (64, 1024)), ("gate_bias", (64,)),
                   ("W_dt", (1024, 1024)), ("b_dt", (1024,)), ("W_B", (16, 1024)),
                   ("W_C", (16, 1024)), ("A_log", (1024, 16)), ("D_in", (1024,)),
                   ("W1", (8, 1024, 4096)), ("b1", (8, 4096)), ("W2", (8, 4096, 1024)),
                   ("b2", (8, 1024)), ("Wg", (1024, 8)), ("rank_mask", (8, 1)),
                   ("Wq", (1024, 1024)), ("Wkv", (1024, 128)), ("Wk", (128, 512)),
                   ("Wv", (128, 512)), ("Wo", (1024, 1024)), ("q_g", (64,)), ("k_g", (32,))]:
        g[nm] = _decl(nc, nm, sh)
    y = _decl(nc, "y", [B_PER_CORE, T_SEQ, D], kind="ExternalOutput")
    xb1 = _decl(nc, "xb1", [B_PER_CORE, T_SEQ, D], kind="Internal")
    xb2 = _decl(nc, "xb2", [B_PER_CORE, T_SEQ, D], kind="Internal")
    x3 = _decl(nc, "x3", [B_PER_CORE, T_SEQ, D], kind="Internal")
    xb4 = _decl(nc, "xb4", [B_PER_CORE, T_SEQ, D], kind="Internal")
    x5 = _decl(nc, "x5", [B_PER_CORE, T_SEQ, D], kind="Internal")
    cc4_in = _decl(nc, "cc4_in", [1, 128], kind="Internal")
    cc4_all = _decl(nc, "cc4_all", [8, 128], kind="Internal")
    cc6_in = _decl(nc, "cc6_in", [1, 128], kind="Internal")
    cc6_all = _decl(nc, "cc6_all", [8, 128], kind="Internal")
    with nc.Block() as blk:
        ctx = Ctx(nc, blk)
        c = make_consts(ctx)
        st0 = emit_ffn0_stage(ctx, c, g["eg_w"], g["eu_w"], g["ed_w"], g["sg_w"],
                              g["su_w"], g["sd_w"], g["gate_w"], g["gate_bias"])
        emit_ffn0(ctx, c, st0, g["x_in"], xb1, "s1")
        emit_mamba(ctx, c, xb1, xb2, g["W_dt"], g["b_dt"], g["W_B"], g["W_C"],
                   g["A_log"], g["D_in"])
        emit_ffn0(ctx, c, st0, xb2, x3, "s3")
        ctx.free("f0_gb", "f0_gwT", "sel16")
        st1 = emit_ffn1_stage(ctx, c, g["W1"], g["W2"], g["Wg"])
        emit_ffn1_small(ctx, c, st1, g["b1"], g["b2"])
        emit_ffn1(ctx, c, st1, x3, xb4, g["rank_mask"], "s4", cc4_in, cc4_all)
        ctx.free("f1_b1c", "f1_b2r")
        emit_attn(ctx, c, xb4, x5, g["Wq"], g["Wkv"], g["Wk"], g["Wv"], g["Wo"],
                  g["q_g"], g["k_g"])
        emit_ffn1_small(ctx, c, st1, g["b1"], g["b2"])
        emit_ffn1(ctx, c, st1, x5, y, g["rank_mask"], "s6", cc6_in, cc6_all)
        finish(ctx)
    return nc


_NC_CACHE = {}


def kernel(**inputs):
    shards = np.split(np.ascontiguousarray(inputs["L0_attn_x"]), NCORES, axis=0)
    rms = []
    for cix in range(NCORES):
        rm = np.zeros((8, 1), np.float32)
        rm[:cix] = 1.0
        rms.append(rm)

    if "F" not in _NC_CACHE:
        _NC_CACHE["F"] = _build_full()
    maps = [dict(x_in=shards[cix], eg_w=inputs["L0_ffn_eg_w"], eu_w=inputs["L0_ffn_eu_w"],
                 ed_w=inputs["L0_ffn_ed_w"], sg_w=inputs["L0_ffn_sg_w"],
                 su_w=inputs["L0_ffn_su_w"], sd_w=inputs["L0_ffn_sd_w"],
                 gate_w=inputs["L0_ffn_gate_w"], gate_bias=inputs["L0_ffn_gate_bias"],
                 W_dt=inputs["L0_attn_W_dt"], b_dt=inputs["L0_attn_b_dt"],
                 W_B=inputs["L0_attn_W_B"], W_C=inputs["L0_attn_W_C"],
                 A_log=inputs["L0_attn_A_log"], D_in=inputs["L0_attn_D"],
                 W1=inputs["L1_ffn_W1"], b1=inputs["L1_ffn_b1"],
                 W2=inputs["L1_ffn_W2"], b2=inputs["L1_ffn_b2"],
                 Wg=inputs["L1_ffn_Wg"], rank_mask=rms[cix],
                 Wq=inputs["L1_attn_Wq"], Wkv=inputs["L1_attn_Wkv"],
                 Wk=inputs["L1_attn_Wk"], Wv=inputs["L1_attn_Wv"],
                 Wo=inputs["L1_attn_Wo"], q_g=inputs["L1_attn_q_g"],
                 k_g=inputs["L1_attn_k_g"]) for cix in range(NCORES)]
    res = run_bass_kernel_spmd(_NC_CACHE["F"], maps, core_ids=list(range(NCORES)))
    return np.concatenate([r["y"] for r in res.results], axis=0)


# version 08
# speedup vs baseline: 1.0727x; 1.0727x over previous
import math
import numpy as np
import ml_dtypes
import concourse.bass as bass
import concourse.mybir as mybir
from concourse.bass_utils import run_bass_kernel_spmd

F32 = mybir.dt.float32
BF16 = mybir.dt.bfloat16
AF = mybir.ActivationFunctionType
ALU = mybir.AluOpType
AX = mybir.AxisListType

NCORES = 8
B_PER_CORE = 2
T_SEQ = 2048
D = 1024
TOK = B_PER_CORE * T_SEQ


class Sem:

    def __init__(self, nc, name, step=1):
        self.h = nc.alloc_semaphore(name)
        self.v = 0
        self.step = step

    def inc(self, instr, k=None):
        k = self.step if k is None else k
        instr.then_inc(self.h, k)
        self.v += k
        return self.v


class Ctx:

    def __init__(self, nc, blk):
        self.nc = nc
        self.blk = blk
        self.sd = Sem(nc, "s_dma_sync", 16)
        self.sg = Sem(nc, "s_dma_gp", 16)
        self.sq = Sem(nc, "s_dma_sc", 16)
        self.sp = Sem(nc, "s_pe")
        self.sv = Sem(nc, "s_dve")
        self.sa = Sem(nc, "s_act")
        self.sm = Sem(nc, "s_pool")
        self._bufs = []
        self._serial = {}

    def alloc(self, name, shape, dtype, psum=False):
        n = self._serial.get(name, 0)
        self._serial[name] = n + 1
        uname = name if n == 0 else f"{name}__{n}"
        guard = (self.nc.psum_tensor if psum else self.nc.sbuf_tensor)(uname, shape, dtype)
        t = guard.__enter__()
        self._bufs.append((name, guard))
        return t

    def free(self, *names):
        keep = []
        for name, g in reversed(self._bufs):
            if name in names:
                g.__exit__(None, None, None)
            else:
                keep.append((name, g))
        self._bufs = keep[::-1]

    def barrier(self):
        blk = self.blk
        targets = [(s, s.v) for s in (self.sd, self.sg, self.sq, self.sp,
                                      self.sv, self.sa, self.sm)]
        for sec in (blk.sync, blk.tensor, blk.vector, blk.scalar, blk.gpsimd):
            def f(eng):
                for sem, val in targets:
                    eng.wait_ge(sem.h, val)
            sec(f)


def make_consts(ctx):
    blk = ctx.blk
    c = {}
    c["id16"] = ctx.alloc("id16", [128, 128], BF16)
    c["id32"] = ctx.alloc("id32", [128, 128], F32)
    c["ones32"] = ctx.alloc("ones32", [1, 128], F32)
    c["ones16"] = ctx.alloc("ones16", [1, 128], BF16)
    cc = ctx.alloc("cc", [128, 10], F32)
    c["sel16"] = ctx.alloc("sel16", [16, 16 * 128], F32)
    c["cc"] = cc
    c["c_one"] = cc[:, 0:1]
    c["c_neg1"] = cc[:, 1:2]
    c["c_inv1024"] = cc[:, 2:3]
    c["c_eps_ln"] = cc[:, 3:4]
    c["c_eps_rms"] = cc[:, 4:5]
    c["c_inv64"] = cc[:, 5:6]
    c["c_inv32"] = cc[:, 6:7]
    c["c_eps8"] = cc[:, 7:8]
    c["c_half"] = cc[:, 8:9]

    def f(g):
        ctx.sm.inc(g.memset(c["id16"][:, :], 0.0))
        ctx.sm.inc(g.memset(c["id32"][:, :], 0.0))
        ctx.sm.inc(g.memset(c["sel16"][:, :], 0.0))
        ctx.sm.inc(g.memset(c["ones32"][:, :], 1.0))
        ctx.sm.inc(g.memset(c["ones16"][:, :], 1.0))
        for i, v in enumerate([1.0, -1.0, 1.0 / 1024, 1e-5, 1e-6, 1.0 / 64, 1.0 / 32, 1e-8, 0.5, 0.0]):
            ctx.sm.inc(g.memset(cc[:, i:i + 1], v))
        g.wait_ge(ctx.sm.h, ctx.sm.v)
        for t in (c["id16"], c["id32"]):
            ctx.sm.inc(g.affine_select(out=t[:, :], in_=t[:, :], compare_op=ALU.not_equal,
                            fill=1.0, base=0, pattern=[[-1, 128]], channel_multiplier=1))
        sel3 = c["sel16"].ap().rearrange("p (n j) -> p n j", n=16)
        ctx.sm.inc(g.affine_select(out=sel3, in_=sel3, compare_op=ALU.not_equal,
                        fill=1.0, base=0, pattern=[[-1, 16], [0, 128]],
                        channel_multiplier=1))
    blk.gpsimd(f)
    ctx.barrier()
    return c


def emit_ln(ctx, c, x_tile, scratch, xn16_tile):
    blk = ctx.blk
    cent, sq, m_col, v_col, r_col = scratch

    def fv(v):
        ctx.sv.inc(v.reduce_sum(m_col, x_tile[:, :], axis=AX.X))
        v.wait_ge(ctx.sv.h, ctx.sv.v)
        ctx.sv.inc(v.tensor_scalar_mul(m_col, m_col, 1.0 / 1024))
        v.wait_ge(ctx.sv.h, ctx.sv.v)
        ctx.sv.inc(v.tensor_scalar(cent[:, :], x_tile[:, :], m_col, None, ALU.subtract))
    blk.vector(fv)

    def fa(s):
        s.wait_ge(ctx.sv.h, ctx.sv.v)
        ctx.sa.inc(s.activation(sq[:, :], cent[:, :], AF.Square, accum_out=v_col))
        s.wait_ge(ctx.sa.h, ctx.sa.v)
        ctx.sa.inc(s.activation(v_col, v_col, AF.Sqrt, bias=c["c_eps_ln"], scale=c["c_inv1024"]))
    blk.scalar(fa)

    def fv2(v):
        v.wait_ge(ctx.sa.h, ctx.sa.v)
        ctx.sv.inc(v.reciprocal(r_col, v_col))
        v.wait_ge(ctx.sv.h, ctx.sv.v)
        ctx.sv.inc(v.tensor_scalar_mul(xn16_tile[:, :], cent[:, :], r_col))
    blk.vector(fv2)


def emit_mamba(ctx, c, x_in, x_out, W_dt, b_dt, W_B, W_C, A_log, D_in, dbg=None):
    blk = ctx.blk

    def dump(name, src_ap):
        if dbg is None or name not in dbg:
            return
        dst = dbg[name]

        def fg(g):
            g.wait_ge(ctx.sv.h, ctx.sv.v)
            g.wait_ge(ctx.sa.h, ctx.sa.v)
            g.wait_ge(ctx.sp.h, ctx.sp.v)
            ctx.sg.inc(g.dma_start(out=dst, in_=src_ap))
        blk.gpsimd(fg)
        ctx.barrier()
    N_ST = 16
    HALF = 1024

    wdtT = ctx.alloc("wdtT", [128, 8 * 1024], BF16)
    wbT = ctx.alloc("wbT", [128, 8 * 16], BF16)
    wcT = ctx.alloc("wcT", [128, 8 * 16], BF16)
    bdt_col = ctx.alloc("bdt_col", [128, 8], F32)
    d_col = ctx.alloc("d_col", [128, 8], F32)
    a_sb = ctx.alloc("a_sb", [128, 128], F32)
    wload = ctx.alloc("wload", [128, 1024], F32)
    wload16 = ctx.alloc("wload16", [128, 1024], BF16)
    p16 = ctx.alloc("p_tr16", [128, 128], BF16, psum=True)
    p32 = ctx.alloc("p_tr32", [128, 128], F32, psum=True)

    for i in range(8):
        def fs(s, i=i):
            s.wait_ge(ctx.sp.h, ctx.sp.v)
            ctx.sd.inc(s.dma_start(out=wload[:, :], in_=W_dt[i * 128:(i + 1) * 128, :]))
        blk.sync(fs)

        def fa(a):
            a.wait_ge(ctx.sd.h, ctx.sd.v)
            a.wait_ge(ctx.sp.h, ctx.sp.v)
            ctx.sa.inc(a.copy(wload16[:, :], wload[:, :]))
        blk.scalar(fa)
        for k in range(8):
            def ft(t, i=i, k=k):
                t.wait_ge(ctx.sa.h, ctx.sa.v)
                t.wait_ge(ctx.sv.h, ctx.sv.v)
                ctx.sp.inc(t.transpose(p16[:, :], wload16[:, k * 128:(k + 1) * 128], c["id16"][:, :]))
            blk.tensor(ft)

            def fv(v, i=i, k=k):
                v.wait_ge(ctx.sp.h, ctx.sp.v)
                ctx.sv.inc(v.tensor_copy(wdtT[:, k * 1024 + i * 128: k * 1024 + (i + 1) * 128], p16[:, :]))
            blk.vector(fv)

    for (W, dst) in ((W_B, wbT), (W_C, wcT)):
        def fs(s, W=W):
            s.wait_ge(ctx.sp.h, ctx.sp.v)
            ctx.sd.inc(s.dma_start(out=wload[0:16, :], in_=W[:, :]))
        blk.sync(fs)

        def fa(a):
            a.wait_ge(ctx.sd.h, ctx.sd.v)
            a.wait_ge(ctx.sp.h, ctx.sp.v)
            ctx.sa.inc(a.copy(wload16[0:16, :], wload[0:16, :]))
        blk.scalar(fa)
        for k in range(8):
            def ft(t, k=k):
                t.wait_ge(ctx.sa.h, ctx.sa.v)
                t.wait_ge(ctx.sv.h, ctx.sv.v)
                ctx.sp.inc(t.transpose(p16[0:128, 0:16], wload16[0:16, k * 128:(k + 1) * 128], c["id16"][0:16, 0:16]))
            blk.tensor(ft)

            def fv(v, dst=dst, k=k):
                v.wait_ge(ctx.sp.h, ctx.sp.v)
                ctx.sv.inc(v.tensor_copy(dst[:, k * 16:(k + 1) * 16], p16[0:128, 0:16]))
            blk.vector(fv)

    for (W, dst) in ((b_dt, bdt_col), (D_in, d_col)):
        def fs(s, W=W):
            s.wait_ge(ctx.sp.h, ctx.sp.v)
            ctx.sd.inc(s.dma_start(out=wload[0:8, 0:128], in_=W.rearrange("(c p) -> c p", p=128)))
        blk.sync(fs)

        def ft(t):
            t.wait_ge(ctx.sd.h, ctx.sd.v)
            t.wait_ge(ctx.sv.h, ctx.sv.v)
            ctx.sp.inc(t.transpose(p32[0:128, 0:8], wload[0:8, 0:128], c["id32"][0:8, 0:8]))
        blk.tensor(ft)

        def fv(v, dst=dst):
            v.wait_ge(ctx.sp.h, ctx.sp.v)
            ctx.sv.inc(v.tensor_copy(dst[:, :], p32[0:128, 0:8]))
        blk.vector(fv)

    def fs(s):
        s.wait_ge(ctx.sp.h, ctx.sp.v)
        ctx.sd.inc(s.dma_start(out=wload[:, 0:128].rearrange("p (c n) -> p c n", c=8),
                               in_=A_log.rearrange("(c p) n -> p c n", p=128)))
    blk.sync(fs)

    def fa(a):
        a.wait_ge(ctx.sd.h, ctx.sd.v)
        ctx.sa.inc(a.activation(a_sb[:, :], wload[:, 0:128], AF.Exp))
    blk.scalar(fa)

    def fv(v):
        v.wait_ge(ctx.sa.h, ctx.sa.v)
        ctx.sv.inc(v.tensor_scalar_mul(a_sb[:, :], a_sb[:, :], -1.0))
    blk.vector(fv)
    ctx.barrier()

    xnT16 = ctx.alloc("m_xnT16", [128, 8 * HALF], BF16)
    xnT32 = ctx.alloc("m_xnT32", [128, 8 * HALF], F32)
    x_tile = ctx.alloc("m_xt", [128, 1024], F32)
    cent = ctx.alloc("m_cent", [128, 1024], F32)
    lncol = ctx.alloc("m_lncol", [128, 4], F32)
    xn16_t = ctx.alloc("m_xn16t", [128, 1024], BF16)
    dtT = ctx.alloc("m_dtT", [128, HALF], F32)
    u_c = ctx.alloc("m_u", [128, HALF], F32)
    v_sb = ctx.alloc("m_vsb", [128, HALF], F32)
    sp_a = ctx.alloc("m_spa", [128, HALF], F32)
    bp2 = [ctx.alloc("m_bp20", [128, HALF], F32), ctx.alloc("m_bp21", [128, HALF], F32)]
    cp2 = [ctx.alloc("m_cp20", [128, HALF], F32), ctx.alloc("m_cp21", [128, HALF], F32)]
    da_sb = ctx.alloc("m_da", [128, HALF], F32)
    dbx_sb = ctx.alloc("m_dbx", [128, HALF], F32)
    h_sb = ctx.alloc("m_h", [128, HALF], F32)
    y_acc = ctx.alloc("m_yacc", [128, HALF], F32)
    hc_sb = ctx.alloc("m_hc", [128, HALF], F32)
    y_tok = ctx.alloc("m_ytok", [128, 8 * 1024], F32)
    bpT = ctx.alloc("m_bpT", [16, HALF], F32)
    cpT = ctx.alloc("m_cpT", [16, HALF], F32)
    state = ctx.alloc("m_state", [128, 128], F32)
    pdt = ctx.alloc("m_pdt", [128, 512], F32, psum=True)
    pbp = ctx.alloc("m_pbp", [16, 512], F32, psum=True)
    bc_scr = ctx.nc.dram_tensor("m_bc_scr", [2, 16, HALF], F32, kind="Internal").ap()

    ln_scratch = (cent, x_tile, lncol[:, 0:1], lncol[:, 1:2], lncol[:, 2:3])

    load_done = [0, 0]
    cons_done = [0, 0]

    def emit_bc_load(j):
        n_, p = j % 16, j % 2

        def fsl(s, n_=n_, p=p, cd=cons_done[p]):
            s.wait_ge(ctx.sv.h, cd)
            ctx.sd.inc(s.dma_start(out=bp2[p][:, :],
                                   in_=bc_scr[0, n_:n_ + 1, :].broadcast_to([128, HALF])))
            ctx.sd.inc(s.dma_start(out=cp2[p][:, :],
                                   in_=bc_scr[1, n_:n_ + 1, :].broadcast_to([128, HALF])))
            load_done[p] = ctx.sd.v
        blk.sync(fsl)

    for b in range(B_PER_CORE):
        for half in range(2):
            t0 = half * HALF
            for ti in range(8):
                def fs(s, b=b, t0=t0, ti=ti):
                    s.wait_ge(ctx.sa.h, ctx.sa.v)
                    s.wait_ge(ctx.sv.h, ctx.sv.v)
                    ctx.sd.inc(s.dma_start(out=x_tile[:, :],
                                           in_=x_in[b, t0 + ti * 128: t0 + (ti + 1) * 128, :]))
                blk.sync(fs)

                def fv0(v):
                    v.wait_ge(ctx.sd.h, ctx.sd.v)
                blk.vector(fv0)
                emit_ln(ctx, c, x_tile, ln_scratch, xn16_t)

                def fv1(v):
                    ctx.sv.inc(v.tensor_scalar_mul(cent[:, :], cent[:, :], lncol[:, 2:3]))
                blk.vector(fv1)
                for k in range(8):
                    def ft(t, k=k):
                        t.wait_ge(ctx.sv.h, ctx.sv.v)
                        t.wait_ge(ctx.sa.h, ctx.sa.v)
                        ctx.sp.inc(t.transpose(p16[:, :], xn16_t[:, k * 128:(k + 1) * 128], c["id16"][:, :]))
                        ctx.sp.inc(t.transpose(p32[:, :], cent[:, k * 128:(k + 1) * 128], c["id32"][:, :]))
                    blk.tensor(ft)

                    def fa2(a, k=k, ti=ti):
                        a.wait_ge(ctx.sp.h, ctx.sp.v)
                        a.copy(xnT16[:, k * HALF + ti * 128: k * HALF + (ti + 1) * 128], p16[:, :])
                        ctx.sa.inc(a.copy(xnT32[:, k * HALF + ti * 128: k * HALF + (ti + 1) * 128], p32[:, :]))
                    blk.scalar(fa2)

            if b == 0 and half == 0:
                dump("xnT16", xnT16[:, :])
                dump("xnT32", xnT32[:, :])
            for (wT, dstT) in ((wbT, bpT), (wcT, cpT)):
                for tg in range(HALF // 512):
                    def ft(t, wT=wT, tg=tg):
                        t.wait_ge(ctx.sa.h, ctx.sa.v)
                        t.wait_ge(ctx.sv.h, ctx.sv.v)
                        for k in range(8):
                            ins = t.matmul(pbp[0:16, :], wT[:, k * 16:(k + 1) * 16],
                                           xnT16[:, k * HALF + tg * 512: k * HALF + (tg + 1) * 512],
                                           start=(k == 0), stop=(k == 7))
                        ctx.sp.inc(ins)
                    blk.tensor(ft)

                    def fv(v, dstT=dstT, tg=tg):
                        v.wait_ge(ctx.sp.h, ctx.sp.v)
                        ctx.sv.inc(v.tensor_copy(dstT[:, tg * 512:(tg + 1) * 512], pbp[0:16, :]))
                    blk.vector(fv)

            if b == 0 and half == 0:
                dump("bpT", bpT[:, :])
                dump("cpT", cpT[:, :])

            def fgw(g):
                g.wait_ge(ctx.sv.h, ctx.sv.v)
                ctx.sg.inc(g.dma_start(out=bc_scr[0], in_=bpT[:, :]))
                ctx.sg.inc(g.dma_start(out=bc_scr[1], in_=cpT[:, :]))
            blk.gpsimd(fgw)
            ctx.barrier()
            emit_bc_load(0)
            for cdx in range(8):
                for tg in range(HALF // 512):
                    def ft(t, cdx=cdx, tg=tg):
                        t.wait_ge(ctx.sv.h, ctx.sv.v)
                        t.wait_ge(ctx.sa.h, ctx.sa.v)
                        for k in range(8):
                            ins = t.matmul(pdt[:, :],
                                           wdtT[:, k * 1024 + cdx * 128: k * 1024 + (cdx + 1) * 128],
                                           xnT16[:, k * HALF + tg * 512: k * HALF + (tg + 1) * 512],
                                           start=(k == 0), stop=(k == 7))
                        ctx.sp.inc(ins)
                    blk.tensor(ft)

                    def fv(v, cdx=cdx, tg=tg):
                        v.wait_ge(ctx.sp.h, ctx.sp.v)
                        sl = slice(tg * 512, (tg + 1) * 512)
                        ctx.sv.inc(v.tensor_scalar(v_sb[:, sl], pdt[:, :],
                                                   bdt_col[:, cdx:cdx + 1], None, ALU.add))
                    blk.vector(fv)
                def fa(a, cdx=cdx):
                    a.wait_ge(ctx.sv.h, ctx.sv.v)
                    a.activation(sp_a[:, :], v_sb[:, :], AF.Abs)
                    a.activation(sp_a[:, :], sp_a[:, :], AF.Exp, scale=c["c_neg1"])
                    ctx.sa.inc(a.activation(sp_a[:, :], sp_a[:, :], AF.Ln, bias=c["c_one"]))
                blk.scalar(fa)

                def fv(v, cdx=cdx):
                    v.wait_ge(ctx.sa.h, ctx.sa.v)
                    v.tensor_scalar_max(dtT[:, :], v_sb[:, :], 0.0)
                    v.tensor_add(dtT[:, :], dtT[:, :], sp_a[:, :])
                    ctx.sv.inc(v.tensor_mul(u_c[:, :], dtT[:, :], xnT32[:, cdx * HALF:(cdx + 1) * HALF]))
                blk.vector(fv)

                if b == 0 and half == 0 and cdx == 0:
                    dump("dtT0", dtT[:, :])
                    dump("u0", u_c[:, :])
                for n in range(N_ST):
                    j = cdx * 16 + n
                    if j + 1 < 8 * N_ST:
                        emit_bc_load(j + 1)

                    def fv1(v, n=n, cdx=cdx):
                        v.wait_ge(ctx.sa.h, ctx.sa.v)
                        ctx.sv.inc(v.tensor_scalar_mul(da_sb[:, :], dtT[:, :],
                                                       a_sb[:, cdx * 16 + n: cdx * 16 + n + 1]))
                    blk.vector(fv1)

                    def fa4(a):
                        a.wait_ge(ctx.sv.h, ctx.sv.v)
                        ctx.sa.inc(a.activation(da_sb[:, :], da_sb[:, :], AF.Exp))
                    blk.scalar(fa4)

                    def fv2(v, half=half, n=n, cdx=cdx, j=j, ld=load_done[j % 2]):
                        v.wait_ge(ctx.sa.h, ctx.sa.v)
                        v.wait_ge(ctx.sd.h, ld)
                        v.tensor_mul(dbx_sb[:, :], u_c[:, :], bp2[j % 2][:, :])
                        col = cdx * 16 + n
                        init = 0.0 if half == 0 else state[:, col:col + 1]
                        ctx.sv.inc(v.tensor_tensor_scan(h_sb[:, :], da_sb[:, :], dbx_sb[:, :],
                                             init, ALU.mult, ALU.add))
                        v.wait_ge(ctx.sv.h, ctx.sv.v)
                        v.tensor_mul(hc_sb[:, :], h_sb[:, :], cp2[j % 2][:, :])
                        if n == 0:
                            v.tensor_copy(y_acc[:, :], hc_sb[:, :])
                        else:
                            v.tensor_add(y_acc[:, :], y_acc[:, :], hc_sb[:, :])
                        ctx.sv.inc(v.tensor_copy(state[:, col:col + 1], h_sb[:, HALF - 1:HALF]))
                        cons_done[j % 2] = ctx.sv.v
                    blk.vector(fv2)

                if b == 0 and half == 0 and cdx == 0:
                    dump("bp_bc0", bp2[0][:, :])
                    dump("da0", da_sb[:, :])
                    dump("h0", h_sb[:, :])
                    dump("yacc0", y_acc[:, :])
                def fv3(v, cdx=cdx):
                    ctx.sv.inc(v.scalar_tensor_tensor(y_acc[:, :], xnT32[:, cdx * HALF:(cdx + 1) * HALF],
                                                      d_col[:, cdx:cdx + 1], y_acc[:, :], ALU.mult, ALU.add))
                blk.vector(fv3)

                for ti in range(8):
                    def ft(t, ti=ti):
                        t.wait_ge(ctx.sv.h, ctx.sv.v)
                        t.wait_ge(ctx.sa.h, ctx.sa.v)
                        ctx.sp.inc(t.transpose(p32[:, :], y_acc[:, ti * 128:(ti + 1) * 128], c["id32"][:, :]))
                    blk.tensor(ft)

                    def fa5(a, ti=ti, cdx=cdx):
                        a.wait_ge(ctx.sp.h, ctx.sp.v)
                        ctx.sa.inc(a.copy(y_tok[:, ti * 1024 + cdx * 128: ti * 1024 + (cdx + 1) * 128],
                                          p32[:, :]))
                    blk.scalar(fa5)

            for ti in range(8):
                def fs(s, b=b, t0=t0, ti=ti):
                    s.wait_ge(ctx.sv.h, ctx.sv.v)
                    s.wait_ge(ctx.sa.h, ctx.sa.v)
                    ctx.sd.inc(s.dma_start(out=x_tile[:, :],
                                           in_=x_in[b, t0 + ti * 128: t0 + (ti + 1) * 128, :]))
                blk.sync(fs)

                def fv4(v, ti=ti):
                    v.wait_ge(ctx.sd.h, ctx.sd.v)
                    v.wait_ge(ctx.sa.h, ctx.sa.v)
                    ctx.sv.inc(v.tensor_add(y_tok[:, ti * 1024:(ti + 1) * 1024],
                                            y_tok[:, ti * 1024:(ti + 1) * 1024], x_tile[:, :]))
                blk.vector(fv4)

                def fg(g, b=b, t0=t0, ti=ti):
                    g.wait_ge(ctx.sv.h, ctx.sv.v)
                    ctx.sg.inc(g.dma_start(out=x_out[b, t0 + ti * 128: t0 + (ti + 1) * 128, :],
                                           in_=y_tok[:, ti * 1024:(ti + 1) * 1024]))
                blk.gpsimd(fg)
            ctx.barrier()

    ctx.free("wdtT", "wbT", "wcT", "bdt_col", "d_col", "a_sb", "wload", "wload16",
             "p_tr16", "p_tr32", "m_xnT16", "m_xnT32", "m_xt", "m_cent", "m_lncol",
             "m_xn16t", "m_dtT", "m_u", "m_vsb", "m_spa", "m_bp20", "m_bp21",
             "m_cp20", "m_cp21",
             "m_da", "m_dbx", "m_h", "m_yacc", "m_hc", "m_ytok", "m_bpT", "m_cpT",
             "m_state", "m_pdt", "m_pbp")


def finish(ctx):
    blk = ctx.blk

    def fg(g):
        g.wait_ge(ctx.sg.h, ctx.sg.v)
        g.wait_ge(ctx.sd.h, ctx.sd.v)
        g.wait_ge(ctx.sq.h, ctx.sq.v)
    blk.gpsimd(fg)



def emit_ffn1_stage(ctx, c, W1, W2, Wg):
    nc, blk = ctx.nc, ctx.blk
    st = {}
    st["w1s"] = nc.dram_tensor("w1s", [8, 1024, 4096], BF16, kind="Internal").ap()
    st["w2s"] = nc.dram_tensor("w2s", [8, 4096, 1024], BF16, kind="Internal").ap()
    st["wg"] = ctx.alloc("f1_wg", [128, 8 * 8], F32)
    wld = ctx.alloc("f1_wld", [128, 2 * 4096], F32)
    wst = ctx.alloc("f1_wst", [128, 2 * 4096], BF16)

    tiles = []
    for e in range(8):
        for k in range(8):
            tiles.append((W1[e, k * 128:(k + 1) * 128, :],
                          st["w1s"][e, k * 128:(k + 1) * 128, :]))
    for e in range(8):
        for k in range(0, 32, 4):
            tiles.append((W2[e, k * 128:(k + 4) * 128, :].rearrange("(a p) h -> p a h", p=128),
                          st["w2s"][e, k * 128:(k + 4) * 128, :].rearrange("(a p) h -> p a h", p=128)))
    conv_done = [0, 0]
    store_done = [0, 0]
    load_done = [0, 0]
    for j, (src_ap, dst_ap) in enumerate(tiles):
        b = j % 2
        def fs(s, src_ap=src_ap, b=b):
            s.wait_ge(ctx.sa.h, conv_done[b])
            dst = wld[:, b * 4096:(b + 1) * 4096]
            if len(src_ap.shape) == 3:
                dst = dst.rearrange("p (a h) -> p a h", a=src_ap.shape[1])
            ctx.sd.inc(s.dma_start(out=dst, in_=src_ap))
            load_done[b] = ctx.sd.v
        blk.sync(fs)

        def fa(a, b=b):
            a.wait_ge(ctx.sd.h, load_done[b])
            a.wait_ge(ctx.sg.h, store_done[b])
            ctx.sa.inc(a.copy(wst[:, b * 4096:(b + 1) * 4096], wld[:, b * 4096:(b + 1) * 4096]))
            conv_done[b] = ctx.sa.v
        blk.scalar(fa)

        def fg(g, dst_ap=dst_ap, b=b):
            g.wait_ge(ctx.sa.h, conv_done[b])
            src2 = wst[:, b * 4096:(b + 1) * 4096]
            if len(dst_ap.shape) == 3:
                src2 = src2.rearrange("p (a h) -> p a h", a=dst_ap.shape[1])
            ctx.sg.inc(g.dma_start(out=dst_ap, in_=src2))
            store_done[b] = ctx.sg.v
        blk.gpsimd(fg)
    ctx.barrier()

    def fs(s):
        ctx.sd.inc(s.dma_start(out=wld[0:128, 1024:1024 + 64].rearrange("p (k e) -> p k e", k=8),
                               in_=Wg.rearrange("(k p) e -> p k e", p=128)))
    blk.sync(fs)

    def fa(a):
        a.wait_ge(ctx.sd.h, ctx.sd.v)
        ctx.sa.inc(a.copy(st["wg"][:, :], wld[0:128, 1024:1024 + 64]))
    blk.scalar(fa)
    ctx.barrier()
    ctx.free("f1_wld", "f1_wst")
    return st


def emit_ffn1_small(ctx, c, st, b1, b2):
    blk = ctx.blk
    st["b1c"] = ctx.alloc("f1_b1c", [128, 8 * 32], F32)
    st["b2r"] = ctx.alloc("f1_b2r", [1, 8 * 1024], BF16)
    wld = ctx.alloc("f1_wld2", [128, 8192], F32)
    pt = ctx.alloc("f1_pt2", [128, 128], F32, psum=True)

    def fs(s):
        s.wait_ge(ctx.sa.h, ctx.sa.v)
        s.wait_ge(ctx.sv.h, ctx.sv.v)
        for e in range(8):
            ctx.sd.inc(s.dma_start(out=wld[0:32, e * 128:(e + 1) * 128],
                                   in_=b1[e].rearrange("(a p) -> a p", p=128)))
        ctx.sd.inc(s.dma_start(out=wld[0:1, 4096:8192],
                               in_=b2.rearrange("e h -> (e h)").unsqueeze(0)[:, 0:4096]))
        ctx.sd.inc(s.dma_start(out=wld[32:33, 4096:8192],
                               in_=b2.rearrange("e h -> (e h)").unsqueeze(0)[:, 4096:8192]))
    blk.sync(fs)

    def fa(a):
        a.wait_ge(ctx.sd.h, ctx.sd.v)
        ctx.sa.inc(a.copy(st["b2r"][0:1, 0:4096], wld[0:1, 4096:8192]))
        ctx.sa.inc(a.copy(st["b2r"][0:1, 4096:8192], wld[32:33, 4096:8192]))
    blk.scalar(fa)
    for e in range(8):
        def ft(t, e=e):
            t.wait_ge(ctx.sd.h, ctx.sd.v)
            t.wait_ge(ctx.sv.h, ctx.sv.v)
            ctx.sp.inc(t.transpose(pt[0:128, 0:32], wld[0:32, e * 128:(e + 1) * 128],
                                   c["id32"][0:32, 0:32]))
        blk.tensor(ft)

        def fv(v, e=e):
            v.wait_ge(ctx.sp.h, ctx.sp.v)
            ctx.sv.inc(v.tensor_copy(st["b1c"][:, e * 32:(e + 1) * 32], pt[0:128, 0:32]))
        blk.vector(fv)
    ctx.barrier()
    ctx.free("f1_wld2", "f1_pt2")


def emit_ffn1(ctx, c, st, x_in, x_out, rank_mask, tag, cc_in, cc_all, dbg=None):
    nc, blk = ctx.nc, ctx.blk

    def dump(name, src_ap):
        if dbg is None or name not in dbg:
            return
        dst = dbg[name]

        def fgd(g):
            g.wait_ge(ctx.sv.h, ctx.sv.v)
            g.wait_ge(ctx.sa.h, ctx.sa.v)
            g.wait_ge(ctx.sp.h, ctx.sp.v)
            ctx.sg.inc(g.dma_start(out=dst, in_=src_ap))
        blk.gpsimd(fgd)
        ctx.barrier()
    NTOK = B_PER_CORE * T_SEQ
    NCH = NTOK // 128
    CAP = 4096.0

    x_tile = ctx.alloc("f_xt", [128, 1024], F32)
    cent = ctx.alloc("f_cent", [128, 1024], F32)
    lncol = ctx.alloc("f_lncol", [128, 4], F32)
    xn16_t = ctx.alloc("f_xn16t", [128, 1024], BF16)
    sel_all = ctx.alloc("f_sel", [128, NCH * 8], F32)
    xnT_tmp = ctx.alloc("f_xnTt", [128, 1024], F32)
    xn32_t = ctx.alloc("f_xn32t", [128, 1024], F32)
    gates_sb = ctx.alloc("f_gates", [128, 8], F32)
    gcol = ctx.alloc("f_gcol", [128, 4], F32)
    mask1_all = ctx.alloc("f_mask1", [128, NCH * 8], F32)
    g1_all = ctx.alloc("f_g1", [128, NCH], F32)
    m1T = ctx.alloc("f_m1T", [8, NTOK], F32)
    cumT = ctx.alloc("f_cumT", [8, NTOK], F32)
    ones_w = ctx.alloc("f_onesw", [8, NTOK], F32)
    ones8 = ctx.alloc("f_ones8", [8, 1], F32)
    cnt_sb = ctx.alloc("f_cnt", [8, 8], F32)
    rkm_sb = ctx.alloc("f_rkm", [8, 1], F32)
    pfx_sb = ctx.alloc("f_pfx", [8, 2], F32)
    crow = ctx.alloc("f_crow", [1, 128], F32)
    keep_row = ctx.alloc("f_keepr", [1, NTOK], F32)
    keep_cp = ctx.alloc("f_keepcp", [32, 128], F32)
    keepcol = ctx.alloc("f_keepc", [128, 32], F32)
    p16 = ctx.alloc("f_p16", [128, 128], BF16, psum=True)
    pg = ctx.alloc("f_pg", [128, 8], F32, psum=True)
    pT = ctx.alloc("f_pT", [128, 128], F32, psum=True)
    pk = ctx.alloc("f_pk", [1, 512], F32, psum=True)
    keep_scr = nc.dram_tensor(f"keep_scr_{tag}", [1, NTOK], F32, kind="Internal").ap()

    ln_scratch = (cent, x_tile, lncol[:, 0:1], lncol[:, 1:2], lncol[:, 2:3])

    def fg0(g):
        ctx.sm.inc(g.memset(ones_w[:, :], 1.0))
        ctx.sm.inc(g.memset(ones8[:, :], 1.0))
        ctx.sm.inc(g.memset(crow[:, :], 0.0))
    blk.gpsimd(fg0)

    def fs0(s):
        ctx.sd.inc(s.dma_start(out=rkm_sb[:, 0:1], in_=rank_mask[:, :]))
    blk.sync(fs0)

    for ch in range(NCH):
        b, trow = divmod(ch * 128, T_SEQ)

        def fs(s, b=b, trow=trow):
            s.wait_ge(ctx.sa.h, ctx.sa.v)
            s.wait_ge(ctx.sv.h, ctx.sv.v)
            ctx.sd.inc(s.dma_start(out=x_tile[:, :], in_=x_in[b, trow:trow + 128, :]))
        blk.sync(fs)

        def fv0(v):
            v.wait_ge(ctx.sd.h, ctx.sd.v)
        blk.vector(fv0)
        emit_ln(ctx, c, x_tile, ln_scratch, xn16_t)
        def fvx(v):
            ctx.sv.inc(v.tensor_scalar_mul(xn32_t[:, :], cent[:, :], lncol[:, 2:3]))
        blk.vector(fvx)
        for k in range(8):
            def ft(t, k=k):
                t.wait_ge(ctx.sv.h, ctx.sv.v)
                t.wait_ge(ctx.sa.h, ctx.sa.v)
                ctx.sp.inc(t.transpose(pT[:, :], xn32_t[:, k * 128:(k + 1) * 128], c["id32"][:, :]))
            blk.tensor(ft)

            def fa(a, k=k):
                a.wait_ge(ctx.sp.h, ctx.sp.v)
                ctx.sa.inc(a.copy(xnT_tmp[:, k * 128:(k + 1) * 128], pT[:, :]))
            blk.scalar(fa)

        def ft2(t):
            t.wait_ge(ctx.sa.h, ctx.sa.v)
            t.wait_ge(ctx.sv.h, ctx.sv.v)
            for k in range(8):
                ins = t.matmul(pg[:, :], xnT_tmp[:, k * 128:(k + 1) * 128],
                               st["wg"][:, k * 8:(k + 1) * 8], start=(k == 0), stop=(k == 7))
            ctx.sp.inc(ins)
        blk.tensor(ft2)

        def fv1(v):
            v.wait_ge(ctx.sp.h, ctx.sp.v)
            ctx.sv.inc(v.reduce_max(gcol[:, 0:1], pg[:, :], axis=AX.X, negate=True))
        blk.vector(fv1)

        def fa2(a):
            a.wait_ge(ctx.sv.h, ctx.sv.v)
            ctx.sa.inc(a.activation(gates_sb[:, :], pg[:, :], AF.Exp,
                                    bias=gcol[:, 0:1], accum_out=gcol[:, 2:3]))
        blk.scalar(fa2)

        def fv2(v, ch=ch):
            v.wait_ge(ctx.sa.h, ctx.sa.v)
            ctx.sv.inc(v.reduce_max(gcol[:, 1:2], gates_sb[:, :], axis=AX.X))
            v.wait_ge(ctx.sv.h, ctx.sv.v)
            v.tensor_scalar(mask1_all[:, ch * 8:(ch + 1) * 8], gates_sb[:, :],
                            gcol[:, 1:2], None, ALU.is_ge)
            ctx.sv.inc(v.reciprocal(gcol[:, 3:4], gcol[:, 2:3]))
            v.wait_ge(ctx.sv.h, ctx.sv.v)
            ctx.sv.inc(v.tensor_tensor(g1_all[:, ch:ch + 1], gcol[:, 1:2], gcol[:, 3:4], ALU.mult))
        blk.vector(fv2)

        def ft3(t, ch=ch):
            t.wait_ge(ctx.sv.h, ctx.sv.v)
            t.wait_ge(ctx.sa.h, ctx.sa.v)
            ctx.sp.inc(t.transpose(pT[0:8, 0:128], mask1_all[:, ch * 8:(ch + 1) * 8],
                                   c["id32"][:, :]))
        blk.tensor(ft3)

        def fa3(a, ch=ch):
            a.wait_ge(ctx.sp.h, ctx.sp.v)
            ctx.sa.inc(a.copy(m1T[:, ch * 128:(ch + 1) * 128], pT[0:8, 0:128]))
        blk.scalar(fa3)

    def fv3(v):
        v.wait_ge(ctx.sa.h, ctx.sa.v)
        v.wait_ge(ctx.sm.h, ctx.sm.v)
        ctx.sv.inc(v.tensor_tensor_scan(cumT[:, :], ones_w[:, :], m1T[:, :], 0.0, ALU.mult, ALU.add))
        v.wait_ge(ctx.sv.h, ctx.sv.v)
        ctx.sv.inc(v.tensor_copy(cnt_sb[:, 0:1], cumT[:, NTOK - 1:NTOK]))
    blk.vector(fv3)

    def ftc(t):
        t.wait_ge(ctx.sv.h, ctx.sv.v)
        t.wait_ge(ctx.sa.h, ctx.sa.v)
        ctx.sp.inc(t.transpose(pT[0:1, 0:8], cnt_sb[:, 0:1], c["id32"][0:8, 0:8]))
    blk.tensor(ftc)

    def fvc(v):
        v.wait_ge(ctx.sp.h, ctx.sp.v)
        v.wait_ge(ctx.sm.h, ctx.sm.v)
        ctx.sv.inc(v.tensor_copy(crow[0:1, 0:8], pT[0:1, 0:8]))
    blk.vector(fvc)

    def fg1(g):
        g.wait_ge(ctx.sv.h, ctx.sv.v)
        ctx.sg.inc(g.dma_start(out=cc_in[:, :], in_=crow[0:1, :]))
        g.wait_ge(ctx.sg.h, ctx.sg.v)
        ctx.sm.inc(g.collective_compute(
            "AllGather", ALU.bypass, replica_groups=[[0, 1, 2, 3, 4, 5, 6, 7]],
            ins=[cc_in.opt()], outs=[cc_all.opt()]))
        g.wait_ge(ctx.sm.h, ctx.sm.v)
        ctx.sg.inc(g.dma_start(out=cnt_sb[:, 0:8], in_=cc_all[:, 0:8]))
    blk.gpsimd(fg1)

    def ft4(t):
        t.wait_ge(ctx.sg.h, ctx.sg.v)
        t.wait_ge(ctx.sd.h, ctx.sd.v)
        t.wait_ge(ctx.sa.h, ctx.sa.v)
        t.wait_ge(ctx.sv.h, ctx.sv.v)
        ctx.sp.inc(t.matmul(pT[0:1, 0:8], rkm_sb[:, 0:1], cnt_sb[:, 0:8], start=True, stop=True))
    blk.tensor(ft4)

    def fv4(v):
        v.wait_ge(ctx.sp.h, ctx.sp.v)
        ctx.sv.inc(v.tensor_copy(cnt_sb[0:1, 0:8], pT[0:1, 0:8]))
    blk.vector(fv4)

    def ft5(t):
        t.wait_ge(ctx.sv.h, ctx.sv.v)
        ctx.sp.inc(t.transpose(pT[0:8, 0:1], cnt_sb[0:1, 0:8], c["id32"][0:1, 0:1]))
    blk.tensor(ft5)

    def fv5(v):
        v.wait_ge(ctx.sp.h, ctx.sp.v)
        ctx.sv.inc(v.tensor_copy(pfx_sb[:, 0:1], pT[0:8, 0:1]))
        v.wait_ge(ctx.sv.h, ctx.sv.v)
        ctx.sv.inc(v.tensor_scalar(cumT[:, :], cumT[:, :], pfx_sb[:, 0:1], None, ALU.add))
        v.wait_ge(ctx.sv.h, ctx.sv.v)
        ctx.sv.inc(v.tensor_scalar(ones_w[:, :], cumT[:, :], CAP, None, ALU.is_le))
        v.wait_ge(ctx.sv.h, ctx.sv.v)
        ctx.sv.inc(v.tensor_mul(m1T[:, :], m1T[:, :], ones_w[:, :]))
    blk.vector(fv5)

    for tg in range(NTOK // 512):
        def ft6(t, tg=tg):
            t.wait_ge(ctx.sv.h, ctx.sv.v)
            t.wait_ge(ctx.sa.h, ctx.sa.v)
            ctx.sp.inc(t.matmul(pk[0:1, :], ones8[:, 0:1], m1T[:, tg * 512:(tg + 1) * 512],
                                start=True, stop=True))
        blk.tensor(ft6)

        def fa4(a, tg=tg):
            a.wait_ge(ctx.sp.h, ctx.sp.v)
            ctx.sa.inc(a.copy(keep_row[:, tg * 512:(tg + 1) * 512], pk[0:1, :]))
        blk.scalar(fa4)

    def fg2(g):
        g.wait_ge(ctx.sa.h, ctx.sa.v)
        ctx.sg.inc(g.dma_start(out=keep_scr[:, :], in_=keep_row[0:1, :]))
        g.wait_ge(ctx.sg.h, ctx.sg.v)
        ctx.sg.inc(g.dma_start(out=keep_cp[:, :], in_=keep_scr.rearrange("o (a p) -> (o a) p", p=128)))
    blk.gpsimd(fg2)

    def ft7(t):
        t.wait_ge(ctx.sg.h, ctx.sg.v)
        t.wait_ge(ctx.sv.h, ctx.sv.v)
        ctx.sp.inc(t.transpose(pT[0:128, 0:32], keep_cp[:, :], c["id32"][0:32, 0:32]))
    blk.tensor(ft7)

    dump("m1T_cap", m1T[:, :])
    dump("cumT", cumT[:, :])
    dump("pfx", pfx_sb[:, 0:1])

    def fv6(v):
        v.wait_ge(ctx.sp.h, ctx.sp.v)
        ctx.sv.inc(v.tensor_copy(keepcol[:, :], pT[0:128, 0:32]))
        v.wait_ge(ctx.sv.h, ctx.sv.v)
        for ch in range(NCH):
            v.tensor_scalar(sel_all[:, ch * 8:(ch + 1) * 8], mask1_all[:, ch * 8:(ch + 1) * 8],
                            g1_all[:, ch:ch + 1], keepcol[:, ch:ch + 1], ALU.mult, ALU.mult)
        ctx.sv.inc(v.tensor_scalar_mul(sel_all[:, :], sel_all[:, :], 0.5))
    blk.vector(fv6)
    dump("sel", sel_all[:, :])
    dump("g1", g1_all[:, :])
    dump("mask1", mask1_all[:, :])
    dump("keepcol", keepcol[:, :])
    ctx.barrier()
    ctx.free("f_xnTt", "f_xn32t", "f_gates", "f_gcol", "f_mask1", "f_g1", "f_m1T", "f_cumT",
             "f_onesw", "f_ones8", "f_cnt", "f_rkm", "f_pfx", "f_crow", "f_keepr", "f_keepcp",
             "f_keepc", "f_p16", "f_pg", "f_pT", "f_pk")

    ST = 1024
    xnT_st = ctx.alloc("f_xnT", [128, 8 * ST], BF16)
    out_acc = ctx.alloc("f_oacc", [128, 8 * 1024], F32)
    h_sb = ctx.alloc("f_h", [128, 8 * ST], BF16)
    w1_sb = ctx.alloc("f_w1", [128, 2 * 8192], BF16)
    w2_sb = ctx.alloc("f_w2", [128, 2 * 8192], BF16)
    p16b = ctx.alloc("f_p16b", [128, 128], BF16, psum=True)
    ph = [ctx.alloc("f_ph0", [128, 512], F32, psum=True),
          ctx.alloc("f_ph1", [128, 512], F32, psum=True)]
    po = [ctx.alloc("f_po0", [128, 512], F32, psum=True),
          ctx.alloc("f_po1", [128, 512], F32, psum=True)]

    for sti in range(NTOK // ST):
        for ti in range(ST // 128):
            gch = sti * 8 + ti
            b, trow = divmod(gch * 128, T_SEQ)

            def fs(s, b=b, trow=trow):
                s.wait_ge(ctx.sa.h, ctx.sa.v)
                s.wait_ge(ctx.sv.h, ctx.sv.v)
                ctx.sd.inc(s.dma_start(out=x_tile[:, :], in_=x_in[b, trow:trow + 128, :]))
            blk.sync(fs)

            def fv0(v):
                v.wait_ge(ctx.sd.h, ctx.sd.v)
            blk.vector(fv0)
            emit_ln(ctx, c, x_tile, ln_scratch, xn16_t)
            for k in range(8):
                def ft(t, k=k):
                    t.wait_ge(ctx.sv.h, ctx.sv.v)
                    t.wait_ge(ctx.sa.h, ctx.sa.v)
                    ctx.sp.inc(t.transpose(p16b[:, :], xn16_t[:, k * 128:(k + 1) * 128], c["id16"][:, :]))
                blk.tensor(ft)

                def fa(a, k=k, ti=ti):
                    a.wait_ge(ctx.sp.h, ctx.sp.v)
                    ctx.sa.inc(a.copy(xnT_st[:, k * ST + ti * 128: k * ST + (ti + 1) * 128], p16b[:, :]))
                blk.scalar(fa)
        ctx.barrier()

        iters = [(e, fq) for e in range(8) for fq in range(4)]
        load_done = [0, 0]
        comp_done = [0, 0]
        ph_evac = [0, 0]
        po_evac = [0, 0]
        for j, (e, fq) in enumerate(iters):
            bw = j % 2

            def fs(s, e=e, fq=fq, bw=bw):
                s.wait_ge(ctx.sp.h, comp_done[bw])
                for k in range(8):
                    ctx.sd.inc(s.dma_start(
                        out=w1_sb[:, bw * 8192 + k * 1024: bw * 8192 + (k + 1) * 1024],
                        in_=st["w1s"][e, k * 128:(k + 1) * 128, fq * 1024:(fq + 1) * 1024]))
                for k in range(8):
                    ctx.sd.inc(s.dma_start(
                        out=w2_sb[:, bw * 8192 + k * 1024: bw * 8192 + (k + 1) * 1024],
                        in_=st["w2s"][e, fq * 1024 + k * 128: fq * 1024 + (k + 1) * 128, :]))
                load_done[bw] = ctx.sd.v
            blk.sync(fs)

            for fc in range(8):
                bh = fc % 2
                for tg in range(ST // 512):
                    def ft(t, fc=fc, tg=tg, bw=bw, bh=bh):
                        t.wait_ge(ctx.sd.h, load_done[bw])
                        t.wait_ge(ctx.sa.h, ph_evac[bh])
                        if fc == 0 and tg == 0:
                            t.wait_ge(ctx.sa.h, ctx.sa.v)
                        for k in range(8):
                            ins = t.matmul(ph[bh][:, :],
                                           w1_sb[:, bw * 8192 + k * 1024 + fc * 128: bw * 8192 + k * 1024 + (fc + 1) * 128],
                                           xnT_st[:, k * ST + tg * 512: k * ST + (tg + 1) * 512],
                                           start=(k == 0), stop=(k == 7))
                        ctx.sp.inc(ins)
                    blk.tensor(ft)

                    def fa(a, e=e, fq=fq, fc=fc, tg=tg, bh=bh):
                        a.wait_ge(ctx.sp.h, ctx.sp.v)
                        ctx.sa.inc(a.activation(h_sb[:, fc * ST + tg * 512: fc * ST + (tg + 1) * 512],
                                                ph[bh][:, :], AF.Relu,
                                                bias=st["b1c"][:, e * 32 + fq * 8 + fc: e * 32 + fq * 8 + fc + 1]))
                        ph_evac[bh] = ctx.sa.v
                    blk.scalar(fa)

            for tc in range(ST // 128):
                gch = sti * 8 + tc
                for hh in range(2):
                    bo = (tc * 2 + hh) % 2

                    def ft(t, e=e, fq=fq, tc=tc, hh=hh, bo=bo, bw=bw):
                        t.wait_ge(ctx.sa.h, ctx.sa.v)
                        t.wait_ge(ctx.sv.h, po_evac[bo])
                        for k in range(8):
                            ins = t.matmul(po[bo][:, :],
                                           h_sb[:, k * ST + tc * 128: k * ST + (tc + 1) * 128],
                                           w2_sb[:, bw * 8192 + k * 1024 + hh * 512: bw * 8192 + k * 1024 + (hh + 1) * 512],
                                           start=(k == 0), stop=(k == 7 and fq != 0))
                        if fq == 0:
                            ins = t.matmul(po[bo][:, :], c["ones16"][0:1, :],
                                           st["b2r"][0:1, e * 1024 + hh * 512: e * 1024 + (hh + 1) * 512],
                                           start=False, stop=True)
                        ctx.sp.inc(ins)
                    blk.tensor(ft)

                    def fv(v, e=e, fq=fq, tc=tc, hh=hh, bo=bo, gch=gch):
                        v.wait_ge(ctx.sp.h, ctx.sp.v)
                        dst = out_acc[:, tc * 1024 + hh * 512: tc * 1024 + (hh + 1) * 512]
                        selc = sel_all[:, gch * 8 + e: gch * 8 + e + 1]
                        if e == 0 and fq == 0:
                            ctx.sv.inc(v.tensor_scalar(dst, po[bo][:, :], selc, None, ALU.mult))
                        else:
                            ctx.sv.inc(v.scalar_tensor_tensor(dst, po[bo][:, :], selc, dst,
                                                              ALU.mult, ALU.add))
                        po_evac[bo] = ctx.sv.v
                    blk.vector(fv)
            comp_done[bw] = ctx.sp.v

        if sti == 0:
            dump("oacc0", out_acc[:, :])
            dump("xnT0", xnT_st[:, :])
        for tc in range(ST // 128):
            gch = sti * 8 + tc
            b, trow = divmod(gch * 128, T_SEQ)

            def fs(s, b=b, trow=trow):
                s.wait_ge(ctx.sv.h, ctx.sv.v)
                s.wait_ge(ctx.sa.h, ctx.sa.v)
                ctx.sd.inc(s.dma_start(out=x_tile[:, :], in_=x_in[b, trow:trow + 128, :]))
            blk.sync(fs)

            def fv7(v, tc=tc):
                v.wait_ge(ctx.sd.h, ctx.sd.v)
                ctx.sv.inc(v.tensor_add(out_acc[:, tc * 1024:(tc + 1) * 1024],
                                        out_acc[:, tc * 1024:(tc + 1) * 1024], x_tile[:, :]))
            blk.vector(fv7)

            def fg(g, b=b, trow=trow, tc=tc):
                g.wait_ge(ctx.sv.h, ctx.sv.v)
                ctx.sg.inc(g.dma_start(out=x_out[b, trow:trow + 128, :],
                                       in_=out_acc[:, tc * 1024:(tc + 1) * 1024]))
            blk.gpsimd(fg)
        ctx.barrier()

    ctx.free("f_xt", "f_cent", "f_lncol", "f_xn16t", "f_sel",
             "f_xnT", "f_oacc", "f_h", "f_w1", "f_w2", "f_p16b",
             "f_ph0", "f_ph1", "f_po0", "f_po1")



def emit_attn(ctx, c, x_in, x_out, Wq, Wkv, Wk, Wv, Wo, q_g, k_g):
    nc, blk = ctx.nc, ctx.blk
    NH, HD, RP, NG = 16, 64, 32, 4

    wkv_sb = ctx.alloc("a_wkv", [128, 8 * 128], BF16)
    qg_eff = ctx.alloc("a_qg", [128, 32], F32)
    kg_bc = ctx.alloc("a_kg", [128, 32], F32)
    mask_var = ctx.alloc("a_mask", [128, 4 * 512], F32)
    grow = ctx.alloc("a_grow", [1, 64], F32)
    wldq = ctx.alloc("a_wld", [128, 1024], F32)
    p32a = ctx.alloc("a_p32", [128, 512], F32, psum=True)

    for k in range(8):
        def fs(s, k=k):
            s.wait_ge(ctx.sa.h, ctx.sa.v)
            ctx.sd.inc(s.dma_start(out=wldq[:, 0:128], in_=Wkv[k * 128:(k + 1) * 128, :]))
        blk.sync(fs)

        def fa(a, k=k):
            a.wait_ge(ctx.sd.h, ctx.sd.v)
            ctx.sa.inc(a.copy(wkv_sb[:, k * 128:(k + 1) * 128], wldq[:, 0:128]))
        blk.scalar(fa)

    def fs3(s):
        s.wait_ge(ctx.sa.h, ctx.sa.v)
        ctx.sd.inc(s.dma_start(out=grow[0:1, 0:32], in_=q_g.unsqueeze(0)[:, 0:32]))
        ctx.sd.inc(s.dma_start(out=grow[0:1, 32:64], in_=k_g.unsqueeze(0)))
    blk.sync(fs3)

    def ft(t):
        t.wait_ge(ctx.sd.h, ctx.sd.v)
        ctx.sp.inc(t.matmul(p32a[:, 0:32], c["ones32"][0:1, :], grow[0:1, 0:32],
                            start=True, stop=True))
        ctx.sp.inc(t.matmul(p32a[:, 32:64], c["ones32"][0:1, :], grow[0:1, 32:64],
                            start=True, stop=True))
    blk.tensor(ft)

    def fv(v):
        v.wait_ge(ctx.sp.h, ctx.sp.v)
        ctx.sv.inc(v.tensor_scalar_mul(qg_eff[:, :], p32a[:, 0:32], 0.125))
        ctx.sv.inc(v.tensor_copy(kg_bc[:, :], p32a[:, 32:64]))
    blk.vector(fv)

    def fg(g):
        for r in range(4):
            ctx.sm.inc(g.memset(mask_var[:, r * 512:(r + 1) * 512], 0.0))
        g.wait_ge(ctx.sm.h, ctx.sm.v)
        for r in range(4):
            ctx.sm.inc(g.affine_select(
                out=mask_var[:, r * 512:(r + 1) * 512],
                in_=mask_var[:, r * 512:(r + 1) * 512],
                compare_op=ALU.is_ge, fill=-1e30, base=r * 128,
                pattern=[[-1, 512]], channel_multiplier=1))
    blk.gpsimd(fg)
    ctx.barrier()
    ctx.free("a_p32")

    xnT = ctx.alloc("a_xnT", [128, 8 * 2048], BF16)
    out_acc = ctx.alloc("a_oacc", [128, 16 * 1024], F32)
    x_tile = ctx.alloc("a_xt", [128, 1024], F32)
    cent = ctx.alloc("a_cent", [128, 1024], F32)
    lncol = ctx.alloc("a_lncol", [128, 4], F32)
    xn16_t = ctx.alloc("a_xn16t", [128, 1024], BF16)
    wq_g = ctx.alloc("a_wqg", [128, 8 * 256], BF16)
    wk_g = ctx.alloc("a_wkg", [128, 128], BF16)
    wv_g = ctx.alloc("a_wvg", [128, 128], BF16)
    wo_g = ctx.alloc("a_wog", [32, 4 * 1024], BF16)
    kvT = ctx.alloc("a_kvT", [128, 2048], BF16)
    q_sb = ctx.alloc("a_q", [128, 256], F32)
    qsc_sb = ctx.alloc("a_qsc", [128, 128], BF16)
    rcol = ctx.alloc("a_rcol", [128, 4], F32)
    qT = ctx.alloc("a_qT", [32, 4 * 2048], BF16)
    kT = ctx.alloc("a_kT", [32, 4 * 2048], BF16)
    v_g = ctx.alloc("a_vg", [128, 16 * 128], BF16)
    scores = ctx.alloc("a_scores", [128, 2048], F32)
    probs = ctx.alloc("a_probs", [128, 2048], BF16)
    smcol = ctx.alloc("a_smcol", [128, 2], F32)
    pbT_buf = ctx.alloc("a_pbT", [128, 2048], BF16)
    oc_sb = ctx.alloc("a_oc", [32, 128], BF16)
    p16 = ctx.alloc("a_p16", [128, 128], BF16, psum=True)
    ps = [ctx.alloc("a_ps0", [128, 512], F32, psum=True),
          ctx.alloc("a_ps1", [128, 512], F32, psum=True)]
    pp = [ctx.alloc("a_pp0", [128, 128], BF16, psum=True),
          ctx.alloc("a_pp1", [128, 128], BF16, psum=True)]
    pav = ctx.alloc("a_pav", [32, 128], F32, psum=True)
    po = [ctx.alloc("a_po0", [128, 512], F32, psum=True),
          ctx.alloc("a_po1", [128, 512], F32, psum=True)]

    ln_scratch = (cent, x_tile, lncol[:, 0:1], lncol[:, 1:2], lncol[:, 2:3])

    for b in range(B_PER_CORE):
        for ti in range(16):
            def fs(s, b=b, ti=ti):
                s.wait_ge(ctx.sa.h, ctx.sa.v)
                s.wait_ge(ctx.sv.h, ctx.sv.v)
                ctx.sd.inc(s.dma_start(out=x_tile[:, :], in_=x_in[b, ti * 128:(ti + 1) * 128, :]))
            blk.sync(fs)

            def fv0(v):
                v.wait_ge(ctx.sd.h, ctx.sd.v)
            blk.vector(fv0)
            emit_ln(ctx, c, x_tile, ln_scratch, xn16_t)
            for k in range(8):
                def ft(t, k=k):
                    t.wait_ge(ctx.sv.h, ctx.sv.v)
                    t.wait_ge(ctx.sa.h, ctx.sa.v)
                    ctx.sp.inc(t.transpose(p16[:, :], xn16_t[:, k * 128:(k + 1) * 128], c["id16"][:, :]))
                blk.tensor(ft)

                def fve(v, k=k, ti=ti):
                    v.wait_ge(ctx.sp.h, ctx.sp.v)
                    ctx.sv.inc(v.tensor_copy(xnT[:, k * 2048 + ti * 128: k * 2048 + (ti + 1) * 128], p16[:, :]))
                blk.vector(fve)
        for ti in range(16):
            def ft3(t, ti=ti):
                t.wait_ge(ctx.sa.h, ctx.sa.v)
                t.wait_ge(ctx.sv.h, ctx.sv.v)
                for k in range(8):
                    ins = t.matmul(ps[0][:, 0:128],
                                   xnT[:, k * 2048 + ti * 128: k * 2048 + (ti + 1) * 128],
                                   wkv_sb[:, k * 128:(k + 1) * 128],
                                   start=(k == 0), stop=(k == 7))
                ctx.sp.inc(ins)
            blk.tensor(ft3)

            def fv4(v):
                v.wait_ge(ctx.sp.h, ctx.sp.v)
                ctx.sv.inc(v.tensor_copy(xn16_t[:, 0:128], ps[0][:, 0:128]))
            blk.vector(fv4)

            def ft4(t, ti=ti):
                t.wait_ge(ctx.sv.h, ctx.sv.v)
                t.wait_ge(ctx.sa.h, ctx.sa.v)
                ctx.sp.inc(t.transpose(p16[:, :], xn16_t[:, 0:128], c["id16"][:, :]))
            blk.tensor(ft4)

            def fv6k(v, ti=ti):
                v.wait_ge(ctx.sp.h, ctx.sp.v)
                ctx.sv.inc(v.tensor_copy(kvT[:, ti * 128:(ti + 1) * 128], p16[:, :]))
            blk.vector(fv6k)

        for hg in range(NG):
            def fsw(s, hg=hg):
                s.wait_ge(ctx.sa.h, ctx.sa.v)
                s.wait_ge(ctx.sp.h, ctx.sp.v)
                for k in range(8):
                    ctx.sd.inc(s.dma_start(out=wldq[:, k * 128:(k + 1) * 128],
                                           in_=Wq[k * 128:(k + 1) * 128, hg * 256: hg * 256 + 128]))
            blk.sync(fsw)

            def faw0(a):
                a.wait_ge(ctx.sd.h, ctx.sd.v)
                for k in range(8):
                    ins = a.copy(wq_g[:, k * 256: k * 256 + 128], wldq[:, k * 128:(k + 1) * 128])
                ctx.sa.inc(ins)
            blk.scalar(faw0)

            def fsw3(s, hg=hg):
                s.wait_ge(ctx.sa.h, ctx.sa.v)
                for k in range(8):
                    ctx.sd.inc(s.dma_start(out=wldq[:, k * 128:(k + 1) * 128],
                                           in_=Wq[k * 128:(k + 1) * 128, hg * 256 + 128: hg * 256 + 256]))
            blk.sync(fsw3)

            def faw1(a):
                a.wait_ge(ctx.sd.h, ctx.sd.v)
                for k in range(8):
                    ins = a.copy(wq_g[:, k * 256 + 128: k * 256 + 256], wldq[:, k * 128:(k + 1) * 128])
                ctx.sa.inc(ins)
            blk.scalar(faw1)

            def fsw4(s, hg=hg):
                s.wait_ge(ctx.sa.h, ctx.sa.v)
                ctx.sd.inc(s.dma_start(out=wldq[:, 0:128], in_=Wk[:, hg * 128:(hg + 1) * 128]))
                ctx.sd.inc(s.dma_start(out=wldq[:, 128:256], in_=Wv[:, hg * 128:(hg + 1) * 128]))
            blk.sync(fsw4)

            def faw2(a):
                a.wait_ge(ctx.sd.h, ctx.sd.v)
                a.copy(wk_g[:, :], wldq[:, 0:128])
                ctx.sa.inc(a.copy(wv_g[:, :], wldq[:, 128:256]))
            blk.scalar(faw2)
            for hi in range(4):
                def fsw5(s, hg=hg, hi=hi):
                    s.wait_ge(ctx.sa.h, ctx.sa.v)
                    h = hg * 4 + hi
                    ctx.sd.inc(s.dma_start(out=wldq[0:32, 0:1024], in_=Wo[h * 64: h * 64 + 32, :]))
                blk.sync(fsw5)

                def faw3(a, hi=hi):
                    a.wait_ge(ctx.sd.h, ctx.sd.v)
                    ctx.sa.inc(a.copy(wo_g[0:32, hi * 1024:(hi + 1) * 1024], wldq[0:32, 0:1024]))
                blk.scalar(faw3)

            for ti in range(16):
                def ftq(t, ti=ti):
                    t.wait_ge(ctx.sa.h, ctx.sa.v)
                    t.wait_ge(ctx.sv.h, ctx.sv.v)
                    for k in range(8):
                        ins = t.matmul(ps[0][:, 0:256],
                                       xnT[:, k * 2048 + ti * 128: k * 2048 + (ti + 1) * 128],
                                       wq_g[:, k * 256:(k + 1) * 256],
                                       start=(k == 0), stop=(k == 7))
                    ctx.sp.inc(ins)
                blk.tensor(ftq)

                def fvq(v):
                    v.wait_ge(ctx.sp.h, ctx.sp.v)
                    ctx.sv.inc(v.tensor_copy(q_sb[:, :], ps[0][:, 0:256]))
                blk.vector(fvq)

                def fvq2(v):
                    v.tensor_mul(cent[:, 0:256], q_sb[:, :], q_sb[:, :])
                    ctx.sv.inc(v.reduce_sum(rcol[:, 0:4], cent.ap()[:, 0:256].rearrange("p (h d) -> p h d", h=4), axis=AX.X))
                blk.vector(fvq2)

                def faq2(a):
                    a.wait_ge(ctx.sv.h, ctx.sv.v)
                    ctx.sa.inc(a.activation(rcol[:, 0:4], rcol[:, 0:4], AF.Sqrt,
                                            scale=c["c_inv64"], bias=c["c_eps_rms"]))
                blk.scalar(faq2)

                def fvq3(v):
                    v.wait_ge(ctx.sa.h, ctx.sa.v)
                    ctx.sv.inc(v.reciprocal(rcol[:, 0:4], rcol[:, 0:4]))
                    v.wait_ge(ctx.sv.h, ctx.sv.v)
                    for hi in range(4):
                        v.tensor_scalar_mul(cent[:, hi * 32:(hi + 1) * 32],
                                            q_sb[:, hi * 64: hi * 64 + 32], rcol[:, hi:hi + 1])
                    for hi in range(4):
                        ins = v.tensor_mul(qsc_sb[:, hi * 32:(hi + 1) * 32],
                                           cent[:, hi * 32:(hi + 1) * 32], qg_eff[:, :])
                    ctx.sv.inc(ins)
                blk.vector(fvq3)
                for hi in range(4):
                    def ftq2(t, hi=hi, ti=ti):
                        t.wait_ge(ctx.sv.h, ctx.sv.v)
                        t.wait_ge(ctx.sa.h, ctx.sa.v)
                        ctx.sp.inc(t.transpose(p16[0:32, 0:128], qsc_sb[:, hi * 32:(hi + 1) * 32], c["id16"][:, :]))
                    blk.tensor(ftq2)

                    def fvq4(v, hi=hi, ti=ti):
                        v.wait_ge(ctx.sp.h, ctx.sp.v)
                        ctx.sv.inc(v.tensor_copy(qT[0:32, hi * 2048 + ti * 128: hi * 2048 + (ti + 1) * 128], p16[0:32, 0:128]))
                    blk.vector(fvq4)

                def ftk(t, ti=ti):
                    t.wait_ge(ctx.sa.h, ctx.sa.v)
                    t.wait_ge(ctx.sv.h, ctx.sv.v)
                    ctx.sp.inc(t.matmul(ps[1][:, 0:128], kvT[:, ti * 128:(ti + 1) * 128], wk_g[:, :],
                                        start=True, stop=True))
                    ctx.sp.inc(t.matmul(ps[1][:, 128:256], kvT[:, ti * 128:(ti + 1) * 128], wv_g[:, :],
                                        start=True, stop=True))
                blk.tensor(ftk)

                def fvk(v, ti=ti):
                    v.wait_ge(ctx.sp.h, ctx.sp.v)
                    ctx.sv.inc(v.tensor_copy(q_sb[:, 0:128], ps[1][:, 0:128]))
                    ctx.sv.inc(v.tensor_copy(v_g[:, ti * 128:(ti + 1) * 128], ps[1][:, 128:256]))
                blk.vector(fvk)

                def fvk2(v):
                    v.tensor_mul(cent[:, 0:128], q_sb[:, 0:128], q_sb[:, 0:128])
                    ctx.sv.inc(v.reduce_sum(rcol[:, 0:4], cent.ap()[:, 0:128].rearrange("p (h d) -> p h d", h=4), axis=AX.X))
                blk.vector(fvk2)

                def fak2(a):
                    a.wait_ge(ctx.sv.h, ctx.sv.v)
                    ctx.sa.inc(a.activation(rcol[:, 0:4], rcol[:, 0:4], AF.Sqrt,
                                            scale=c["c_inv32"], bias=c["c_eps_rms"]))
                blk.scalar(fak2)

                def fvk3(v):
                    v.wait_ge(ctx.sa.h, ctx.sa.v)
                    ctx.sv.inc(v.reciprocal(rcol[:, 0:4], rcol[:, 0:4]))
                    v.wait_ge(ctx.sv.h, ctx.sv.v)
                    for hi in range(4):
                        v.tensor_scalar_mul(cent[:, hi * 32:(hi + 1) * 32],
                                            q_sb[:, hi * 32:(hi + 1) * 32], rcol[:, hi:hi + 1])
                    for hi in range(4):
                        ins = v.tensor_mul(qsc_sb[:, hi * 32:(hi + 1) * 32],
                                           cent[:, hi * 32:(hi + 1) * 32], kg_bc[:, :])
                    ctx.sv.inc(ins)
                blk.vector(fvk3)
                for hi in range(4):
                    def ftk2(t, hi=hi, ti=ti):
                        t.wait_ge(ctx.sv.h, ctx.sv.v)
                        t.wait_ge(ctx.sa.h, ctx.sa.v)
                        ctx.sp.inc(t.transpose(p16[0:32, 0:128], qsc_sb[:, hi * 32:(hi + 1) * 32], c["id16"][:, :]))
                    blk.tensor(ftk2)

                    def fvk4(v, hi=hi, ti=ti):
                        v.wait_ge(ctx.sp.h, ctx.sp.v)
                        ctx.sv.inc(v.tensor_copy(kT[0:32, hi * 2048 + ti * 128: hi * 2048 + (ti + 1) * 128], p16[0:32, 0:128]))
                    blk.vector(fvk4)

            for tq in range(16):
                nkc = (tq * 128 + 128 + 511) // 512
                dkc = nkc - 1
                for hi in range(4):
                    for kc in range(nkc):
                        bs = kc % 2
                        def ft7(t, tq=tq, hi=hi, kc=kc, bs=bs):
                            t.wait_ge(ctx.sv.h, ctx.sv.v)
                            t.wait_ge(ctx.sa.h, ctx.sa.v)
                            ctx.sp.inc(t.matmul(ps[bs][:, :],
                                                qT[0:32, hi * 2048 + tq * 128: hi * 2048 + (tq + 1) * 128],
                                                kT[0:32, hi * 2048 + kc * 512: hi * 2048 + (kc + 1) * 512],
                                                start=True, stop=True))
                        blk.tensor(ft7)

                        def fv8(v, tq=tq, kc=kc, bs=bs, dkc=dkc):
                            v.wait_ge(ctx.sp.h, ctx.sp.v)
                            if kc == dkc:
                                ctx.sv.inc(v.tensor_add(scores[:, kc * 512:(kc + 1) * 512],
                                                        ps[bs][:, :], mask_var[:, (tq % 4) * 512:(tq % 4 + 1) * 512]))
                            else:
                                ctx.sv.inc(v.tensor_copy(scores[:, kc * 512:(kc + 1) * 512], ps[bs][:, :]))
                        blk.vector(fv8)

                    L = nkc * 512

                    def fv9(v, L=L):
                        ctx.sv.inc(v.reduce_max(smcol[:, 0:1], scores[:, 0:L], axis=AX.X, negate=True))
                    blk.vector(fv9)

                    def fa10(a, L=L):
                        a.wait_ge(ctx.sv.h, ctx.sv.v)
                        ctx.sa.inc(a.activation(probs[:, 0:L], scores[:, 0:L], AF.Exp,
                                                bias=smcol[:, 0:1], accum_out=smcol[:, 1:2]))
                    blk.scalar(fa10)

                    def fv10(v, L=L):
                        v.wait_ge(ctx.sa.h, ctx.sa.v)
                        ctx.sv.inc(v.reciprocal(smcol[:, 1:2], smcol[:, 1:2]))
                        v.wait_ge(ctx.sv.h, ctx.sv.v)
                        ctx.sv.inc(v.tensor_scalar_mul(probs[:, 0:L], probs[:, 0:L], smcol[:, 1:2]))
                    blk.vector(fv10)

                    probs_ready = ctx.sv.v
                    pp_free = [ctx.sv.v, ctx.sv.v]
                    for kt in range(tq + 1):
                        bq = kt % 2

                        def ft8(t, kt=kt, bq=bq, probs_ready=probs_ready,
                                ppf=pp_free[bq]):
                            t.wait_ge(ctx.sv.h, max(probs_ready, ppf))
                            ctx.sp.inc(t.transpose(pp[bq][:, :], probs[:, kt * 128:(kt + 1) * 128], c["id16"][:, :]))
                        blk.tensor(ft8)
                        tr_done = ctx.sp.v

                        def fv14(v, kt=kt, bq=bq, tr_done=tr_done):
                            v.wait_ge(ctx.sp.h, tr_done)
                            ctx.sv.inc(v.tensor_copy(pbT_buf[:, kt * 128:(kt + 1) * 128], pp[bq][:, :]))
                        blk.vector(fv14)
                        pp_free[bq] = ctx.sv.v
                    copies_done = ctx.sv.v

                    def ft9(t, tq=tq, hi=hi, copies_done=copies_done):
                        t.wait_ge(ctx.sv.h, copies_done)
                        for kt in range(tq + 1):
                            ins = t.matmul(pav[0:32, :],
                                           v_g[:, kt * 128 + hi * 32: kt * 128 + (hi + 1) * 32],
                                           pbT_buf[:, kt * 128:(kt + 1) * 128],
                                           start=(kt == 0), stop=(kt == tq))
                        ctx.sp.inc(ins)
                    blk.tensor(ft9)

                    def fv11(v):
                        v.wait_ge(ctx.sp.h, ctx.sp.v)
                        ctx.sv.inc(v.tensor_copy(oc_sb[0:32, :], pav[0:32, :]))
                    blk.vector(fv11)

                    for nh in range(2):
                        def ft10(t, hi=hi, nh=nh):
                            t.wait_ge(ctx.sv.h, ctx.sv.v)
                            ctx.sp.inc(t.matmul(po[nh][:, :], oc_sb[0:32, :],
                                                wo_g[0:32, hi * 1024 + nh * 512: hi * 1024 + (nh + 1) * 512],
                                                start=(hi == 0), stop=(hi == 3)))
                        blk.tensor(ft10)

                def fv13(v, hg=hg, tq=tq):
                    v.wait_ge(ctx.sp.h, ctx.sp.v)
                    dst0 = out_acc[:, tq * 1024: tq * 1024 + 512]
                    dst1 = out_acc[:, tq * 1024 + 512: (tq + 1) * 1024]
                    if hg == 0:
                        v.tensor_copy(dst0, po[0][:, :])
                        ctx.sv.inc(v.tensor_copy(dst1, po[1][:, :]))
                    else:
                        v.tensor_add(dst0, dst0, po[0][:, :])
                        ctx.sv.inc(v.tensor_add(dst1, dst1, po[1][:, :]))
                blk.vector(fv13)

        for tq in range(16):
            def fs4(s, b=b, tq=tq):
                s.wait_ge(ctx.sa.h, ctx.sa.v)
                s.wait_ge(ctx.sv.h, ctx.sv.v)
                ctx.sd.inc(s.dma_start(out=x_tile[:, :], in_=x_in[b, tq * 128:(tq + 1) * 128, :]))
            blk.sync(fs4)

            def fv12(v, tq=tq):
                v.wait_ge(ctx.sd.h, ctx.sd.v)
                ctx.sv.inc(v.tensor_add(out_acc[:, tq * 1024:(tq + 1) * 1024],
                                        out_acc[:, tq * 1024:(tq + 1) * 1024], x_tile[:, :]))
            blk.vector(fv12)

            def fg2(g, b=b, tq=tq):
                g.wait_ge(ctx.sv.h, ctx.sv.v)
                ctx.sg.inc(g.dma_start(out=x_out[b, tq * 128:(tq + 1) * 128, :],
                                       in_=out_acc[:, tq * 1024:(tq + 1) * 1024]))
            blk.gpsimd(fg2)
        ctx.barrier()

    ctx.free("a_wkv", "a_qg", "a_kg", "a_mask", "a_grow", "a_wld",
             "a_xnT", "a_oacc", "a_xt", "a_cent", "a_lncol", "a_xn16t",
             "a_wqg", "a_wkg", "a_wvg", "a_wog", "a_kvT", "a_q", "a_qsc", "a_rcol",
             "a_qT", "a_kT", "a_vg", "a_scores", "a_probs", "a_smcol", "a_pbT",
             "a_oc", "a_p16", "a_ps0", "a_ps1", "a_pp0", "a_pp1", "a_pav", "a_po0", "a_po1")



def emit_ffn0_stage(ctx, c, eg_w, eu_w, ed_w, sg_w, su_w, sd_w, gate_w, gate_bias):
    nc, blk = ctx.nc, ctx.blk
    st = {}
    st["eg"] = nc.dram_tensor("eg_s", [66, 512, 1024], BF16, kind="Internal").ap()
    st["eu"] = nc.dram_tensor("eu_s", [66, 512, 1024], BF16, kind="Internal").ap()
    st["ed"] = nc.dram_tensor("ed_s", [66, 1024, 512], BF16, kind="Internal").ap()
    st["gwT"] = ctx.alloc("f0_gwT", [128, 8 * 64], F32)
    st["gb"] = ctx.alloc("f0_gb", [128, 64], F32)
    wld = ctx.alloc("f0_wld", [128, 2 * 1024], F32)
    wst = ctx.alloc("f0_wst", [128, 2 * 1024], BF16)
    pt = ctx.alloc("f0_pt", [128, 128], F32, psum=True)

    tiles = []
    for e in range(64):
        for k in range(4):
            tiles.append((eg_w[e, k * 128:(k + 1) * 128, :], st["eg"][e, k * 128:(k + 1) * 128, :]))
            tiles.append((eu_w[e, k * 128:(k + 1) * 128, :], st["eu"][e, k * 128:(k + 1) * 128, :]))
        for k in range(8):
            tiles.append((ed_w[e, k * 128:(k + 1) * 128, :],
                          st["ed"][e, k * 128:(k + 1) * 128, :]))
    for half in range(2):
        for k in range(4):
            r = half * 512 + k * 128
            tiles.append((sg_w[r:r + 128, :], st["eg"][64 + half, k * 128:(k + 1) * 128, :]))
            tiles.append((su_w[r:r + 128, :], st["eu"][64 + half, k * 128:(k + 1) * 128, :]))
        for k in range(8):
            tiles.append((sd_w[k * 128:(k + 1) * 128, half * 512:(half + 1) * 512],
                          st["ed"][64 + half, k * 128:(k + 1) * 128, :]))
    conv_done = [0, 0]
    store_done = [0, 0]
    load_done = [0, 0]
    for j, (src_ap, dst_ap) in enumerate(tiles):
        b = j % 2
        w = src_ap.shape[1]
        def fs(s, src_ap=src_ap, b=b, w=w):
            s.wait_ge(ctx.sa.h, conv_done[b])
            ctx.sd.inc(s.dma_start(out=wld[:, b * 1024: b * 1024 + w], in_=src_ap))
            load_done[b] = ctx.sd.v
        blk.sync(fs)

        def fa(a, b=b, w=w):
            a.wait_ge(ctx.sd.h, load_done[b])
            a.wait_ge(ctx.sg.h, store_done[b])
            ctx.sa.inc(a.copy(wst[:, b * 1024: b * 1024 + w], wld[:, b * 1024: b * 1024 + w]))
            conv_done[b] = ctx.sa.v
        blk.scalar(fa)

        def fg(g, dst_ap=dst_ap, b=b, w=w):
            g.wait_ge(ctx.sa.h, conv_done[b])
            ctx.sg.inc(g.dma_start(out=dst_ap, in_=wst[:, b * 1024: b * 1024 + w]))
            store_done[b] = ctx.sg.v
        blk.gpsimd(fg)
    ctx.barrier()

    def fs(s):
        ctx.sd.inc(s.dma_start(out=wld[0:64, 0:1024], in_=gate_w[:, :]))
        ctx.sd.inc(s.dma_start(out=wld[0:1, 1024:1088], in_=gate_bias.unsqueeze(0)))
    blk.sync(fs)
    for k in range(8):
        def ft(t, k=k):
            t.wait_ge(ctx.sd.h, ctx.sd.v)
            t.wait_ge(ctx.sv.h, ctx.sv.v)
            ctx.sp.inc(t.transpose(pt[0:128, 0:64], wld[0:64, k * 128:(k + 1) * 128],
                                   c["id32"][0:64, 0:64]))
        blk.tensor(ft)

        def fv(v, k=k):
            v.wait_ge(ctx.sp.h, ctx.sp.v)
            ctx.sv.inc(v.tensor_copy(st["gwT"][:, k * 64:(k + 1) * 64], pt[0:128, 0:64]))
        blk.vector(fv)

    def ft2(t):
        t.wait_ge(ctx.sd.h, ctx.sd.v)
        t.wait_ge(ctx.sv.h, ctx.sv.v)
        ctx.sp.inc(t.matmul(pt[0:128, 0:64], c["ones32"][0:1, :], wld[0:1, 1024:1088],
                            start=True, stop=True))
    blk.tensor(ft2)

    def fv2(v):
        v.wait_ge(ctx.sp.h, ctx.sp.v)
        ctx.sv.inc(v.tensor_copy(st["gb"][:, :], pt[0:128, 0:64]))
    blk.vector(fv2)
    ctx.barrier()
    ctx.free("f0_wld", "f0_wst", "f0_pt")
    return st


def emit_ffn0(ctx, c, st, x_in, x_out, tag):
    nc, blk = ctx.nc, ctx.blk
    NTOK = B_PER_CORE * T_SEQ
    ST = 2048
    NCH = ST // 128
    BIG = 1e30

    x_tile = ctx.alloc("g_xt", [128, 1024], F32)
    cent = ctx.alloc("g_cent", [128, 1024], F32)
    lncol = ctx.alloc("g_lncol", [128, 4], F32)
    xn16_t = ctx.alloc("g_xn16t", [128, 1024], BF16)
    xn32_t = ctx.alloc("g_xn32t", [128, 1024], F32)
    xnT32_tmp = ctx.alloc("g_xnT32", [128, 1024], F32)
    xnT = ctx.alloc("g_xnT", [128, 8 * ST], BF16)
    out_acc = ctx.alloc("g_oacc", [128, 16 * 1024], F32)
    comb_all = ctx.alloc("g_comb", [128, NCH * 64], F32)
    sc_sb = ctx.alloc("g_sc", [128, 64], F32)
    ws_sb = ctx.alloc("g_ws", [128, 64], F32)
    gs_sb = ctx.alloc("g_gs", [128, 24], F32)
    hw_sb = ctx.alloc("g_h", [128, 4 * ST], BF16)
    htmp = ctx.alloc("g_htmp", [128, 2 * 512], BF16)
    weg = [ctx.alloc("g_weg0", [128, 8 * 512], BF16), ctx.alloc("g_weg1", [128, 8 * 512], BF16)]
    weu = [ctx.alloc("g_weu0", [128, 8 * 512], BF16), ctx.alloc("g_weu1", [128, 8 * 512], BF16)]
    wed = [ctx.alloc("g_wed0", [128, 4 * 1024], BF16), ctx.alloc("g_wed1", [128, 4 * 1024], BF16)]
    p16 = ctx.alloc("g_p16", [128, 128], BF16, psum=True)
    pT = ctx.alloc("g_pT", [128, 128], F32, psum=True)
    pg = pT[:, 0:64]
    pA = [ctx.alloc("g_pA0", [128, 512], F32, psum=True),
          ctx.alloc("g_pA1", [128, 512], F32, psum=True)]
    pB = [ctx.alloc("g_pB0", [128, 512], F32, psum=True),
          ctx.alloc("g_pB1", [128, 512], F32, psum=True)]
    po = [ctx.alloc("g_po0", [128, 512], F32, psum=True),
          ctx.alloc("g_po1", [128, 512], F32, psum=True)]

    ln_scratch = (cent, x_tile, lncol[:, 0:1], lncol[:, 1:2], lncol[:, 2:3])

    for sti in range(NTOK // ST):
        for ti in range(NCH):
            gch = sti * NCH + ti
            b, trow = divmod(gch * 128, T_SEQ)

            def fs(s, b=b, trow=trow):
                s.wait_ge(ctx.sa.h, ctx.sa.v)
                s.wait_ge(ctx.sv.h, ctx.sv.v)
                ctx.sd.inc(s.dma_start(out=x_tile[:, :], in_=x_in[b, trow:trow + 128, :]))
            blk.sync(fs)

            def fv0(v):
                v.wait_ge(ctx.sd.h, ctx.sd.v)
            blk.vector(fv0)
            emit_ln(ctx, c, x_tile, ln_scratch, xn16_t)

            def fvx(v):
                ctx.sv.inc(v.tensor_scalar_mul(xn32_t[:, :], cent[:, :], lncol[:, 2:3]))
            blk.vector(fvx)
            for k in range(8):
                def ft(t, k=k):
                    t.wait_ge(ctx.sv.h, ctx.sv.v)
                    t.wait_ge(ctx.sa.h, ctx.sa.v)
                    ctx.sp.inc(t.transpose(p16[:, :], xn16_t[:, k * 128:(k + 1) * 128], c["id16"][:, :]))
                    ctx.sp.inc(t.transpose(pT[:, :], xn32_t[:, k * 128:(k + 1) * 128], c["id32"][:, :]))
                blk.tensor(ft)

                def fa(a, k=k, ti=ti):
                    a.wait_ge(ctx.sp.h, ctx.sp.v)
                    a.copy(xnT[:, k * ST + ti * 128: k * ST + (ti + 1) * 128], p16[:, :])
                    ctx.sa.inc(a.copy(xnT32_tmp[:, k * 128:(k + 1) * 128], pT[:, :]))
                blk.scalar(fa)

            def ftg(t):
                t.wait_ge(ctx.sa.h, ctx.sa.v)
                t.wait_ge(ctx.sv.h, ctx.sv.v)
                for k in range(8):
                    ins = t.matmul(pg, xnT32_tmp[:, k * 128:(k + 1) * 128],
                                   st["gwT"][:, k * 64:(k + 1) * 64], start=(k == 0), stop=(k == 7))
                ctx.sp.inc(ins)
            blk.tensor(ftg)

            def fag(a):
                a.wait_ge(ctx.sp.h, ctx.sp.v)
                ctx.sa.inc(a.activation(sc_sb[:, :], pg, AF.Sigmoid))
            blk.scalar(fag)

            def fvg(v, ti=ti):
                v.wait_ge(ctx.sa.h, ctx.sa.v)
                ctx.sv.inc(v.tensor_add(sc_sb[:, :], sc_sb[:, :], st["gb"][:, :]))
                v.wait_ge(ctx.sv.h, ctx.sv.v)
                ctx.sv.inc(v.reduce_max(gs_sb[:, 0:8],
                                        sc_sb.ap().rearrange("p (g e) -> p g e", g=8), axis=AX.X))
                v.wait_ge(ctx.sv.h, ctx.sv.v)
                ctx.sv.inc(v.tensor_copy(gs_sb[:, 8:16], gs_sb[:, 0:8]))
                v.wait_ge(ctx.sv.h, ctx.sv.v)
                for it in range(4):
                    ctx.sv.inc(v.reduce_max(gs_sb[:, 16 + it:17 + it], gs_sb[:, 8:16], axis=AX.X))
                    v.wait_ge(ctx.sv.h, ctx.sv.v)
                    if it < 3:
                        ctx.sv.inc(v.tensor_scalar(ws_sb[:, 0:8], gs_sb[:, 8:16],
                                                   gs_sb[:, 16 + it:17 + it], None, ALU.is_ge))
                        v.wait_ge(ctx.sv.h, ctx.sv.v)
                        ctx.sv.inc(v.scalar_tensor_tensor(gs_sb[:, 8:16], ws_sb[:, 0:8], -BIG,
                                                          gs_sb[:, 8:16], ALU.mult, ALU.add))
                        v.wait_ge(ctx.sv.h, ctx.sv.v)
                ctx.sv.inc(v.tensor_scalar(gs_sb[:, 8:16], gs_sb[:, 0:8],
                                           gs_sb[:, 19:20], None, ALU.is_ge))
                v.wait_ge(ctx.sv.h, ctx.sv.v)
                gmb = gs_sb.ap()[:, 8:16].unsqueeze(2).broadcast_to([128, 8, 8])
                ctx.sv.inc(v.tensor_tensor(ws_sb.ap().rearrange("p (g e) -> p g e", g=8),
                                           sc_sb.ap().rearrange("p (g e) -> p g e", g=8),
                                           gmb, ALU.mult))
                v.wait_ge(ctx.sv.h, ctx.sv.v)
                ctx.sv.inc(v.tensor_copy(sc_sb[:, :], ws_sb[:, :]))
                v.wait_ge(ctx.sv.h, ctx.sv.v)
                for it in range(6):
                    ctx.sv.inc(v.reduce_max(gs_sb[:, 16 + it:17 + it], sc_sb[:, :], axis=AX.X))
                    v.wait_ge(ctx.sv.h, ctx.sv.v)
                    if it < 5:
                        ctx.sv.inc(v.tensor_scalar(comb_all[:, ti * 64:(ti + 1) * 64], sc_sb[:, :],
                                                   gs_sb[:, 16 + it:17 + it], None, ALU.is_ge))
                        v.wait_ge(ctx.sv.h, ctx.sv.v)
                        ctx.sv.inc(v.scalar_tensor_tensor(sc_sb[:, :], comb_all[:, ti * 64:(ti + 1) * 64],
                                                          -BIG, sc_sb[:, :], ALU.mult, ALU.add))
                        v.wait_ge(ctx.sv.h, ctx.sv.v)
                ctx.sv.inc(v.tensor_scalar(sc_sb[:, :], ws_sb[:, :], gs_sb[:, 21:22],
                                           None, ALU.is_ge))
                ctx.sv.inc(v.tensor_scalar(comb_all[:, ti * 64:(ti + 1) * 64], ws_sb[:, :], 0.0,
                                           None, ALU.is_gt))
                v.wait_ge(ctx.sv.h, ctx.sv.v)
                ctx.sv.inc(v.tensor_mul(sc_sb[:, :], sc_sb[:, :], comb_all[:, ti * 64:(ti + 1) * 64]))
                v.wait_ge(ctx.sv.h, ctx.sv.v)
                ctx.sv.inc(v.tensor_mul(ws_sb[:, :], ws_sb[:, :], sc_sb[:, :]))
                v.wait_ge(ctx.sv.h, ctx.sv.v)
                ctx.sv.inc(v.reduce_sum(gs_sb[:, 22:23], ws_sb[:, :], axis=AX.X))
                v.wait_ge(ctx.sv.h, ctx.sv.v)
                ctx.sv.inc(v.tensor_scalar_add(gs_sb[:, 22:23], gs_sb[:, 22:23], 1e-8))
                v.wait_ge(ctx.sv.h, ctx.sv.v)
                ctx.sv.inc(v.reciprocal(gs_sb[:, 22:23], gs_sb[:, 22:23]))
                v.wait_ge(ctx.sv.h, ctx.sv.v)
                ctx.sv.inc(v.tensor_scalar(comb_all[:, ti * 64:(ti + 1) * 64], ws_sb[:, :],
                                           gs_sb[:, 22:23], 0.5, ALU.mult, ALU.mult))
            blk.vector(fvg)
        ctx.barrier()

        load_done = [0, 0]
        comp_done = [0, 0]
        po_evac = [0, 0]
        hAB_done = [0, 0]
        for e in range(66):
            bw = e % 2

            def fsw(s, e=e, bw=bw):
                s.wait_ge(ctx.sp.h, comp_done[bw])
                for k in range(8):
                    ctx.sd.inc(s.dma_start(out=weg[bw][:, k * 512:(k + 1) * 512],
                                           in_=st["eg"][e, 0:512, k * 128:(k + 1) * 128],
                                           transpose=True))
                for k in range(8):
                    ctx.sd.inc(s.dma_start(out=weu[bw][:, k * 512:(k + 1) * 512],
                                           in_=st["eu"][e, 0:512, k * 128:(k + 1) * 128],
                                           transpose=True))
                for k in range(4):
                    ctx.sd.inc(s.dma_start(out=wed[bw][:, k * 1024:(k + 1) * 1024],
                                           in_=st["ed"][e, 0:1024, k * 128:(k + 1) * 128],
                                           transpose=True))
                load_done[bw] = ctx.sd.v
            blk.sync(fsw)

            for m in range(4):
                for tg in range(ST // 512):
                    jj = m * (ST // 512) + tg
                    bp = jj % 2

                    def fth(t, e=e, bw=bw, m=m, tg=tg, bp=bp):
                        t.wait_ge(ctx.sd.h, load_done[bw])
                        t.wait_ge(ctx.sv.h, hAB_done[bp])
                        if m == 0 and tg == 0:
                            t.wait_ge(ctx.sa.h, ctx.sa.v)
                        for k in range(8):
                            t.matmul(pA[bp][:, :], weg[bw][:, k * 512 + m * 128: k * 512 + (m + 1) * 128],
                                     xnT[:, k * ST + tg * 512: k * ST + (tg + 1) * 512],
                                     start=(k == 0), stop=(k == 7))
                        for k in range(8):
                            ins = t.matmul(pB[bp][:, :], weu[bw][:, k * 512 + m * 128: k * 512 + (m + 1) * 128],
                                           xnT[:, k * ST + tg * 512: k * ST + (tg + 1) * 512],
                                           start=(k == 0), stop=(k == 7))
                        ctx.sp.inc(ins)
                    blk.tensor(fth)

                    def fah(a, m=m, tg=tg, bp=bp):
                        a.wait_ge(ctx.sp.h, ctx.sp.v)
                        ctx.sa.inc(a.activation(htmp[:, bp * 512:(bp + 1) * 512], pA[bp][:, :], AF.Silu))
                    blk.scalar(fah)

                    def fvh(v, m=m, tg=tg, bp=bp):
                        v.wait_ge(ctx.sa.h, ctx.sa.v)
                        ctx.sv.inc(v.tensor_mul(hw_sb[:, m * ST + tg * 512: m * ST + (tg + 1) * 512],
                                                htmp[:, bp * 512:(bp + 1) * 512], pB[bp][:, :]))
                        hAB_done[bp] = ctx.sv.v
                    blk.vector(fvh)

            for tc in range(NCH):
                gch = sti * NCH + tc
                for hh in range(2):
                    bo = (tc * 2 + hh) % 2

                    def fto(t, e=e, bw=bw, tc=tc, hh=hh, bo=bo):
                        t.wait_ge(ctx.sv.h, po_evac[bo])
                        if tc == 0 and hh == 0:
                            t.wait_ge(ctx.sv.h, ctx.sv.v)
                        for k in range(4):
                            ins = t.matmul(po[bo][:, :],
                                           hw_sb[:, k * ST + tc * 128: k * ST + (tc + 1) * 128],
                                           wed[bw][:, k * 1024 + hh * 512: k * 1024 + (hh + 1) * 512],
                                           start=(k == 0), stop=(k == 3))
                        ctx.sp.inc(ins)
                    blk.tensor(fto)

                    def fvo(v, e=e, tc=tc, hh=hh, bo=bo, gch=gch, sti=sti):
                        v.wait_ge(ctx.sp.h, ctx.sp.v)
                        dst = out_acc[:, tc * 1024 + hh * 512: tc * 1024 + (hh + 1) * 512]
                        if e >= 64:
                            selc = c["c_half"]
                        else:
                            selc = comb_all[:, tc * 64 + e: tc * 64 + e + 1]
                        if e == 0:
                            ctx.sv.inc(v.tensor_scalar(dst, po[bo][:, :], selc, None, ALU.mult))
                        else:
                            ctx.sv.inc(v.scalar_tensor_tensor(dst, po[bo][:, :], selc, dst,
                                                              ALU.mult, ALU.add))
                        po_evac[bo] = ctx.sv.v
                    blk.vector(fvo)
            comp_done[bw] = ctx.sp.v

        for tc in range(NCH):
            gch = sti * NCH + tc
            b, trow = divmod(gch * 128, T_SEQ)

            def fs(s, b=b, trow=trow):
                s.wait_ge(ctx.sv.h, ctx.sv.v)
                s.wait_ge(ctx.sa.h, ctx.sa.v)
                ctx.sd.inc(s.dma_start(out=x_tile[:, :], in_=x_in[b, trow:trow + 128, :]))
            blk.sync(fs)

            def fv7(v, tc=tc):
                v.wait_ge(ctx.sd.h, ctx.sd.v)
                ctx.sv.inc(v.tensor_add(out_acc[:, tc * 1024:(tc + 1) * 1024],
                                        out_acc[:, tc * 1024:(tc + 1) * 1024], x_tile[:, :]))
            blk.vector(fv7)

            def fg(g, b=b, trow=trow, tc=tc):
                g.wait_ge(ctx.sv.h, ctx.sv.v)
                ctx.sg.inc(g.dma_start(out=x_out[b, trow:trow + 128, :],
                                       in_=out_acc[:, tc * 1024:(tc + 1) * 1024]))
            blk.gpsimd(fg)
        ctx.barrier()

    ctx.free("g_xt", "g_cent", "g_lncol", "g_xn16t", "g_xn32t", "g_xnT32", "g_xnT",
             "g_oacc", "g_comb", "g_sc", "g_ws", "g_gs", "g_h", "g_htmp",
             "g_weg0", "g_weg1", "g_weu0", "g_weu1", "g_wed0", "g_wed1",
             "g_p16", "g_pT", "g_pA0", "g_pA1", "g_pB0", "g_pB1", "g_po0", "g_po1")


def _decl(nc, name, shape, kind="ExternalInput"):
    return nc.dram_tensor(name, list(shape), F32, kind=kind).ap()


def _build_full():
    nc = bass.Bass("TRN2", target_bir_lowering=False)
    g = {}
    g["x_in"] = _decl(nc, "x_in", [B_PER_CORE, T_SEQ, D])
    for nm, sh in [("eg_w", (64, 512, 1024)), ("eu_w", (64, 512, 1024)),
                   ("ed_w", (64, 1024, 512)), ("sg_w", (1024, 1024)),
                   ("su_w", (1024, 1024)), ("sd_w", (1024, 1024)),
                   ("gate_w", (64, 1024)), ("gate_bias", (64,)),
                   ("W_dt", (1024, 1024)), ("b_dt", (1024,)), ("W_B", (16, 1024)),
                   ("W_C", (16, 1024)), ("A_log", (1024, 16)), ("D_in", (1024,)),
                   ("W1", (8, 1024, 4096)), ("b1", (8, 4096)), ("W2", (8, 4096, 1024)),
                   ("b2", (8, 1024)), ("Wg", (1024, 8)), ("rank_mask", (8, 1)),
                   ("Wq", (1024, 1024)), ("Wkv", (1024, 128)), ("Wk", (128, 512)),
                   ("Wv", (128, 512)), ("Wo", (1024, 1024)), ("q_g", (64,)), ("k_g", (32,))]:
        g[nm] = _decl(nc, nm, sh)
    y = _decl(nc, "y", [B_PER_CORE, T_SEQ, D], kind="ExternalOutput")
    xb1 = _decl(nc, "xb1", [B_PER_CORE, T_SEQ, D], kind="Internal")
    xb2 = _decl(nc, "xb2", [B_PER_CORE, T_SEQ, D], kind="Internal")
    x3 = _decl(nc, "x3", [B_PER_CORE, T_SEQ, D], kind="Internal")
    xb4 = _decl(nc, "xb4", [B_PER_CORE, T_SEQ, D], kind="Internal")
    x5 = _decl(nc, "x5", [B_PER_CORE, T_SEQ, D], kind="Internal")
    cc4_in = _decl(nc, "cc4_in", [1, 128], kind="Internal")
    cc4_all = _decl(nc, "cc4_all", [8, 128], kind="Internal")
    cc6_in = _decl(nc, "cc6_in", [1, 128], kind="Internal")
    cc6_all = _decl(nc, "cc6_all", [8, 128], kind="Internal")
    with nc.Block() as blk:
        ctx = Ctx(nc, blk)
        c = make_consts(ctx)
        st0 = emit_ffn0_stage(ctx, c, g["eg_w"], g["eu_w"], g["ed_w"], g["sg_w"],
                              g["su_w"], g["sd_w"], g["gate_w"], g["gate_bias"])
        emit_ffn0(ctx, c, st0, g["x_in"], xb1, "s1")
        emit_mamba(ctx, c, xb1, xb2, g["W_dt"], g["b_dt"], g["W_B"], g["W_C"],
                   g["A_log"], g["D_in"])
        emit_ffn0(ctx, c, st0, xb2, x3, "s3")
        ctx.free("f0_gb", "f0_gwT", "sel16")
        st1 = emit_ffn1_stage(ctx, c, g["W1"], g["W2"], g["Wg"])
        emit_ffn1_small(ctx, c, st1, g["b1"], g["b2"])
        emit_ffn1(ctx, c, st1, x3, xb4, g["rank_mask"], "s4", cc4_in, cc4_all)
        ctx.free("f1_b1c", "f1_b2r")
        emit_attn(ctx, c, xb4, x5, g["Wq"], g["Wkv"], g["Wk"], g["Wv"], g["Wo"],
                  g["q_g"], g["k_g"])
        emit_ffn1_small(ctx, c, st1, g["b1"], g["b2"])
        emit_ffn1(ctx, c, st1, x5, y, g["rank_mask"], "s6", cc6_in, cc6_all)
        finish(ctx)
    return nc


_NC_CACHE = {}


def kernel(**inputs):
    shards = np.split(np.ascontiguousarray(inputs["L0_attn_x"]), NCORES, axis=0)
    rms = []
    for cix in range(NCORES):
        rm = np.zeros((8, 1), np.float32)
        rm[:cix] = 1.0
        rms.append(rm)

    if "F" not in _NC_CACHE:
        _NC_CACHE["F"] = _build_full()
    maps = [dict(x_in=shards[cix], eg_w=inputs["L0_ffn_eg_w"], eu_w=inputs["L0_ffn_eu_w"],
                 ed_w=inputs["L0_ffn_ed_w"], sg_w=inputs["L0_ffn_sg_w"],
                 su_w=inputs["L0_ffn_su_w"], sd_w=inputs["L0_ffn_sd_w"],
                 gate_w=inputs["L0_ffn_gate_w"], gate_bias=inputs["L0_ffn_gate_bias"],
                 W_dt=inputs["L0_attn_W_dt"], b_dt=inputs["L0_attn_b_dt"],
                 W_B=inputs["L0_attn_W_B"], W_C=inputs["L0_attn_W_C"],
                 A_log=inputs["L0_attn_A_log"], D_in=inputs["L0_attn_D"],
                 W1=inputs["L1_ffn_W1"], b1=inputs["L1_ffn_b1"],
                 W2=inputs["L1_ffn_W2"], b2=inputs["L1_ffn_b2"],
                 Wg=inputs["L1_ffn_Wg"], rank_mask=rms[cix],
                 Wq=inputs["L1_attn_Wq"], Wkv=inputs["L1_attn_Wkv"],
                 Wk=inputs["L1_attn_Wk"], Wv=inputs["L1_attn_Wv"],
                 Wo=inputs["L1_attn_Wo"], q_g=inputs["L1_attn_q_g"],
                 k_g=inputs["L1_attn_k_g"]) for cix in range(NCORES)]
    res = run_bass_kernel_spmd(_NC_CACHE["F"], maps, core_ids=list(range(NCORES)))
    return np.concatenate([r["y"] for r in res.results], axis=0)
